# version 08
# baseline (speedup 1.0000x reference)
import math
from collections import defaultdict

import numpy as np
import concourse.bass as bass
import concourse.mybir as mybir
from concourse.bass_isa import ReduceOp
from concourse.bass_utils import run_bass_kernel_spmd

F32 = mybir.dt.float32
BF16 = mybir.dt.bfloat16
F16 = mybir.dt.float16
I32 = mybir.dt.int32
AF = mybir.ActivationFunctionType
ALU = mybir.AluOpType
AX = mybir.AxisListType

NCORES = 8
P = 128
D = 4096
B = 2
S = 4096
NTOK = B * S
TCORE = NTOK // NCORES
NTT = TCORE // P
NIF = D // P
MAGIC = float(np.float32(1.5 * 2 ** 23))
LN_EPS = 1e-5
RMS_EPS = 1.1920929e-07
INV_TAU = 10.0
SINK_ITERS = 50
RANK = 1024
RHI = 256
RLO = 768
INV_SQRT_HD = 1.0 / math.sqrt(128.0)

WT_OFF = 0
VQ_OFF = 512 * D
BT_OFF = VQ_OFF + 512 * RANK
WO_OFF = BT_OFF + 128 * D
CONTRIB = WO_OFF + 512 * D
ARN = 2 * D + RANK

ENGINES = ("tensor", "vector", "scalar", "gpsimd", "sync")


class Prog:
    def __init__(self):
        self.ops = {e: [] for e in ENGINES}
        self.cnt = defaultdict(int)

    def val(self, sem):
        return self.cnt[sem]

    def emit(self, eng, fn, wait=(), inc=()):
        self.ops[eng].append((tuple(wait), fn, tuple(inc)))
        for sem, amt in inc:
            self.cnt[sem] += amt

    def after(self, *sems):
        return [(s, self.cnt[s]) for s in sems]

    def replay(self, eng_name, eng, sems):
        hw = defaultdict(int)
        for waits, fn, incs in self.ops[eng_name]:
            for sem, v in waits:
                if v > hw[sem]:
                    eng.wait_ge(sems[sem], v)
                    hw[sem] = v
            inst = fn(eng)
            for sem, amt in incs:
                inst.then_inc(sems[sem], amt)


def build_program(dev=0):
    nc = bass.Bass("TRN2", target_bir_lowering=False)

    x0_tm = nc.dram_tensor("x0_tm", [TCORE, D], F32, kind="ExternalInput").ap()
    x0_fm = nc.dram_tensor("x0_fm", [D, TCORE], F32, kind="ExternalInput").ap()
    l0_wt_sl = nc.dram_tensor("l0_wt_sl", [512, D], F32, kind="ExternalInput").ap()
    l0_mask_sl = nc.dram_tensor("l0_mask_sl", [D, 512], I32, kind="ExternalInput").ap()
    l0_bias = nc.dram_tensor("l0_bias", [1, D], F32, kind="ExternalInput").ap()
    wcatT = nc.dram_tensor("wcatT", [D, 24], F32, kind="ExternalInput").ap()
    hbases = nc.dram_tensor("hbases", [1, 24], F32, kind="ExternalInput").ap()
    gate_in = nc.dram_tensor("gate_in", [1, 1], F32, kind="ExternalInput").ap()
    wbr = nc.dram_tensor("wbr", [1024, 1024], F32, kind="ExternalInput").ap()
    wq_slT = nc.dram_tensor("wq_slT", [D, 512], F32, kind="ExternalInput").ap()
    wkv_slT = nc.dram_tensor("wkv_slT", [D, 256], F32, kind="ExternalInput").ap()
    wo_slT = nc.dram_tensor("wo_slT", [512, D], F32, kind="ExternalInput").ap()
    v_slT = nc.dram_tensor("v_slT", [512, RANK], F32, kind="ExternalInput").ap()
    uT_sl = nc.dram_tensor("uT_sl", [128, D], F32, kind="ExternalInput").ap()
    r_base_col = nc.dram_tensor("r_base_col", [P, 1], F32, kind="ExternalInput").ap()
    hi_idx = nc.dram_tensor("hi_idx", [1, RHI], I32, kind="ExternalInput").ap()
    lo_idx = nc.dram_tensor("lo_idx", [1, RLO], I32, kind="ExternalInput").ap()
    l1_bias = nc.dram_tensor("l1_bias", [1, D], F32, kind="ExternalInput").ap()
    tok_base = nc.dram_tensor("tok_base", [1, 1], I32, kind="ExternalInput").ap()

    out = nc.dram_tensor("out", [TCORE, D], F32, kind="ExternalOutput").ap()

    contrib = nc.dram_tensor("contrib", [CONTRIB], BF16)
    agw = nc.dram_tensor("agw", [NCORES * CONTRIB], BF16)
    ar_in = nc.dram_tensor("ar_in", [1, ARN], F32)
    ar_out = nc.dram_tensor("ar_out", [1, ARN], F32)
    x2c = nc.dram_tensor("x2c", [D, TCORE], BF16)
    x2full = nc.dram_tensor("x2full", [NCORES * D, TCORE], BF16)
    q_scr = nc.dram_tensor("q_scr", [512, NTOK], BF16)
    k_scr = nc.dram_tensor("k_scr", [128, NTOK], BF16)
    v_scr = nc.dram_tensor("v_scr", [NTOK, 128], BF16)
    c_o = nc.dram_tensor("c_o", [NTOK, 512], BF16)
    rowstash = nc.dram_tensor("rowstash", [16, 1024], F32)
    ag_o = nc.dram_tensor("ag_o", [NCORES * NTOK, 512], BF16)

    dv = {}
    if dev == 1:
        dv["x1"] = nc.dram_tensor("dv_x1", [TCORE, D], F32, kind="ExternalOutput").ap()
    if dev == 2:
        dv["x2"] = nc.dram_tensor("dv_x2", [TCORE, D], F32, kind="ExternalOutput").ap()
    if dev == 4:
        dv["x3"] = nc.dram_tensor("dv_x3", [TCORE, D], F32, kind="ExternalOutput").ap()

    prog = Prog()
    import contextlib
    st = contextlib.ExitStack()
    sb = lambda name, shape, dt: st.enter_context(nc.sbuf_tensor(name, shape, dt))
    pst = lambda name, shape, dt: st.enter_context(nc.psum_tensor(name, shape, dt))

    xs = sb("xs", [P, NTT * D], F32)
    big = sb("big", [P, 16384], BF16)
    mixbuf = sb("mixbuf", [P, 1024], F32)
    y1g = sb("y1g", [P, 2048], F32)
    x0buf = sb("x0buf", [P, 1024], F32)
    hbuf = sb("hbuf", [24, 1024], F32)
    sk_a = sb("sk_a", [P, 32], F32)
    sk_b = sb("sk_b", [P, 32], F32)
    sk_t = sb("sk_t", [P, P], F32)
    skrow = sb("skrow", [1, 1024], F32)
    Eit = sb("Eit", [16, 512], BF16)
    P32 = sb("P32", [P, 4], BF16)
    identF = sb("identF", [P, P], F32)
    ones_f32 = sb("ones_f32", [P, 1], F32)
    ones_bf = sb("ones_bf", [1, P], BF16)
    wbuf = sb("wbuf", [P, 1024], BF16)
    wbuf2 = sb("wbuf2", [P, 1024], BF16)
    onerow_f = sb("onerow_f", [1, P], F32)
    sums_s = sb("sums_s", [P, 64], F32)
    sqs_s = sb("sqs_s", [P, 64], F32)
    stat = sb("stat", [P, 64], F32)
    mh_col = sb("mh_col", [P, 1], F32)
    ml_col = sb("ml_col", [P, 1], F32)
    r0col = sb("r0col", [P, 1], F32)
    f16row = sb("f16row", [1, 1024], F16)

    ibase = sb("ibase", [1, 1], I32)
    gcol = sb("gcol", [P, 1], F32)
    ipat = sb("ipat", [P, 1], F32)
    ptbuf = sb("ptbuf", [P, 2048], BF16)

    XW = lambda a, b: xs[:, a:b]
    idxbuf = xs[0:1, 31744:32768]
    ROW = lambda a, b: xs[0:1, a:b]

    pkA = pst("pkA", [P, 1024], F32)
    pkB = pst("pkB", [P, 1024], F32)
    pkC = pst("pkC", [P, 1024], F32)
    pkD = pst("pkD", [P, 1024], F32)

    sems = {}
    semnames = ["dW", "dS", "dG", "dO", "mm", "vx", "sx", "gx", "cc"]

    def dma(queue, sem, dst, src, wait=()):
        prog.emit(queue, lambda e, d=dst, s=src: e.dma_start(out=d, in_=s),
                  wait=wait, inc=[(sem, 16)])

    def agw_wt(ic, of0, nof):
        r, row0 = ic // 4, (ic % 4) * P
        base = r * CONTRIB + WT_OFF + row0 * D
        return agw[base: base + P * D].rearrange("(p f) -> p f", f=D)[:, of0: of0 + nof]

    def agw_vq(ic, r0, nr):
        r, row0 = ic // 4, (ic % 4) * P
        base = r * CONTRIB + VQ_OFF + row0 * RANK
        return agw[base: base + P * RANK].rearrange("(p f) -> p f", f=RANK)[:, r0: r0 + nr]

    def agw_bt(rc, of0, nof):
        base = rc * CONTRIB + BT_OFF
        return agw[base: base + P * D].rearrange("(p f) -> p f", f=D)[:, of0: of0 + nof]

    def agw_wo(fc, of0, nof):
        r, row0 = fc // 4, (fc % 4) * P
        base = r * CONTRIB + WO_OFF + row0 * D
        return agw[base: base + P * D].rearrange("(p f) -> p f", f=D)[:, of0: of0 + nof]

    T = lambda fn, wait=(), inc=(): prog.emit("tensor", fn, wait, inc)

    def V(fn, wait=(), inc=(("vx", 1),)):
        prog.emit("vector", fn, tuple(wait) + tuple(prog.after("vx")), inc)

    def Sc(fn, wait=(), inc=(("sx", 1),)):
        prog.emit("scalar", fn, tuple(wait) + tuple(prog.after("sx")), inc)
    def G(fn, wait=(), inc=(("gx", 1),)):
        prog.emit("gpsimd", fn, tuple(wait) + tuple(prog.after("gx")), inc)

    def early_finish():
        dma("sync", "dO", out[0:P, 0:64], stat[:, :],
            wait=prog.after("mm", "vx", "sx", "gx", "dW", "dS", "dG", "cc"))
        prog.emit("sync", lambda e: e.nop(), wait=prog.after("dO"), inc=[])
        finish(nc, st, prog, sems, semnames)
        return nc

    V(lambda e: e.memset(ones_f32[:, :], 1.0))
    V(lambda e: e.memset(onerow_f[:, :], 1.0))
    V(lambda e: e.memset(ones_bf[:, :], 1.0))
    V(lambda e: e.memset(identF[:, :], 1.0))
    G(lambda e: e.affine_select(out=identF[:, :], in_=identF[:, :],
                                compare_op=ALU.is_equal, fill=0.0,
                                pattern=[[-1, P]], base=0, channel_multiplier=1),
      wait=prog.after("vx"))
    for it in range(4):
        G(lambda e, it=it: e.iota(sk_t[0:16, 0:P].rearrange("p (a c) -> p a c", c=32),
                                  pattern=[[-1, 4], [0, 32]], channel_multiplier=1,
                                  base=-4 * it,
                                  allow_small_or_imprecise_dtypes=True),
          wait=prog.after("vx"))
        V(lambda e, it=it: e.tensor_single_scalar(Eit[:, P * it:P * (it + 1)],
                                                  sk_t[0:16, 0:P], 0.0, ALU.is_equal),
          wait=prog.after("gx"))
    G(lambda e: e.iota(sk_t[:, 0:4], pattern=[[32, 4]], channel_multiplier=-1,
                       allow_small_or_imprecise_dtypes=True), wait=prog.after("vx"))
    V(lambda e: e.tensor_single_scalar(sk_t[:, 4:8], sk_t[:, 0:4], 0.0, ALU.is_le),
      wait=prog.after("gx"))
    V(lambda e: e.tensor_single_scalar(sk_t[:, 8:12], sk_t[:, 0:4], -31.5, ALU.is_ge))
    V(lambda e: e.tensor_tensor(P32[:, :], sk_t[:, 4:8], sk_t[:, 8:12], ALU.mult))
    CONSTS = prog.after("gx", "vx")
    if dev == 20:
        return early_finish()

    dma("gpsimd", "dG", idxbuf[:, 0:RHI], hi_idx)
    dma("gpsimd", "dG", idxbuf[:, RHI:RHI + RLO], lo_idx)
    dma("sync", "dW", r0col[:, :], r_base_col)
    dma("sync", "dW", ibase[:, :], tok_base)
    dma("sync", "dW", ROW(0, D), l1_bias)
    for k in range(4):
        V(lambda e, k=k: e.tensor_copy(f16row[:, :], ROW(1024 * k, 1024 * (k + 1))),
          wait=prog.after("dW") if k == 0 else ())
        V(lambda e, k=k: e.tensor_copy(ROW(1024 * k, 1024 * (k + 1)), f16row[:, :]))
    for k in range(4):
        dma("sync", "dO", rowstash.ap()[3 + k:4 + k, :], ROW(1024 * k, 1024 * (k + 1)),
            wait=prog.after("vx"))
    IDX_DONE = prog.after("dG", "dW")

    def member_cols(dst_col, idx_off, nidx, base_col):
        for j in range(0, nidx, 512):
            w = min(512, nidx - j)
            T(lambda e, j=j, w=w: e.matmul(pkA[:, j:j + w], onerow_f[:, :],
                                           idxbuf[:, idx_off + j: idx_off + j + w],
                                           start=True, stop=True),
              wait=IDX_DONE + prog.after("vx"),
              inc=[("mm", 1)] if j + 512 >= nidx else [])
        mmv = prog.val("mm")
        V(lambda e, n=nidx, bc=base_col: e.tensor_scalar(
            XW(0, nidx), pkA[:, 0:nidx], bc, None, ALU.is_equal), wait=[("mm", mmv)])
        V(lambda e, n=nidx, dc=dst_col: e.reduce_max(dc, XW(0, n), axis=AX.X))

    member_cols(mh_col[:, :], 0, RHI, r0col[:, 0:1])
    member_cols(ml_col[:, :], RHI, RLO, r0col[:, 0:1])
    G(lambda e: e.iota(stat[:, 63:64], pattern=[[0, 1]], base=0, channel_multiplier=1,
                       allow_small_or_imprecise_dtypes=True), wait=CONSTS)
    for k in range(8):
        V(lambda e, k=k: e.tensor_scalar(stat[:, 62:63], stat[:, 63:64],
                                         float(128 * k), None, ALU.add),
          wait=prog.after("gx"))
        member_cols(stat[:, 40 + k:41 + k], 0, RHI, stat[:, 62:63])
        member_cols(stat[:, 48 + k:49 + k], RHI, RLO, stat[:, 62:63])
    T(lambda e: e.transpose(pkC[0:8, 0:P], stat[:, 40:48], identF[:, :]),
      wait=prog.after("vx"), inc=[])
    T(lambda e: e.transpose(pkC[0:8, 128:256], stat[:, 48:56], identF[:, :]),
      inc=[("mm", 1)])
    V(lambda e: e.tensor_copy(sk_t[0:8, 0:P], pkC[0:8, 0:P]), wait=prog.after("mm"))
    V(lambda e: e.tensor_copy(mixbuf[0:8, 0:P], pkC[0:8, 128:256]))
    dma("sync", "dW", ROW(13312, 14336), sk_t[0:8, 0:P], wait=prog.after("vx"))
    dma("sync", "dW", ROW(31744, 32768), mixbuf[0:8, 0:P])
    dma("sync", "dO", rowstash.ap()[0:1, :], sk_t[0:8, 0:P])
    dma("sync", "dO", rowstash.ap()[1:2, :], mixbuf[0:8, 0:P])
    prog.emit("vector", lambda e: e.engine_nop(), wait=prog.after("dW"), inc=[("vx", 1)])

    if dev == 21:
        return early_finish()
    dma("sync", "dW", XW(0, D), uT_sl, wait=prog.after("vx"))
    Sc(lambda e: e.activation(XW(D, 2 * D), XW(0, D), AF.Abs), wait=prog.after("dW"))
    V(lambda e: e.tensor_scalar(XW(2 * D, 3 * D), XW(D, 2 * D), mh_col[:, 0:1], None, ALU.mult),
      wait=prog.after("sx"))
    for j in range(32):
        slot = pkA[:, 0:128] if j % 2 == 0 else pkA[:, 512:640]
        T(lambda e, j=j, slot=slot: e.transpose(slot, XW(2 * D, 3 * D)[:, 128 * j:128 * (j + 1)],
                                                identF[:, :]),
          wait=prog.after("vx") , inc=[("mm", 1)])
        mv = prog.val("mm")
        V(lambda e, j=j, slot=slot: e.reduce_max(mixbuf[:, 0 + j:0 + j + 1],
                                                 slot, axis=AX.X), wait=[("mm", mv)])
    T(lambda e: e.transpose(pkB[0:32, 0:128], mixbuf[:, 0:0 + 32],
                            identF[:, :]), wait=prog.after("vx"), inc=[("mm", 1)])
    mv = prog.val("mm")
    V(lambda e: e.tensor_copy(mixbuf[0:32, 512:640], pkB[0:32, 0:128]), wait=[("mm", mv)])
    dma("sync", "dW", ar_in[0:1, 0:D].rearrange("one (j c) -> (one j) c", c=P),
        mixbuf[0:32, 512:640], wait=prog.after("vx"))
    V(lambda e: e.tensor_scalar(XW(2 * D, 3 * D), XW(D, 2 * D), ml_col[:, 0:1], None, ALU.mult),
      wait=prog.after("dW"))
    for j in range(32):
        slot = pkA[:, 0:128] if j % 2 == 0 else pkA[:, 512:640]
        T(lambda e, j=j, slot=slot: e.transpose(slot, XW(2 * D, 3 * D)[:, 128 * j:128 * (j + 1)],
                                                identF[:, :]),
          wait=prog.after("vx") , inc=[("mm", 1)])
        mv = prog.val("mm")
        V(lambda e, j=j, slot=slot: e.reduce_max(mixbuf[:, 32 + j:32 + j + 1],
                                                 slot, axis=AX.X), wait=[("mm", mv)])
    T(lambda e: e.transpose(pkB[0:32, 0:128], mixbuf[:, 32:32 + 32],
                            identF[:, :]), wait=prog.after("vx"), inc=[("mm", 1)])
    mv = prog.val("mm")
    V(lambda e: e.tensor_copy(mixbuf[0:32, 512:640], pkB[0:32, 0:128]), wait=[("mm", mv)])
    dma("sync", "dW", ar_in[0:1, D:2 * D].rearrange("one (j c) -> (one j) c", c=P),
        mixbuf[0:32, 512:640], wait=prog.after("vx"))

    for vt in range(4):
        dma("scalar", "dS", XW(16384, 17408), v_slT[P * vt:P * (vt + 1), :],
            wait=prog.after("sx", "vx"))
        Sc(lambda e: e.activation(XW(17408, 18432),
                                  XW(16384, 17408), AF.Abs),
           wait=prog.after("dS"))
        if vt == 0:
            V(lambda e: e.tensor_copy(XW(18432, 19456),
                                      XW(17408, 18432)),
              wait=prog.after("sx"))
        else:
            V(lambda e: e.tensor_tensor(XW(18432, 19456),
                                        XW(18432, 19456),
                                        XW(17408, 18432), ALU.max),
              wait=prog.after("sx"))
    for j in range(8):
        slot = pkA[:, 0:128] if j % 2 == 0 else pkA[:, 512:640]
        T(lambda e, j=j, slot=slot: e.transpose(slot, XW(18432, 19456)[:, 128 * j:128 * (j + 1)],
                                                identF[:, :]),
          wait=prog.after("vx") , inc=[("mm", 1)])
        mv = prog.val("mm")
        V(lambda e, j=j, slot=slot: e.reduce_max(mixbuf[:, 64 + j:64 + j + 1],
                                                 slot, axis=AX.X), wait=[("mm", mv)])
    T(lambda e: e.transpose(pkB[0:8, 0:128], mixbuf[:, 64:64 + 8],
                            identF[:, :]), wait=prog.after("vx"), inc=[("mm", 1)])
    mv = prog.val("mm")
    V(lambda e: e.tensor_copy(mixbuf[0:8, 512:640], pkB[0:8, 0:128]), wait=[("mm", mv)])
    dma("sync", "dW", ar_in[0:1, 2 * D:2 * D + RANK].rearrange("one (j c) -> (one j) c", c=P),
        mixbuf[0:8, 512:640], wait=prog.after("vx"))

    prog.emit("gpsimd",
              lambda e: e.collective_compute("AllReduce", ALU.max,
                                             replica_groups=[list(range(NCORES))],
                                             ins=[ar_in.ap()], outs=[ar_out.ap()]),
              wait=prog.after("dW"), inc=[("cc", 1)])
    dma("sync", "dW", ROW(20480, 20480 + ARN), ar_out.ap(), wait=prog.after("cc"))
    AR_DONE = prog.after("dW")
    if dev == 22:
        return early_finish()

    US8, US4, I8, I4 = 16384, 4096, 20480, 24576
    U16A, U16B = 16384, 4096
    SC, ISC = 29696, 30720
    V(lambda e: e.tensor_scalar(ROW(US8, US8 + D), ROW(20480, 24576), 1e-8, 1.0 / 127.0,
                                ALU.max, ALU.mult), wait=AR_DONE)
    V(lambda e: e.tensor_scalar(ROW(US4, US4 + D), ROW(24576, 28672), 1e-8, 1.0 / 7.0,
                                ALU.max, ALU.mult))
    V(lambda e: e.reciprocal(ROW(I8, I8 + D), ROW(US8, US8 + D)))
    V(lambda e: e.reciprocal(ROW(I4, I4 + D), ROW(US4, US4 + D)))
    for k in range(4):
        V(lambda e, k=k: e.tensor_copy(f16row[:, :], ROW(US8 + 1024 * k, US8 + 1024 * (k + 1))))
        V(lambda e, k=k: e.tensor_copy(ROW(U16A + 1024 * k, U16A + 1024 * (k + 1)), f16row[:, :]))
    for k in range(4):
        V(lambda e, k=k: e.tensor_copy(f16row[:, :], ROW(US4 + 1024 * k, US4 + 1024 * (k + 1))))
        V(lambda e, k=k: e.tensor_copy(ROW(U16B + 1024 * k, U16B + 1024 * (k + 1)), f16row[:, :]))
    V(lambda e: e.tensor_scalar(ROW(SC + RANK, SC + 2 * RANK), ROW(13312, 14336),
                                1.0 / 127.0, None, ALU.mult))
    V(lambda e: e.scalar_tensor_tensor(ROW(SC + RANK, SC + 2 * RANK), ROW(31744, 32768),
                                       1.0 / 7.0, ROW(SC + RANK, SC + 2 * RANK),
                                       ALU.mult, ALU.add))
    V(lambda e: e.tensor_scalar(ROW(SC, SC + RANK), ROW(28672, 29696), 1e-8, None,
                                ALU.max))
    V(lambda e: e.tensor_tensor(ROW(SC, SC + RANK), ROW(SC, SC + RANK),
                                ROW(SC + RANK, SC + 2 * RANK), ALU.mult))
    V(lambda e: e.reciprocal(ROW(ISC, ISC + RANK), ROW(SC, SC + RANK)))
    V(lambda e: e.tensor_copy(f16row[:, 0:RANK], ROW(SC, SC + RANK)))
    V(lambda e: e.tensor_copy(ROW(28672, 29696), f16row[:, 0:RANK]))
    dma("sync", "dO", rowstash.ap()[2:3, :], ROW(28672, 29696), wait=prog.after("vx"))

    if dev == 23:
        return early_finish()
    for oc in range(8):
        sl = slice(512 * oc, 512 * (oc + 1))
        T(lambda e, oc=oc: e.matmul(pkA[:, 0:512], onerow_f[:, :],
                                    ROW(I8 + 512 * oc, I8 + 512 * (oc + 1)),
                                    start=True, stop=True),
          wait=prog.after("vx"), inc=[])
        T(lambda e, oc=oc: e.matmul(pkB[:, 0:512], onerow_f[:, :],
                                    ROW(I4 + 512 * oc, I4 + 512 * (oc + 1)),
                                    start=True, stop=True), inc=[])
        T(lambda e, oc=oc: e.matmul(pkC[:, 0:512], onerow_f[:, :],
                                    ROW(U16A + 512 * oc, U16A + 512 * (oc + 1)),
                                    start=True, stop=True), inc=[])
        T(lambda e, oc=oc: e.matmul(pkD[:, 0:512], onerow_f[:, :],
                                    ROW(U16B + 512 * oc, U16B + 512 * (oc + 1)),
                                    start=True, stop=True), inc=[("mm", 1)])
        mmv = prog.val("mm")
        V(lambda e: e.tensor_scalar(XW(2 * D, 2 * D + 512), pkA[:, 0:512],
                                    mh_col[:, 0:1], None, ALU.mult), wait=[("mm", mmv)])
        V(lambda e: e.scalar_tensor_tensor(XW(2 * D, 2 * D + 512), pkB[:, 0:512],
                                           ml_col[:, 0:1], XW(2 * D, 2 * D + 512),
                                           ALU.mult, ALU.add))
        V(lambda e: e.tensor_scalar(XW(3 * D, 3 * D + 512), pkC[:, 0:512],
                                    mh_col[:, 0:1], None, ALU.mult))
        V(lambda e: e.scalar_tensor_tensor(XW(3 * D, 3 * D + 512), pkD[:, 0:512],
                                           ml_col[:, 0:1], XW(3 * D, 3 * D + 512),
                                           ALU.mult, ALU.add))
        V(lambda e, sl=sl: e.tensor_tensor(XW(2 * D, 2 * D + 512), XW(0, D)[:, sl],
                                           XW(2 * D, 2 * D + 512), ALU.mult))
        V(lambda e: e.tensor_scalar(XW(2 * D, 2 * D + 512), XW(2 * D, 2 * D + 512),
                                    MAGIC, MAGIC, ALU.add, ALU.subtract))
        V(lambda e, sl=sl: e.tensor_tensor(big[:, sl], XW(2 * D, 2 * D + 512),
                                           XW(3 * D, 3 * D + 512), ALU.mult))
    dma("sync", "dO", contrib[BT_OFF: BT_OFF + P * D].rearrange("(p f) -> p f", f=D),
        big[:, 0:D], wait=prog.after("vx"))

    if dev == 24:
        return early_finish()
    T(lambda e: e.matmul(pkA[:, 0:512], onerow_f[:, :], ROW(ISC, ISC + 512),
                         start=True, stop=True), wait=prog.after("vx"), inc=[])
    T(lambda e: e.matmul(pkA[:, 512:1024], onerow_f[:, :], ROW(ISC + 512, ISC + 1024),
                         start=True, stop=True), inc=[("mm", 1)])
    mmv = prog.val("mm")
    for vt in range(4):
        dma("scalar", "dS", XW(0, RANK), v_slT[P * vt:P * (vt + 1), :],
            wait=prog.after("vx") if vt else prog.after("dO", "vx"))
        V(lambda e: e.tensor_tensor(XW(RANK, 2 * RANK), XW(0, RANK), pkA[:, 0:RANK], ALU.mult),
          wait=[("mm", mmv)] + prog.after("dS"))
        V(lambda e, vt=vt: e.tensor_scalar(big[:, D + RANK * vt: D + RANK * (vt + 1)],
                                           XW(RANK, 2 * RANK), MAGIC, MAGIC,
                                           ALU.add, ALU.subtract))
    for vt in range(4):
        dma("sync", "dO", contrib[VQ_OFF + vt * P * RANK: VQ_OFF + (vt + 1) * P * RANK]
            .rearrange("(p f) -> p f", f=RANK), big[:, D + RANK * vt:D + RANK * (vt + 1)],
            wait=prog.after("vx"))

    for wt in range(4):
        dma("gpsimd", "dG", big[:, 8192:12288], wo_slT[P * wt:P * (wt + 1), :],
            wait=prog.after("dO"))
        dma("sync", "dO", contrib[WO_OFF + wt * P * D: WO_OFF + (wt + 1) * P * D]
            .rearrange("(p f) -> p f", f=D), big[:, 8192:12288], wait=prog.after("dG"))

    if dev == 25:
        return early_finish()
    for mt in range(32):
        dma("gpsimd", "dG", XW(0, 512), l0_mask_sl[P * mt:P * (mt + 1), :],
            wait=prog.after("vx", "mm"))
        V(lambda e: e.reduce_sum(XW(512, 528), XW(0, 512).rearrange("p (g k) -> p g k", k=32),
                                 axis=AX.X), wait=prog.after("dG"))
        V(lambda e: e.tensor_scalar(wbuf[:, 0:16], XW(512, 528), 0.5, None, ALU.is_gt))
        T(lambda e: e.matmul(pkA[0:4, 0:16], P32[:, :], wbuf[:, 0:16], start=True, stop=True),
          wait=prog.after("vx"), inc=[("mm", 1)])
        V(lambda e, mt=mt: e.tensor_scalar(hbuf[0:4, 16 * mt:16 * (mt + 1)], pkA[0:4, 0:16],
                                           0.5, None, ALU.is_gt), wait=prog.after("mm"))
    for mt in range(32):
        T(lambda e, mt=mt: e.transpose(pkC[0:16, 4 * mt:4 * mt + 4],
                                       hbuf[0:4, 16 * mt:16 * (mt + 1)], identF[0:4, 0:4]),
          wait=prog.after("vx"), inc=[("mm", 1)] if mt == 31 else [])
    mmv = prog.val("mm")
    V(lambda e: e.tensor_copy(sk_t[0:16, 0:P], pkC[0:16, 0:P]), wait=[("mm", mmv)])
    for it in range(4):
        dma("scalar", "dS", XW(D, 2 * D), l0_wt_sl[P * it:P * (it + 1), :],
            wait=prog.after("vx"))
        if it == 0:
            for s32 in range(32):
                V(lambda e, s32=s32: e.tensor_copy(
                    big[0:16, 2 * D:3 * D].rearrange("p (n s) -> p n s", s=32)
                    [:, :, s32:s32 + 1],
                    sk_t[0:16, 0:P].rearrange("p (n o) -> p n o", o=1)),
                  wait=prog.after("mm") if s32 == 0 else ())
        BMT_V = prog.val("vx")
        for oc in range(8):
            sl = slice(512 * oc, 512 * (oc + 1))
            T(lambda e, it=it, sl=sl: e.matmul(
                pkA[:, 0:512], Eit[:, P * it:P * (it + 1)],
                big[0:16, 2 * D + sl.start:2 * D + sl.stop],
                start=True, stop=True),
              wait=[("vx", BMT_V)] + prog.after("dS", "vx"), inc=[("mm", 1)])
            V(lambda e, sl=sl: e.tensor_tensor(big[:, D + sl.start:D + sl.stop],
                                               XW(D, 2 * D)[:, sl], pkA[:, 0:512], ALU.mult),
              wait=prog.after("mm"))
        dma("sync", "dO", contrib[WT_OFF + it * P * D: WT_OFF + (it + 1) * P * D]
            .rearrange("(p f) -> p f", f=D), big[:, D:2 * D], wait=prog.after("vx"))

    if dev == 26:
        return early_finish()
    prog.emit("gpsimd",
              lambda e: e.collective_compute("AllGather", ALU.bypass,
                                             replica_groups=[list(range(NCORES))],
                                             ins=[contrib.ap()], outs=[agw.ap()]),
              wait=prog.after("dO"), inc=[("cc", 1)])
    AGW_DONE = prog.after("cc")
    if dev == 5:
        for ic in range(4):
            dma("gpsimd", "dG", big[:, 0:1024], agw_vq(ic, 0, 1024),
                wait=list(AGW_DONE) + prog.after("vx", "dO"))
            V(lambda e: e.tensor_copy(mixbuf[:, 0:1024], big[:, 0:1024]),
              wait=prog.after("dG"))
            dma("sync", "dO", out[P * ic:P * (ic + 1), 1024:2048], mixbuf[:, 0:1024],
                wait=prog.after("vx"))
        for q in range(4):
            dma("gpsimd", "dG", big[:, 0:1024], agw_bt(0, 1024 * q, 1024),
                wait=prog.after("vx", "dO"))
            V(lambda e: e.tensor_copy(mixbuf[:, 0:1024], big[:, 0:1024]),
              wait=prog.after("dG"))
            dma("sync", "dO", out[512:640, 1024 * q:1024 * (q + 1)], mixbuf[:, 0:1024],
                wait=prog.after("vx"))
        for rw in range(7):
            dma("sync", "dW", skrow[0:1, :], rowstash.ap()[rw:rw + 1, :],
                wait=prog.after("dO"))
            dma("sync", "dO", out[640 + rw:641 + rw, 0:1024], skrow[0:1, :],
                wait=prog.after("dW"))
        return early_finish()
    if dev == 27:
        return early_finish()

    def big_proj(pre_grp, rhs_fn, lhs_fn, n_groups, nob, nk, evac_fn, bias_row_fn=None):
        kguards = []
        for grp in range(n_groups):
            extra = pre_grp(grp) if pre_grp else []
            for ob in range(nob):
                pk0, pk1 = (pkA, pkB) if ob % 2 == 0 else (pkC, pkD)
                banks = [(pk0, 0), (pk0, 512), (pk1, 0), (pk1, 512)]
                br = bias_row_fn(grp, ob) if bias_row_fn is not None else None
                brw = []
                if isinstance(br, tuple):
                    brw = prog.after("dG")
                    br = br[1]
                for k in range(nk):
                    gw = [("mm", kguards[-2])] if len(kguards) >= 2 else None
                    wsl = rhs_fn(grp, ob, k, gw)
                    wv = prog.after("dW")
                    for ti in range(4):
                        pk, off = banks[ti]
                        w = list(wv) if ti == 0 else []
                        if k == 0 and ti == 0:
                            w += prog.after("vx") + list(extra)
                        if k == 0 and br is not None:
                            T(lambda e, pk=pk, off=off, br=br: e.matmul(
                                pk[:, off:off + 512], ones_bf[:, 0:P], br,
                                start=True, stop=False), wait=list(w) + list(brw), inc=[])
                            w = []
                        lhs = lhs_fn(grp, ti, k)
                        T(lambda e, pk=pk, off=off, lhs=lhs, wsl=wsl, k=k, nk=nk,
                          s=(k == 0 and bias_row_fn is None): e.matmul(
                              pk[:, off:off + 512], lhs, wsl, start=s, stop=(k == nk - 1)),
                          wait=w, inc=[("mm", 1)] if ti == 3 else [])
                    kguards.append(prog.val("mm"))
                mmv = prog.val("mm")
                for ti in range(4):
                    pk, off = banks[ti]
                    evac_fn(grp, ob, ti, pk, off, mmv)

    def p1_pre(grp):
        for ic in range(NIF):
            dma("gpsimd", "dG", big[:, 512 * ic:512 * (ic + 1)],
                x0_fm[P * ic:P * (ic + 1), 512 * grp:512 * (grp + 1)],
                wait=(AGW_DONE + prog.after("mm")) if ic == 0 else ())
        return prog.after("dG")

    def p1_rhs(grp, ob, ic, gw):
        wsl = wbuf[:, 512 * (ic % 2):512 * (ic % 2) + 512]
        dma("sync", "dW", wsl, agw_wt(ic, 512 * ob, 512),
            wait=(gw if gw is not None else list(AGW_DONE)))
        return wsl

    def p1_lhs(grp, ti, ic):
        return big[:, 512 * ic + P * ti:512 * ic + P * (ti + 1)]

    def p1_bias(grp, ob):
        dma("gpsimd", "dG", wbuf2[0:1, 0:512], l0_bias[0:1, 512 * ob:512 * (ob + 1)],
            wait=prog.after("mm"))
        return ("DG", wbuf2[0:1, 0:512])

    def p1_evac(grp, ob, ti, pk, off, mmv):
        tt = 4 * grp + ti
        xb = x0buf[:, 512 * (ti % 2):512 * (ti % 2) + 512]
        dma("scalar", "dS", xb, x0_tm[P * tt:P * (tt + 1), 512 * ob:512 * (ob + 1)],
            wait=prog.after("vx"))
        dst = xs[:, D * tt + 512 * ob: D * tt + 512 * (ob + 1)]
        V(lambda e, dst=dst, pk=pk, off=off, xb=xb, tt=tt, ob=ob: e.scalar_tensor_tensor(
            dst, pk[:, off:off + 512], 1.0, xb, ALU.mult, ALU.add,
            accum_out=sums_s[:, 8 * tt + ob: 8 * tt + ob + 1]),
          wait=[("mm", mmv)] + prog.after("dS"))
        Sc(lambda e, dst=dst, tt=tt, ob=ob: e.activation(
            mixbuf[:, 0:512], dst, AF.Square,
            accum_out=sqs_s[:, 8 * tt + ob: 8 * tt + ob + 1]), wait=prog.after("vx"))

    big_proj(p1_pre, p1_rhs, p1_lhs, 2, 8, NIF, p1_evac, bias_row_fn=p1_bias)
    if dev == 28:
        return early_finish()

    def layernorm(tt, rinv_col, nmr_col, rms_col=None):
        r = stat
        V(lambda e, tt=tt: e.reduce_sum(r[:, 48:49], sums_s[:, 8 * tt:8 * tt + 8], axis=AX.X),
          wait=prog.after("sx"))
        V(lambda e, tt=tt: e.reduce_sum(r[:, 49:50], sqs_s[:, 8 * tt:8 * tt + 8], axis=AX.X))
        V(lambda e: e.tensor_scalar(r[:, 50:51], r[:, 48:49], 1.0 / D, None, ALU.mult))
        V(lambda e: e.tensor_tensor(r[:, 51:52], r[:, 50:51], r[:, 50:51], ALU.mult))
        V(lambda e: e.scalar_tensor_tensor(r[:, 52:53], r[:, 49:50], 1.0 / D, r[:, 51:52],
                                           ALU.mult, ALU.subtract))
        V(lambda e: e.tensor_scalar(r[:, 53:54], r[:, 52:53], LN_EPS, None, ALU.add))
        Sc(lambda e: e.activation(r[:, 54:55], r[:, 53:54], AF.Sqrt), wait=prog.after("vx"))
        V(lambda e: e.reciprocal(r[:, 55:56], r[:, 54:55]), wait=prog.after("sx"))
        V(lambda e: e.tensor_tensor(r[:, 56:57], r[:, 55:56], r[:, 55:56], ALU.mult))
        V(lambda e: e.tensor_tensor(r[:, 57:58], r[:, 56:57], r[:, 53:54], ALU.mult))
        V(lambda e: e.tensor_scalar(r[:, 58:59], r[:, 57:58], -0.5, 1.5, ALU.mult, ALU.add))
        V(lambda e, rc=rinv_col: e.tensor_tensor(rc, r[:, 55:56], r[:, 58:59], ALU.mult))
        V(lambda e, rc=rinv_col, nm=nmr_col: e.scalar_tensor_tensor(
            nm, r[:, 50:51], -1.0, rc, ALU.mult, ALU.mult))
        if rms_col is not None:
            V(lambda e, rc=rinv_col: e.tensor_tensor(r[:, 56:57], rc, rc, ALU.mult))
            V(lambda e: e.tensor_tensor(r[:, 57:58], r[:, 56:57], r[:, 52:53], ALU.mult))
            V(lambda e: e.tensor_scalar(r[:, 57:58], r[:, 57:58], RMS_EPS, None, ALU.add))
            Sc(lambda e: e.activation(r[:, 59:60], r[:, 57:58], AF.Sqrt), wait=prog.after("vx"))
            V(lambda e: e.reciprocal(r[:, 60:61], r[:, 59:60]), wait=prog.after("sx"))
            V(lambda e: e.tensor_tensor(r[:, 56:57], r[:, 60:61], r[:, 60:61], ALU.mult))
            V(lambda e: e.tensor_tensor(r[:, 58:59], r[:, 56:57], r[:, 57:58], ALU.mult))
            V(lambda e: e.tensor_scalar(r[:, 58:59], r[:, 58:59], -0.5, 1.5, ALU.mult, ALU.add))
            V(lambda e, xc=rms_col: e.tensor_tensor(xc, r[:, 60:61], r[:, 58:59], ALU.mult))
        V(lambda e, tt=tt, rc=rinv_col, nm=nmr_col: e.tensor_scalar(
            xs[:, D * tt:D * (tt + 1)], xs[:, D * tt:D * (tt + 1)], rc, nm,
            ALU.mult, ALU.add))

    for tt in range(NTT):
        layernorm(tt, stat[:, tt:tt + 1], stat[:, 8 + tt:9 + tt],
                  rms_col=stat[:, 16 + tt:17 + tt])

    def dump_and_finish(key):
        for tt in range(NTT):
            dma("sync", "dO", dv[key][P * tt:P * (tt + 1), :], xs[:, D * tt:D * (tt + 1)],
                wait=prog.after("sx", "vx"))
        for tt in range(NTT):
            dma("sync", "dO", out[P * tt:P * (tt + 1), :], xs[:, D * tt:D * (tt + 1)])
        prog.emit("sync", lambda e: e.nop(), wait=prog.after("dO"), inc=[])
        finish(nc, st, prog, sems, semnames)
        return nc

    if dev == 1:
        return dump_and_finish("x1")

    WCAT0, X2ST, XPF, BR0 = 4096, 2048, 1024, 4864
    dma("gpsimd", "dG", big[:, WCAT0:WCAT0 + 32 * 24].rearrange("p (c k) -> p c k", k=24),
        wcatT.rearrange("(c p) k -> p c k", p=P), wait=prog.after("mm", "sx"))
    dma("sync", "dW", skrow[0:1, 0:24], hbases)
    dma("sync", "dW", skrow[0:1, 24:25], gate_in)
    WCAT_L = prog.after("dG", "dW")

    for ic in range(NIF):
        for tt in range(NTT):
            T(lambda e, ic=ic, tt=tt: e.transpose(
                pkC[:, P * tt:P * (tt + 1)],
                xs[:, D * tt + P * ic: D * tt + P * (ic + 1)], identF[:, :]),
              wait=prog.after("sx", "vx") if tt == 0 else (),
              inc=[("mm", 1)] if tt == NTT - 1 else [])
        mmv = prog.val("mm")
        V(lambda e: e.tensor_copy(big[:, 0:1024], pkC[:, 0:1024]), wait=[("mm", mmv)])
        vxv = prog.val("vx")
        for half in range(2):
            T(lambda e, ic=ic, half=half: e.matmul(
                pkD[0:24, 512 * half:512 * (half + 1)],
                big[:, WCAT0 + 24 * ic:WCAT0 + 24 * (ic + 1)],
                big[:, 512 * half:512 * (half + 1)],
                start=(ic == 0), stop=(ic == NIF - 1)),
              wait=([("vx", vxv)] + (list(WCAT_L) if ic == 0 else [])),
              inc=[("mm", 1)] if (ic == NIF - 1 and half == 1) else [])
    LOGITS_MM = prog.val("mm")
    V(lambda e: e.tensor_copy(hbuf[:, :], pkD[0:24, :]), wait=[("mm", LOGITS_MM)])
    T(lambda e: e.matmul(pkC[:, 0:24], onerow_f[:, :], skrow[0:1, 0:24],
                         start=True, stop=True),
      wait=prog.after("vx") + list(WCAT_L), inc=[("mm", 1)])
    BASES_MM = prog.val("mm")
    V(lambda e: e.tensor_copy(y1g[:, 0:24], pkC[:, 0:24]), wait=[("mm", BASES_MM)])
    for tt in range(NTT):
        T(lambda e, tt=tt: e.transpose(pkD[:, 0:24], hbuf[:, P * tt:P * (tt + 1)],
                                       identF[0:24, 0:24]),
          wait=prog.after("vx"), inc=[("mm", 1)])
        mv = prog.val("mm")
        V(lambda e, tt=tt: e.scalar_tensor_tensor(
            x0buf[:, 24 * tt:24 * tt + 24], pkD[:, 0:24], stat[:, 16 + tt:17 + tt],
            y1g[:, 0:24], ALU.mult, ALU.add), wait=[("mm", mv)])
    hview = x0buf[:, 0:192].rearrange("p (t j) -> p t j", j=24)
    for r in range(4):
        V(lambda e, r=r: e.reduce_max(bass.AP(sk_t, r, [[P, P], [4, 8]]),
                                      bass.AP(x0buf, 4 * r, [[1024, P], [24, 8], [1, 4]]),
                                      axis=AX.X))
    for c in range(4):
        V(lambda e, c=c: e.tensor_tensor(bass.AP(x0buf, c, [[1024, P], [24, 8], [4, 4]]),
                                         bass.AP(x0buf, c, [[1024, P], [24, 8], [4, 4]]),
                                         bass.AP(sk_t, 0, [[P, P], [4, 8], [1, 4]]),
                                         ALU.subtract))
    Sc(lambda e: e.activation(hview[:, :, 0:16], hview[:, :, 0:16], AF.Exp,
                              scale=INV_TAU), wait=prog.after("vx"))
    Sc(lambda e: e.activation(hview[:, :, 16:24], hview[:, :, 16:24], AF.Exp))
    V(lambda e: e.reduce_sum(sk_t[:, 0:8], hview[:, :, 16:20], axis=AX.X),
      wait=prog.after("sx"))
    V(lambda e: e.reduce_sum(sk_t[:, 8:16], hview[:, :, 20:24], axis=AX.X))
    V(lambda e: e.reciprocal(sk_t[:, 16:24], sk_t[:, 0:8]))
    V(lambda e: e.reciprocal(sk_t[:, 24:32], sk_t[:, 8:16]))
    for j in range(4):
        V(lambda e, j=j: e.tensor_tensor(hview[:, :, 16 + j:17 + j], hview[:, :, 16 + j:17 + j],
                                         sk_t[:, 16:24].rearrange("p (t o) -> p t o", o=1),
                                         ALU.mult))
        V(lambda e, j=j: e.tensor_tensor(hview[:, :, 20 + j:21 + j], hview[:, :, 20 + j:21 + j],
                                         sk_t[:, 24:32].rearrange("p (t o) -> p t o", o=1),
                                         ALU.mult))
    def pres_r(r):
        return bass.AP(x0buf, 4 * r, [[1024, P], [24, 8], [1, 4]])

    def pres_c(c):
        return bass.AP(x0buf, c, [[1024, P], [24, 8], [4, 4]])
    aview = sk_a[:, :].rearrange("p (t r) -> p t r", r=4)
    bview = sk_b[:, :].rearrange("p (t c) -> p t c", c=4)
    V(lambda e: e.memset(sk_b[:, :], 1.0))
    for it in range(SINK_ITERS):
        for r in range(4):
            V(lambda e, r=r: e.tensor_tensor(
                sk_t[:, 32 * r:32 * r + 32].rearrange("p (t c) -> p t c", c=4),
                pres_r(r), bview, ALU.mult))
        for r in range(4):
            V(lambda e, r=r: e.reduce_sum(
                aview[:, :, r:r + 1],
                sk_t[:, 32 * r:32 * r + 32].rearrange("p (t c) -> p t c", c=4), axis=AX.X))
        V(lambda e: e.reciprocal(sk_a[:, :], sk_a[:, :]))
        for c in range(4):
            V(lambda e, c=c: e.tensor_tensor(
                sk_t[:, 32 * c:32 * c + 32].rearrange("p (t r) -> p t r", r=4),
                pres_c(c), aview, ALU.mult))
        for c in range(4):
            V(lambda e, c=c: e.reduce_sum(
                bview[:, :, c:c + 1],
                sk_t[:, 32 * c:32 * c + 32].rearrange("p (t r) -> p t r", r=4), axis=AX.X))
        V(lambda e: e.reciprocal(sk_b[:, :], sk_b[:, :]))
    for r in range(4):
        V(lambda e, r=r: e.tensor_tensor(pres_r(r), pres_r(r), bview, ALU.mult))
    for c in range(4):
        V(lambda e, c=c: e.tensor_tensor(pres_c(c), pres_c(c), aview, ALU.mult))
    Sc(lambda e: e.activation(skrow[0:1, 25:26], skrow[0:1, 24:25], AF.Sigmoid))
    sxv = prog.val("sx")
    T(lambda e: e.matmul(pkD[:, 32:33], onerow_f[:, :], skrow[0:1, 25:26],
                         start=True, stop=True), wait=[("sx", sxv)] + prog.after("vx"),
      inc=[("mm", 1)])
    mmv = prog.val("mm")
    V(lambda e: e.tensor_copy(gcol[:, :], pkD[:, 32:33]), wait=[("mm", mmv)])
    V(lambda e: e.tensor_scalar(stat[:, 60:61], gcol[:, 0:1], -1.0, 1.0, ALU.mult, ALU.add))
    V(lambda e: e.tensor_scalar(hview[:, :, 0:16], hview[:, :, 0:16], gcol[:, 0:1],
                                None, ALU.mult))
    V(lambda e: e.tensor_scalar(bass.AP(x0buf, 0, [[1024, P], [24, 8], [5, 4]]),
                                bass.AP(x0buf, 0, [[1024, P], [24, 8], [5, 4]]),
                                stat[:, 60:61], None, ALU.add))
    if dev == 3:
        dma("sync", "dO", out[0:P, 256:448], x0buf[:, 0:192], wait=prog.after("vx", "sx"))
        dma("sync", "dO", out[0:P, 1024:1056], sk_a[:, :])
        dma("sync", "dO", out[0:P, 2048:2080], sk_b[:, :])
        dma("sync", "dO", out[0:P, 3072:3136], stat[:, 0:64])
        return early_finish()
    for ci in range(8):
        dma("gpsimd", "dG", big[:, BR0 + 1024 * ci:BR0 + 1024 * (ci + 1)],
            wbr[P * ci:P * (ci + 1), :], wait=prog.after("mm") if ci == 0 else ())
    WBR_L = prog.after("dG")

    for tt in range(NTT):
        hcol = lambda j, tt=tt: x0buf[:, 24 * tt + j:24 * tt + j + 1]
        V(lambda e, tt=tt, hcol=hcol: e.tensor_scalar(
            mixbuf[:, :], xs[:, D * tt:D * tt + 1024], hcol(16), None, ALU.mult),
          wait=prog.after("mm"))
        for j in range(1, 4):
            V(lambda e, tt=tt, j=j, hcol=hcol: e.scalar_tensor_tensor(
                mixbuf[:, :], xs[:, D * tt + 1024 * j:D * tt + 1024 * (j + 1)],
                hcol(16 + j), mixbuf[:, :], ALU.mult, ALU.add))
        for ci in range(8):
            T(lambda e, ci=ci: e.transpose(pkC[:, P * ci:P * (ci + 1)],
                                           mixbuf[:, P * ci:P * (ci + 1)], identF[:, :]),
              wait=prog.after("vx") if ci == 0 else (), inc=[("mm", 1)] if ci == 7 else [])
        mv = prog.val("mm")
        V(lambda e: e.tensor_copy(big[:, XPF:XPF + 1024], pkC[:, 0:1024]), wait=[("mm", mv)])
        vv = prog.val("vx")
        for ci in range(8):
            for half in range(2):
                T(lambda e, ci=ci, half=half: e.matmul(
                    pkA[:, 512 * half:512 * (half + 1)],
                    big[:, XPF + P * ci:XPF + P * (ci + 1)],
                    big[:, BR0 + 1024 * ci + 512 * half:BR0 + 1024 * ci + 512 * (half + 1)],
                    start=(ci == 0), stop=(ci == 7)),
                  wait=([("vx", vv)] + (list(WBR_L) if tt == 0 else [])) if (ci == 0 and half == 0) else (),
                  inc=[("mm", 1)] if (ci == 7 and half == 1) else [])
        YMM = prog.val("mm")
        for q in range(4):
            for j in range(4):
                V(lambda e, tt=tt, q=q, j=j: e.tensor_copy(
                    mixbuf[:, 256 * j:256 * (j + 1)],
                    xs[:, D * tt + 1024 * j + 256 * q: D * tt + 1024 * j + 256 * (q + 1)]),
                  wait=[("mm", YMM)] if (q == 0 and j == 0) else ())
            for i in range(4):
                dst = xs[:, D * tt + 1024 * i + 256 * q: D * tt + 1024 * i + 256 * (q + 1)]
                for j in range(4):
                    V(lambda e, dst=dst, hcol=hcol, i=i, j=j: e.scalar_tensor_tensor(
                        dst, mixbuf[:, 256 * j:256 * (j + 1)], hcol(4 * i + j), dst,
                        ALU.mult, ALU.add))
                V(lambda e, dst=dst, hcol=hcol, i=i, q=q: e.scalar_tensor_tensor(
                    dst, pkA[:, 256 * q:256 * (q + 1)], hcol(20 + i), dst, ALU.mult, ALU.add))
        for ob in range(8):
            seg = xs[:, D * tt + 512 * ob: D * tt + 512 * (ob + 1)]
            V(lambda e, seg=seg, tt=tt, ob=ob: e.tensor_scalar(
                seg, seg, 0.0, 0.0, ALU.add, ALU.add,
                accum_out=sums_s[:, 8 * tt + ob:8 * tt + ob + 1]))
            Sc(lambda e, seg=seg, tt=tt, ob=ob: e.activation(
                mixbuf[:, 0:512], seg, AF.Square,
                accum_out=sqs_s[:, 8 * tt + ob:8 * tt + ob + 1]), wait=prog.after("vx"))
        layernorm(tt, stat[:, tt:tt + 1], stat[:, 8 + tt:9 + tt])
        for icb in range(4):
            for k8 in range(8):
                ic = 8 * (icb % 2) + k8 if False else icb * 8 + k8
                pk = pkC if icb % 2 == 0 else pkD
                T(lambda e, tt=tt, ic=ic, k8=k8, pk=pk: e.transpose(
                    pk[:, P * k8:P * (k8 + 1)],
                    xs[:, D * tt + P * ic: D * tt + P * (ic + 1)], identF[:, :]),
                  wait=prog.after("sx", "vx") if k8 == 0 else (),
                  inc=[("mm", 1)] if k8 == 7 else [])
            mv = prog.val("mm")
            pk = pkC if icb % 2 == 0 else pkD
            stg = big[:, X2ST + 1024 * (icb % 2):X2ST + 1024 * (icb % 2) + 1024]
            V(lambda e, pk=pk, stg=stg: e.tensor_copy(stg, pk[:, 0:1024]), wait=[("mm", mv)])
            dma("sync", "dO",
                x2c.ap()[P * 8 * icb:P * 8 * (icb + 1), P * tt:P * (tt + 1)]
                .rearrange("(c p) t -> p c t", p=P),
                stg.rearrange("p (c t) -> p c t", t=P), wait=prog.after("vx"))
    prog.emit("gpsimd",
              lambda e: e.collective_compute("AllGather", ALU.bypass,
                                             replica_groups=[list(range(NCORES))],
                                             ins=[x2c.ap()], outs=[x2full.ap()]),
              wait=prog.after("dO"), inc=[("cc", 1)])
    X2F = prog.after("cc")

    if dev == 2:
        return dump_and_finish("x2")

    for ic in range(NIF):
        dma("gpsimd", "dG", big[:, 512 * ic:512 * (ic + 1)],
            wq_slT[P * ic:P * (ic + 1), :], wait=prog.after("mm", "sx") if ic == 0 else ())
    WQ_L = prog.after("dG")
    qguards = []
    for tb in range(16):
        pk0, pk1 = (pkA, pkB) if tb % 2 == 0 else (pkC, pkD)
        banks = [(pk0, 0), (pk0, 512), (pk1, 0), (pk1, 512)]
        tglob = 512 * tb
        r, tloc = tglob // TCORE, tglob % TCORE
        for ic in range(NIF):
            buf = wbuf[:, 512 * (ic % 2):512 * (ic % 2) + 512]
            gw = [("mm", qguards[-2])] if len(qguards) >= 2 else \
                (prog.after("mm") + list(X2F))
            dma("sync", "dW", buf,
                x2full.ap()[D * r + P * ic: D * r + P * (ic + 1), tloc:tloc + 512],
                wait=gw)
            wv = prog.after("dW")
            for oc in range(4):
                pk, off = banks[oc]
                w = list(wv) if oc == 0 else []
                if ic == 0 and oc == 0:
                    w += prog.after("sx") + (list(WQ_L) if tb == 0 else [])
                T(lambda e, pk=pk, off=off, ic=ic, oc=oc, buf=buf: e.matmul(
                    pk[:, off:off + 512],
                    big[:, 512 * ic + P * oc:512 * ic + P * (oc + 1)], buf,
                    start=(ic == 0), stop=(ic == NIF - 1)),
                  wait=w, inc=[("mm", 1)] if oc == 3 else [])
            qguards.append(prog.val("mm"))
        mmv = prog.val("mm")
        for oc in range(4):
            pk, off = banks[oc]
            stg = wbuf2[:, 512 * (oc % 2):512 * (oc % 2) + 512]
            Sc(lambda e, pk=pk, off=off, stg=stg: e.copy(stg, pk[:, off:off + 512]),
               wait=[("mm", mmv)] + prog.after("dO"))
            dma("sync", "dO", q_scr.ap()[P * oc:P * (oc + 1), 512 * tb:512 * (tb + 1)], stg,
                wait=prog.after("sx"))
    for ic in range(NIF):
        dma("gpsimd", "dG", big[:, 256 * ic:256 * (ic + 1)],
            wkv_slT[P * ic:P * (ic + 1), :], wait=prog.after("mm", "sx") if ic == 0 else ())
    WKV_L = prog.after("dG")
    kvguards = []
    for tb in range(16):
        pk0 = pkA if tb % 2 == 0 else pkC
        tglob = 512 * tb
        r, tloc = tglob // TCORE, tglob % TCORE
        for ic in range(NIF):
            buf = wbuf[:, 512 * (ic % 2):512 * (ic % 2) + 512]
            gw = [("mm", kvguards[-2])] if len(kvguards) >= 2 else prog.after("mm")
            dma("sync", "dW", buf,
                x2full.ap()[D * r + P * ic: D * r + P * (ic + 1), tloc:tloc + 512],
                wait=gw)
            wv = prog.after("dW")
            for oc in range(2):
                w = list(wv) if oc == 0 else []
                if ic == 0 and oc == 0:
                    w += prog.after("sx", "vx") + (list(WKV_L) if tb == 0 else [])
                T(lambda e, pk0=pk0, oc=oc, ic=ic, buf=buf: e.matmul(
                    pk0[:, 512 * oc:512 * (oc + 1)],
                    big[:, 256 * ic + P * oc:256 * ic + P * (oc + 1)], buf,
                    start=(ic == 0), stop=(ic == NIF - 1)),
                  wait=w, inc=[("mm", 1)] if oc == 1 else [])
            kvguards.append(prog.val("mm"))
        mmv = prog.val("mm")
        Sc(lambda e, pk0=pk0: e.copy(big[:, 8192:8704], pk0[:, 0:512]),
           wait=[("mm", mmv)] + prog.after("dO"))
        dma("sync", "dO", k_scr.ap()[:, 512 * tb:512 * (tb + 1)], big[:, 8192:8704],
            wait=prog.after("sx"))
        V(lambda e, pk0=pk0: e.tensor_copy(mixbuf[:, 0:512], pk0[:, 512:1024]),
          wait=[("mm", mmv)])
        vv = prog.val("vx")
        pkt = pkB if tb % 2 == 0 else pkD
        for s_ in range(4):
            T(lambda e, s_=s_, pkt=pkt: e.transpose(pkt[:, P * s_:P * (s_ + 1)],
                                                    mixbuf[:, P * s_:P * (s_ + 1)],
                                                    identF[:, :]),
              wait=[("vx", vv)] if s_ == 0 else (), inc=[("mm", 1)] if s_ == 3 else [])
        mv2 = prog.val("mm")
        V(lambda e, pkt=pkt: e.tensor_copy(big[:, 8704:9216], pkt[:, 0:512]),
          wait=[("mm", mv2)] + prog.after("dO"))
        dma("sync", "dO",
            v_scr.ap()[512 * tb:512 * (tb + 1), :].rearrange("(s p) d -> p s d", p=P),
            big[:, 8704:9216].rearrange("p (s d) -> p s d", d=P), wait=prog.after("vx"))

    KB0, VB0, QB0, OS0 = 0, 4096, 8224, 12320
    V(lambda e: e.memset(big[:, VB0:VB0 + 4128].rearrange("p (k c) -> p k c", c=129)
                         [:, :, 128:129], 1.0), wait=prog.after("dO", "mm", "sx"))
    ONES_SET = prog.after("vx")
    sx_par = {0: 0, 1: 0}
    pv_par = {0: 0, 1: 0}
    for b in range(2):
        dma("sync", "dW", big[:, KB0:KB0 + 4096], k_scr.ap()[:, S * b:S * (b + 1)],
            wait=prog.after("mm") + (list(ONES_SET) if b == 0 else []))
        dma("sync", "dW",
            big[:, VB0:VB0 + 4128].rearrange("p (k c) -> p k c", c=129)[:, :, 0:128],
            v_scr.ap()[S * b:S * (b + 1), :].rearrange("(k p) d -> p k d", p=P))
        KVL = prog.after("dW")
        for qh in range(4):
            dma("scalar", "dS", big[:, QB0:QB0 + 4096],
                q_scr.ap()[P * qh:P * (qh + 1), S * b:S * (b + 1)],
                wait=prog.after("mm"))
            QL = prog.after("dS")
            kjp_abs = 0
            for qi in range(8):
                oslots = [(pkC, 0), (pkC, 512), (pkD, 0), (pkD, 512)]
                nkj = 4 * (qi + 1)
                ev_vx = prog.val("vx")
                for kjp in range((nkj + 1) // 2):
                    par = kjp_abs % 2
                    spk = pkA if par == 0 else pkB
                    kj0 = 2 * kjp
                    npair = min(2, nkj - kj0)
                    w = [("sx", sx_par[par])] + list(KVL) + list(QL)
                    for kk in range(npair):
                        kj = kj0 + kk
                        T(lambda e, spk=spk, kk=kk, kj=kj, qi=qi: e.matmul(
                            spk[:, 512 * kk:512 * (kk + 1)],
                            big[:, KB0 + P * kj:KB0 + P * (kj + 1)],
                            big[:, QB0 + 512 * qi:QB0 + 512 * (qi + 1)],
                            start=True, stop=True),
                          wait=w if kk == 0 else (),
                          inc=[("mm", 1)] if kk == npair - 1 else [])
                    mmv = prog.val("mm")
                    pt = ptbuf[:, 1024 * par:1024 * par + 1024]
                    Sc(lambda e, spk=spk, pt=pt, npair=npair: e.activation(
                        pt[:, 0:512 * npair], spk[:, 0:512 * npair], AF.Exp,
                        scale=INV_SQRT_HD),
                       wait=[("mm", max(mmv, pv_par[par]))] if False else
                       [("mm", mmv), ("mm", pv_par[par])], inc=[("sx", 1)])
                    sx_par[par] = prog.val("sx")
                    gx_need = None
                    for kk in range(npair):
                        kj = kj0 + kk
                        d_ = kj - 4 * qi
                        if d_ >= 0:
                            G(lambda e, pt=pt, kk=kk, d_=d_: e.affine_select(
                                out=pt[:, 512 * kk:512 * (kk + 1)],
                                in_=pt[:, 512 * kk:512 * (kk + 1)],
                                compare_op=ALU.is_ge, fill=0.0,
                                pattern=[[1, 512]], base=-128 * d_,
                                channel_multiplier=-1),
                              wait=prog.after("sx"))
                            gx_need = prog.val("gx")
                    pvw = [("sx", sx_par[par]), ("vx", ev_vx)]
                    if gx_need is not None:
                        pvw.append(("gx", gx_need))
                    for kk in range(npair):
                        kj = kj0 + kk
                        for qt in range(4):
                            pko, ooff = oslots[qt]
                            T(lambda e, pko=pko, ooff=ooff, pt=pt, kk=kk, kj=kj, qt=qt,
                              nkj=nkj: e.matmul(
                                  pko[:, ooff:ooff + 129],
                                  pt[:, 512 * kk + P * qt:512 * kk + P * (qt + 1)],
                                  big[:, VB0 + 129 * kj:VB0 + 129 * (kj + 1)],
                                  start=(kj == 0), stop=(kj == nkj - 1)),
                              wait=pvw if (qt == 0 and kk == 0) else (),
                              inc=[("mm", 1)] if (kj == nkj - 1 and qt == 3) else [])
                    pv_par[par] = prog.val("mm")
                    kjp_abs += 1
                mmo = prog.val("mm")
                for qt in range(4):
                    pko, ooff = oslots[qt]
                    V(lambda e, pko=pko, ooff=ooff: e.reciprocal(
                        stat[:, 62:63], pko[:, ooff + 128:ooff + 129]),
                      wait=[("mm", mmo)] + prog.after("dO"))
                    V(lambda e, pko=pko, ooff=ooff, qt=qt: e.tensor_scalar(
                        big[:, OS0 + P * qt:OS0 + P * (qt + 1)],
                        pko[:, ooff:ooff + 128], stat[:, 62:63], None, ALU.mult))
                dma("sync", "dO",
                    c_o.ap()[S * b + 512 * qi:S * b + 512 * (qi + 1),
                             P * qh:P * (qh + 1)].rearrange("(s p) d -> p s d", p=P),
                    big[:, OS0:OS0 + 512].rearrange("p (s d) -> p s d", d=P),
                    wait=prog.after("vx"))
    prog.emit("gpsimd",
              lambda e: e.collective_compute("AllGather", ALU.bypass,
                                             replica_groups=[list(range(NCORES))],
                                             ins=[c_o.ap()], outs=[ag_o.ap()]),
              wait=prog.after("dO"), inc=[("cc", 1)])
    AGO = prog.after("cc")

    reg_cell = {}

    def p5_pre(grp):
        qeng = "gpsimd" if grp == 0 else "scalar"
        qsem = "dG" if grp == 0 else "dS"
        for i in range(NCORES):
            def fn(e, i=i, grp=grp, qeng=qeng):
                key = qeng
                if key not in reg_cell:
                    reg_cell[key] = e.alloc_register(name="ob_" + key)
                    e.reg_load(reg_cell[key], ibase[0:1, 0:1])
                base = e.snap(reg_cell[key], min_val=0, max_val=NTOK - TCORE)
                return e.dma_start(
                    out=ptbuf[:, :].rearrange("p (tt d) -> p tt d", d=512),
                    in_=ag_o.ap()[bass.ds(base + NTOK * i + 512 * grp, 512), :]
                    .rearrange("(tt p) d -> p tt d", p=P))
            prog.emit(qeng, fn,
                      wait=(list(AGO) + prog.after("mm", "sx", "vx", "dW", "gx"))
                      if (grp == 0 and i == 0) else prog.after("vx", "mm"),
                      inc=[(qsem, 16)])
            LD = prog.after(qsem)
            pview = ptbuf[:, :].rearrange("p (tt d) -> p tt d", d=512)
            for tt in range(4):
                V(lambda e, tt=tt, pview=pview: e.tensor_copy(
                    mixbuf[:, 512:1024], pview[:, tt, :]), wait=LD)
                vv = prog.val("vx")
                pk = pkC if tt % 2 == 0 else pkD
                for cc in range(4):
                    T(lambda e, cc=cc, pk=pk: e.transpose(
                        pk[:, P * cc:P * (cc + 1)], mixbuf[:, 512 + P * cc:512 + P * (cc + 1)],
                        identF[:, :]),
                      wait=([("vx", vv)] + prog.after("mm")) if cc == 0 else (),
                      inc=[("mm", 1)] if cc == 3 else [])
                mv = prog.val("mm")
                V(lambda e, i=i, tt=tt, pk=pk: e.tensor_copy(
                    bass.AP(big, 512 * 4 * i + P * tt, [[16384, P], [512, 4], [1, P]]),
                    pk[:, 0:512].rearrange("p (c d) -> p c d", d=P)), wait=[("mm", mv)])
        return prog.after("vx")

    def wo_rhs(grp, ob, fc, gw):
        wsl = wbuf[:, 512:1024] if fc % 2 else wbuf2[:, 512:1024]
        dma("sync", "dW", wsl, agw_wo(fc, 512 * ob, 512),
            wait=(gw if gw is not None else prog.after("mm")))
        return wsl

    def wo_lhs(grp, ti, fc):
        return big[:, 512 * fc + P * ti:512 * fc + P * (ti + 1)]

    def wo_evac(grp, ob, ti, pk, off, mmv):
        tt = 4 * grp + ti
        dst = xs[:, D * tt + 512 * ob: D * tt + 512 * (ob + 1)]
        V(lambda e, dst=dst, pk=pk, off=off, tt=tt, ob=ob: e.scalar_tensor_tensor(
            dst, pk[:, off:off + 512], 1.0, dst, ALU.mult, ALU.add,
            accum_out=sums_s[:, 8 * tt + ob: 8 * tt + ob + 1]), wait=[("mm", mmv)])
        Sc(lambda e, dst=dst, tt=tt, ob=ob: e.activation(
            mixbuf[:, 0:512], dst, AF.Square,
            accum_out=sqs_s[:, 8 * tt + ob:8 * tt + ob + 1]), wait=prog.after("vx"))

    big_proj(p5_pre, wo_rhs, wo_lhs, 2, 8, NIF, wo_evac)

    for tt in range(NTT):
        layernorm(tt, stat[:, tt:tt + 1], stat[:, 8 + tt:9 + tt])
        V(lambda e, tt=tt: e.reduce_max(stat[:, 16 + tt:17 + tt],
                                        xs[:, D * tt:D * (tt + 1)], axis=AX.X,
                                        apply_absolute_value=True), wait=prog.after("sx"))
        V(lambda e, tt=tt: e.tensor_scalar(stat[:, 24 + tt:25 + tt],
                                           stat[:, 16 + tt:17 + tt], 1e-8, 1.0 / 127.0,
                                           ALU.max, ALU.mult))
        V(lambda e, tt=tt: e.reciprocal(stat[:, 32 + tt:33 + tt], stat[:, 24 + tt:25 + tt]))

    if dev == 4:
        return dump_and_finish("x3")

    T(lambda e: e.transpose(pkC[0:8, 0:P], stat[:, 32:40], identF[:, :]),
      wait=prog.after("vx", "sx"), inc=[("mm", 1)])
    mmv = prog.val("mm")
    V(lambda e: e.tensor_copy(sk_t[0:8, 0:P], pkC[0:8, 0:P]), wait=[("mm", mmv)])
    dma("sync", "dW", hbuf[0:1, :], sk_t[0:8, 0:P], wait=prog.after("vx"))
    INVROW = prog.after("dW")
    dma("sync", "dW", skrow[0:1, :], rowstash.ap()[2:3, :], wait=prog.after("mm"))
    for half in range(2):
        T(lambda e, half=half: e.matmul(pkC[:, 512 * half:512 * (half + 1)], onerow_f[:, :],
                                        skrow[0:1, 512 * half:512 * (half + 1)],
                                        start=True, stop=True),
          wait=prog.after("vx", "dW"), inc=[("mm", 1)] if half else [])
    mmv = prog.val("mm")
    V(lambda e: e.tensor_copy(x0buf[:, 0:1024], pkC[:, 0:1024]), wait=[("mm", mmv)])
    BIA = 12288
    for k in range(4):
        dma("sync", "dW", skrow[0:1, :], rowstash.ap()[3 + k:4 + k, :],
            wait=prog.after("vx"))
        V(lambda e, k=k: e.tensor_copy(big[0:1, BIA + 1024 * k:BIA + 1024 * (k + 1)],
                                       skrow[0:1, :]), wait=prog.after("dW"))
    MH0, ML0, XQ0, AF0 = 8192, 9216, 0, 10240
    dma("sync", "dW", skrow[0:1, :], rowstash.ap()[0:1, :], wait=prog.after("vx"))
    for half in range(2):
        T(lambda e, half=half: e.matmul(pkC[:, 512 * half:512 * (half + 1)], onerow_f[:, :],
                                        skrow[0:1, 512 * half:512 * (half + 1)],
                                        start=True, stop=True),
          wait=prog.after("dW", "vx"), inc=[("mm", 1)] if half else [])
    mmv = prog.val("mm")
    V(lambda e: e.tensor_copy(big[:, MH0:MH0 + 1024], pkC[:, 0:1024]), wait=[("mm", mmv)])
    dma("sync", "dW", skrow[0:1, :], rowstash.ap()[1:2, :], wait=prog.after("vx"))
    for half in range(2):
        T(lambda e, half=half: e.matmul(pkD[:, 512 * half:512 * (half + 1)], onerow_f[:, :],
                                        skrow[0:1, 512 * half:512 * (half + 1)],
                                        start=True, stop=True),
          wait=prog.after("vx", "dW"), inc=[("mm", 1)] if half else [])
    mmv = prog.val("mm")
    V(lambda e: e.tensor_copy(big[:, ML0:ML0 + 1024], pkD[:, 0:1024]), wait=[("mm", mmv)])
    REPS_V = prog.val("vx")
    s1guards = []
    s2guards = []

    for grp in range(4):
        tiles = (2 * grp, 2 * grp + 1)
        T(lambda e, grp=grp: e.matmul(pkC[:, 0:256], onerow_f[:, :],
                                      hbuf[0:1, 256 * grp:256 * (grp + 1)],
                                      start=True, stop=True),
          wait=prog.after("vx") + [("vx", REPS_V)] + list(INVROW), inc=[("mm", 1)])
        mmv = prog.val("mm")
        V(lambda e: e.tensor_copy(mixbuf[:, 0:256], pkC[:, 0:256]), wait=[("mm", mmv)])
        for ic in range(NIF):
            for k2 in range(2):
                tt = tiles[k2]
                T(lambda e, ic=ic, k2=k2, tt=tt: e.transpose(
                    pkC[:, 512 + P * k2:512 + P * (k2 + 1)],
                    xs[:, D * tt + P * ic: D * tt + P * (ic + 1)], identF[:, :]),
                  wait=prog.after("vx") if k2 == 0 else (),
                  inc=[("mm", 1)] if k2 == 1 else [])
            mv = prog.val("mm")
            V(lambda e, ic=ic: e.tensor_tensor(mixbuf[:, 256:512], pkC[:, 512:768],
                                               mixbuf[:, 0:256], ALU.mult), wait=[("mm", mv)])
            V(lambda e, ic=ic: e.tensor_scalar(big[:, XQ0 + 256 * ic:XQ0 + 256 * (ic + 1)],
                                               mixbuf[:, 256:512], MAGIC, MAGIC,
                                               ALU.add, ALU.subtract))
        XQV = prog.val("vx")
        for rb in range(2):
            for ic in range(NIF):
                buf = wbuf[:, 512 * (ic % 2):512 * (ic % 2) + 512]
                gw = [("mm", s1guards[-2])] if len(s1guards) >= 2 else prog.after("mm")
                dma("sync", "dW", buf, agw_vq(ic, 512 * rb, 512), wait=gw)
                wv = prog.after("dW")
                for k2 in range(2):
                    w = list(wv) if k2 == 0 else []
                    if ic == 0 and k2 == 0:
                        w += [("vx", XQV)] + prog.after("vx")
                    T(lambda e, k2=k2, ic=ic, buf=buf: e.matmul(
                        pkA[:, 512 * k2:512 * (k2 + 1)],
                        big[:, XQ0 + 256 * ic + P * k2:XQ0 + 256 * ic + P * (k2 + 1)],
                        buf, start=(ic == 0), stop=(ic == NIF - 1)),
                      wait=w, inc=[("mm", 1)] if (k2 == 1 or ic == NIF - 1) else [])
                s1guards.append(prog.val("mm"))
            mmv = prog.val("mm")
            for k2 in range(2):
                tt = tiles[k2]
                V(lambda e, k2=k2, rb=rb, tt=tt: e.scalar_tensor_tensor(
                    y1g[:, 1024 * k2 + 512 * rb:1024 * k2 + 512 * (rb + 1)],
                    pkA[:, 512 * k2:512 * (k2 + 1)], stat[:, 24 + tt:25 + tt],
                    x0buf[:, 512 * rb:512 * (rb + 1)], ALU.mult, ALU.mult),
                  wait=[("mm", mmv)])
        if dev == 6 and grp == 0:
            dma("sync", "dO", out[128:256, 1088:1344], mixbuf[:, 0:256],
                wait=prog.after("vx"))
            dma("sync", "dO", out[0:P, 0:2048], y1g[:, :])
            V(lambda e: e.tensor_copy(mixbuf[:, 0:1024], big[:, XQ0:XQ0 + 1024]),
              wait=prog.after("dO"))
            dma("sync", "dO", out[0:P, 2048:3072], mixbuf[:, 0:1024],
                wait=prog.after("vx"))
            dma("sync", "dO", out[128:256, 0:64], stat[:, 0:64])
            dma("sync", "dO", out[128:256, 64:1088], x0buf[:, 0:1024])
            return early_finish()
        for k2 in range(2):
            tt = tiles[k2]
            y1 = y1g[:, 1024 * k2:1024 * (k2 + 1)]
            V(lambda e, y1=y1: e.tensor_tensor(mixbuf[:, :], y1, big[:, MH0:MH0 + 1024],
                                               ALU.mult))
            V(lambda e: e.reduce_max(stat[:, 40:41], mixbuf[:, :], axis=AX.X,
                                     apply_absolute_value=True))
            V(lambda e, y1=y1: e.tensor_tensor(mixbuf[:, :], y1, big[:, ML0:ML0 + 1024],
                                               ALU.mult))
            V(lambda e: e.reduce_max(stat[:, 41:42], mixbuf[:, :], axis=AX.X,
                                     apply_absolute_value=True))
            V(lambda e: e.tensor_scalar(stat[:, 42:43], stat[:, 40:41], 1e-8, 1.0 / 127.0,
                                        ALU.max, ALU.mult))
            V(lambda e: e.tensor_scalar(stat[:, 43:44], stat[:, 41:42], 1e-8, 1.0 / 127.0,
                                        ALU.max, ALU.mult))
            V(lambda e: e.reciprocal(stat[:, 44:45], stat[:, 42:43]))
            V(lambda e: e.reciprocal(stat[:, 45:46], stat[:, 43:44]))
            V(lambda e: e.tensor_scalar(mixbuf[:, :], big[:, MH0:MH0 + 1024],
                                        stat[:, 44:45], None, ALU.mult))
            V(lambda e: e.scalar_tensor_tensor(mixbuf[:, :], big[:, ML0:ML0 + 1024],
                                               stat[:, 45:46], mixbuf[:, :],
                                               ALU.mult, ALU.add))
            V(lambda e, y1=y1: e.tensor_tensor(y1, y1, mixbuf[:, :], ALU.mult))
            V(lambda e, y1=y1: e.tensor_scalar(y1, y1, MAGIC, MAGIC, ALU.add, ALU.subtract))
            V(lambda e: e.tensor_scalar(mixbuf[:, :], big[:, MH0:MH0 + 1024],
                                        stat[:, 42:43], None, ALU.mult))
            V(lambda e: e.scalar_tensor_tensor(mixbuf[:, :], big[:, ML0:ML0 + 1024],
                                               stat[:, 43:44], mixbuf[:, :],
                                               ALU.mult, ALU.add))
            V(lambda e, y1=y1: e.tensor_tensor(y1, y1, mixbuf[:, :], ALU.mult))
            vv = prog.val("vx")
            for rc in range(8):
                T(lambda e, y1=y1, rc=rc: e.transpose(
                    pkB[:, P * (rc % 8):P * (rc % 8) + P], y1[:, P * rc:P * (rc + 1)],
                    identF[:, :]),
                  wait=([("vx", vv)] + prog.after("mm")) if rc == 0 else (),
                  inc=[("mm", 1)] if rc == 7 else [])
            mv = prog.val("mm")
            V(lambda e, k2=k2: e.tensor_copy(big[:, AF0 + 1024 * k2:AF0 + 1024 * (k2 + 1)],
                                             pkB[:, 0:1024]), wait=[("mm", mv)])
        AFV = prog.val("vx")
        for ob in range(8):
            pks = pkA if ob % 2 == 0 else pkB
            for rc in range(8):
                buf = wbuf2[:, 512 * (rc % 2):512 * (rc % 2) + 512]
                gw = [("mm", s2guards[-2])] if len(s2guards) >= 2 else prog.after("mm")
                dma("scalar", "dS", buf, agw_bt(rc, 512 * ob, 512), wait=gw)
                wv = prog.after("dS")
                for k2 in range(2):
                    w = list(wv) if k2 == 0 else []
                    if rc == 0:
                        if k2 == 0:
                            w += prog.after("vx") + ([("vx", AFV)] if ob == 0 else [])
                        T(lambda e, pks=pks, k2=k2, ob=ob: e.matmul(
                            pks[:, 512 * k2:512 * (k2 + 1)], ones_bf[:, 0:P],
                            big[0:1, BIA + 512 * ob:BIA + 512 * ob + 512],
                            start=True, stop=False), wait=w, inc=[])
                        w = []
                    T(lambda e, pks=pks, k2=k2, rc=rc, buf=buf: e.matmul(
                        pks[:, 512 * k2:512 * (k2 + 1)],
                        big[:, AF0 + 1024 * k2 + P * rc:AF0 + 1024 * k2 + P * (rc + 1)],
                        buf, start=False, stop=(rc == 7)),
                      wait=w, inc=[("mm", 1)] if (k2 == 1 or rc == 7) else [])
                s2guards.append(prog.val("mm"))
            mmv = prog.val("mm")
            for k2 in range(2):
                tt = tiles[k2]
                dst = xs[:, D * tt + 512 * ob: D * tt + 512 * (ob + 1)]
                V(lambda e, dst=dst, pks=pks, k2=k2, tt=tt, ob=ob: e.scalar_tensor_tensor(
                    dst, pks[:, 512 * k2:512 * (k2 + 1)], 1.0, dst, ALU.mult, ALU.add,
                    accum_out=sums_s[:, 8 * tt + ob:8 * tt + ob + 1]), wait=[("mm", mmv)])
                Sc(lambda e, dst=dst, tt=tt, ob=ob: e.activation(
                    mixbuf[:, 0:512], dst, AF.Square,
                    accum_out=sqs_s[:, 8 * tt + ob:8 * tt + ob + 1]), wait=prog.after("vx"))
        for k2 in range(2):
            tt = tiles[k2]
            layernorm(tt, stat[:, tt:tt + 1], stat[:, 8 + tt:9 + tt])
            dma("sync", "dO", out[P * tt:P * (tt + 1), :], xs[:, D * tt:D * (tt + 1)],
                wait=prog.after("sx", "vx"))

    prog.emit("sync", lambda e: e.nop(), wait=prog.after("dO"), inc=[])
    finish(nc, st, prog, sems, semnames)
    return nc


def finish(nc, st, prog, sems, semnames):
    for name in semnames:
        sems[name] = st.enter_context(nc.semaphore(name))
    with nc.Block() as block:
        @block.tensor
        def _(e):
            prog.replay("tensor", e, sems)

        @block.vector
        def _(e):
            prog.replay("vector", e, sems)

        @block.scalar
        def _(e):
            prog.replay("scalar", e, sems)

        @block.gpsimd
        def _(e):
            prog.replay("gpsimd", e, sems)

        @block.sync
        def _(e):
            prog.replay("sync", e, sems)
    st.close()


def make_in_maps(inputs):
    x0 = np.ascontiguousarray(np.asarray(inputs["L0_attn_data"], dtype=np.float32)
                              .reshape(NTOK, D))
    x0T = np.ascontiguousarray(x0.T)
    W = np.asarray(inputs["L0_attn_weight"], dtype=np.float32)
    WT = np.ascontiguousarray(W.T)
    mask = np.asarray(inputs["L0_attn_mask"], dtype=np.int32)
    wcatT = np.ascontiguousarray(np.concatenate(
        [np.asarray(inputs["L0_ffn_W_res"], np.float32),
         np.asarray(inputs["L0_ffn_W_pre"], np.float32),
         np.asarray(inputs["L0_ffn_W_post"], np.float32)], axis=0).T)
    hb = np.concatenate([np.asarray(inputs["L0_ffn_H_res_base"], np.float32).reshape(16),
                         np.asarray(inputs["L0_ffn_H_pre_base"], np.float32).reshape(4),
                         np.asarray(inputs["L0_ffn_H_post_base"], np.float32).reshape(4)])
    Wq = np.asarray(inputs["L1_attn_Wq"], np.float32)
    Wkv = np.asarray(inputs["L1_attn_Wkv"], np.float32)
    WoT = np.ascontiguousarray(np.asarray(inputs["L1_attn_Wo"], np.float32).T)
    vT = np.ascontiguousarray(np.asarray(inputs["L1_ffn_v_weight"], np.float32).T)
    uT = np.ascontiguousarray(np.asarray(inputs["L1_ffn_u_weight"], np.float32).T)
    maps = []
    for c in range(NCORES):
        kvrows = list(range(128 * c, 128 * (c + 1))) + \
            list(range(1024 + 128 * c, 1024 + 128 * (c + 1)))
        maps.append({
            "x0_tm": np.ascontiguousarray(x0[TCORE * c:TCORE * (c + 1), :]),
            "x0_fm": np.ascontiguousarray(x0T[:, TCORE * c:TCORE * (c + 1)]),
            "l0_wt_sl": np.ascontiguousarray(WT[512 * c:512 * (c + 1), :]),
            "l0_mask_sl": np.ascontiguousarray(mask[:, 512 * c:512 * (c + 1)]),
            "l0_bias": np.ascontiguousarray(
                np.asarray(inputs["L0_attn_bias"], np.float32).reshape(1, D)),
            "wcatT": wcatT,
            "hbases": np.ascontiguousarray(hb.reshape(1, 24)),
            "gate_in": np.ascontiguousarray(
                np.asarray(inputs["L0_ffn_gate"], np.float32).reshape(1, 1)),
            "wbr": np.ascontiguousarray(np.asarray(inputs["L0_ffn_W_branch"], np.float32)),
            "wq_slT": np.ascontiguousarray(Wq[512 * c:512 * (c + 1), :].T),
            "wkv_slT": np.ascontiguousarray(Wkv[kvrows, :].T),
            "wo_slT": np.ascontiguousarray(WoT[512 * c:512 * (c + 1), :]),
            "v_slT": np.ascontiguousarray(vT[512 * c:512 * (c + 1), :]),
            "uT_sl": np.ascontiguousarray(uT[128 * c:128 * (c + 1), :]),
            "r_base_col": np.arange(128 * c, 128 * (c + 1), dtype=np.float32).reshape(P, 1),
            "hi_idx": np.ascontiguousarray(
                np.asarray(inputs["L1_ffn_high_idx"], np.int32).reshape(1, RHI)),
            "lo_idx": np.ascontiguousarray(
                np.asarray(inputs["L1_ffn_low_idx"], np.int32).reshape(1, RLO)),
            "l1_bias": np.ascontiguousarray(
                np.asarray(inputs["L1_ffn_bias"], np.float32).reshape(1, D)),
            "tok_base": np.array([[TCORE * c]], dtype=np.int32),
        })
    return maps


def kernel(**inputs):
    nc = build_program(dev=0)
    maps = make_in_maps(inputs)
    res = run_bass_kernel_spmd(nc, maps, core_ids=list(range(NCORES)))
    full = np.concatenate([res.results[c]["out"] for c in range(NCORES)], axis=0)
    return full.reshape(B, S, D).astype(np.float32)


# version 09
# speedup vs baseline: 1.0580x; 1.0580x over previous
import math
from collections import defaultdict

import numpy as np
import concourse.bass as bass
import concourse.mybir as mybir
from concourse.bass_isa import ReduceOp
from concourse.bass_utils import run_bass_kernel_spmd

F32 = mybir.dt.float32
BF16 = mybir.dt.bfloat16
F16 = mybir.dt.float16
I32 = mybir.dt.int32
AF = mybir.ActivationFunctionType
ALU = mybir.AluOpType
AX = mybir.AxisListType

NCORES = 8
P = 128
D = 4096
B = 2
S = 4096
NTOK = B * S
TCORE = NTOK // NCORES
NTT = TCORE // P
NIF = D // P
MAGIC = float(np.float32(1.5 * 2 ** 23))
LN_EPS = 1e-5
RMS_EPS = 1.1920929e-07
INV_TAU = 10.0
SINK_ITERS = 50
RANK = 1024
RHI = 256
RLO = 768
INV_SQRT_HD = 1.0 / math.sqrt(128.0)

WT_OFF = 0
VQ_OFF = 512 * D
BT_OFF = VQ_OFF + 512 * RANK
WO_OFF = BT_OFF + 128 * D
CONTRIB = WO_OFF + 512 * D
ARN = 2 * D + RANK

ENGINES = ("tensor", "vector", "scalar", "gpsimd", "sync")


class Prog:
    def __init__(self):
        self.ops = {e: [] for e in ENGINES}
        self.cnt = defaultdict(int)

    def val(self, sem):
        return self.cnt[sem]

    def emit(self, eng, fn, wait=(), inc=()):
        self.ops[eng].append((tuple(wait), fn, tuple(inc)))
        for sem, amt in inc:
            self.cnt[sem] += amt

    def after(self, *sems):
        return [(s, self.cnt[s]) for s in sems]

    def replay(self, eng_name, eng, sems):
        hw = defaultdict(int)
        for waits, fn, incs in self.ops[eng_name]:
            for sem, v in waits:
                if v > hw[sem]:
                    eng.wait_ge(sems[sem], v)
                    hw[sem] = v
            inst = fn(eng)
            for sem, amt in incs:
                inst.then_inc(sems[sem], amt)


def build_program(dev=0):
    nc = bass.Bass("TRN2", target_bir_lowering=False)

    x0_tm = nc.dram_tensor("x0_tm", [TCORE, D], F32, kind="ExternalInput").ap()
    x0_fm = nc.dram_tensor("x0_fm", [D, TCORE], F32, kind="ExternalInput").ap()
    l0_wt_sl = nc.dram_tensor("l0_wt_sl", [512, D], F32, kind="ExternalInput").ap()
    l0_mask_sl = nc.dram_tensor("l0_mask_sl", [D, 512], I32, kind="ExternalInput").ap()
    l0_bias = nc.dram_tensor("l0_bias", [1, D], F32, kind="ExternalInput").ap()
    wcatT = nc.dram_tensor("wcatT", [D, 24], F32, kind="ExternalInput").ap()
    hbases = nc.dram_tensor("hbases", [1, 24], F32, kind="ExternalInput").ap()
    gate_in = nc.dram_tensor("gate_in", [1, 1], F32, kind="ExternalInput").ap()
    wbr = nc.dram_tensor("wbr", [1024, 1024], F32, kind="ExternalInput").ap()
    wq_slT = nc.dram_tensor("wq_slT", [D, 512], F32, kind="ExternalInput").ap()
    wkv_slT = nc.dram_tensor("wkv_slT", [D, 256], F32, kind="ExternalInput").ap()
    wo_slT = nc.dram_tensor("wo_slT", [512, D], F32, kind="ExternalInput").ap()
    v_slT = nc.dram_tensor("v_slT", [512, RANK], F32, kind="ExternalInput").ap()
    uT_sl = nc.dram_tensor("uT_sl", [128, D], F32, kind="ExternalInput").ap()
    r_base_col = nc.dram_tensor("r_base_col", [P, 1], F32, kind="ExternalInput").ap()
    hi_idx = nc.dram_tensor("hi_idx", [1, RHI], I32, kind="ExternalInput").ap()
    lo_idx = nc.dram_tensor("lo_idx", [1, RLO], I32, kind="ExternalInput").ap()
    l1_bias = nc.dram_tensor("l1_bias", [1, D], F32, kind="ExternalInput").ap()
    tok_base = nc.dram_tensor("tok_base", [1, 1], I32, kind="ExternalInput").ap()

    out = nc.dram_tensor("out", [TCORE, D], F32, kind="ExternalOutput").ap()

    contrib = nc.dram_tensor("contrib", [CONTRIB], BF16)
    agw = nc.dram_tensor("agw", [NCORES * CONTRIB], BF16, addr_space="Shared")
    ar_in = nc.dram_tensor("ar_in", [1, ARN], F32)
    ar_out = nc.dram_tensor("ar_out", [1, ARN], F32)
    x2c = nc.dram_tensor("x2c", [D, TCORE], BF16)
    x2full = nc.dram_tensor("x2full", [NCORES * D, TCORE], BF16, addr_space="Shared")
    q_scr = nc.dram_tensor("q_scr", [512, NTOK], BF16)
    k_scr = nc.dram_tensor("k_scr", [128, NTOK], BF16)
    v_scr = nc.dram_tensor("v_scr", [NTOK, 128], BF16)
    c_o = nc.dram_tensor("c_o", [NTOK, 512], BF16)
    rowstash = nc.dram_tensor("rowstash", [16, 1024], F32)
    ag_o = nc.dram_tensor("ag_o", [NCORES * NTOK, 512], BF16, addr_space="Shared")

    dv = {}
    if dev == 1:
        dv["x1"] = nc.dram_tensor("dv_x1", [TCORE, D], F32, kind="ExternalOutput").ap()
    if dev == 2:
        dv["x2"] = nc.dram_tensor("dv_x2", [TCORE, D], F32, kind="ExternalOutput").ap()
    if dev == 4:
        dv["x3"] = nc.dram_tensor("dv_x3", [TCORE, D], F32, kind="ExternalOutput").ap()

    prog = Prog()
    import contextlib
    st = contextlib.ExitStack()
    sb = lambda name, shape, dt: st.enter_context(nc.sbuf_tensor(name, shape, dt))
    pst = lambda name, shape, dt: st.enter_context(nc.psum_tensor(name, shape, dt))

    xs = sb("xs", [P, NTT * D], F32)
    big = sb("big", [P, 16384], BF16)
    mixbuf = sb("mixbuf", [P, 1024], F32)
    y1g = sb("y1g", [P, 2048], F32)
    x0buf = sb("x0buf", [P, 1024], F32)
    hbuf = sb("hbuf", [24, 1024], F32)
    sk_a = sb("sk_a", [P, 32], F32)
    sk_b = sb("sk_b", [P, 32], F32)
    sk_t = sb("sk_t", [P, P], F32)
    skrow = sb("skrow", [1, 1024], F32)
    Eit = sb("Eit", [16, 512], BF16)
    P32 = sb("P32", [P, 4], BF16)
    identF = sb("identF", [P, P], F32)
    ones_f32 = sb("ones_f32", [P, 1], F32)
    ones_bf = sb("ones_bf", [1, P], BF16)
    wbuf = sb("wbuf", [P, 1024], BF16)
    wbuf2 = sb("wbuf2", [P, 1024], BF16)
    onerow_f = sb("onerow_f", [1, P], F32)
    sums_s = sb("sums_s", [P, 64], F32)
    sqs_s = sb("sqs_s", [P, 64], F32)
    stat = sb("stat", [P, 64], F32)
    mh_col = sb("mh_col", [P, 1], F32)
    ml_col = sb("ml_col", [P, 1], F32)
    r0col = sb("r0col", [P, 1], F32)
    f16row = sb("f16row", [1, 1024], F16)

    ibase = sb("ibase", [1, 1], I32)
    gcol = sb("gcol", [P, 1], F32)
    ipat = sb("ipat", [P, 1], F32)
    ptbuf = sb("ptbuf", [P, 2048], BF16)

    XW = lambda a, b: xs[:, a:b]
    idxbuf = xs[0:1, 31744:32768]
    ROW = lambda a, b: xs[0:1, a:b]

    pkA = pst("pkA", [P, 1024], F32)
    pkB = pst("pkB", [P, 1024], F32)
    pkC = pst("pkC", [P, 1024], F32)
    pkD = pst("pkD", [P, 1024], F32)

    sems = {}
    semnames = ["dW", "dS", "dG", "dO", "mm", "vx", "sx", "gx", "cc"]

    def dma(queue, sem, dst, src, wait=()):
        prog.emit(queue, lambda e, d=dst, s=src: e.dma_start(out=d, in_=s),
                  wait=wait, inc=[(sem, 16)])

    def agw_wt(ic, of0, nof):
        r, row0 = ic // 4, (ic % 4) * P
        base = r * CONTRIB + WT_OFF + row0 * D
        return agw[base: base + P * D].rearrange("(p f) -> p f", f=D)[:, of0: of0 + nof]

    def agw_vq(ic, r0, nr):
        r, row0 = ic // 4, (ic % 4) * P
        base = r * CONTRIB + VQ_OFF + row0 * RANK
        return agw[base: base + P * RANK].rearrange("(p f) -> p f", f=RANK)[:, r0: r0 + nr]

    def agw_bt(rc, of0, nof):
        base = rc * CONTRIB + BT_OFF
        return agw[base: base + P * D].rearrange("(p f) -> p f", f=D)[:, of0: of0 + nof]

    def agw_wo(fc, of0, nof):
        r, row0 = fc // 4, (fc % 4) * P
        base = r * CONTRIB + WO_OFF + row0 * D
        return agw[base: base + P * D].rearrange("(p f) -> p f", f=D)[:, of0: of0 + nof]

    T = lambda fn, wait=(), inc=(): prog.emit("tensor", fn, wait, inc)

    def V(fn, wait=(), inc=(("vx", 1),)):
        prog.emit("vector", fn, tuple(wait) + tuple(prog.after("vx")), inc)

    def Sc(fn, wait=(), inc=(("sx", 1),)):
        prog.emit("scalar", fn, tuple(wait) + tuple(prog.after("sx")), inc)
    def G(fn, wait=(), inc=(("gx", 1),)):
        prog.emit("gpsimd", fn, tuple(wait) + tuple(prog.after("gx")), inc)

    def early_finish():
        dma("sync", "dO", out[0:P, 0:64], stat[:, :],
            wait=prog.after("mm", "vx", "sx", "gx", "dW", "dS", "dG", "cc"))
        prog.emit("sync", lambda e: e.nop(), wait=prog.after("dO"), inc=[])
        finish(nc, st, prog, sems, semnames)
        return nc

    V(lambda e: e.memset(ones_f32[:, :], 1.0))
    V(lambda e: e.memset(onerow_f[:, :], 1.0))
    V(lambda e: e.memset(ones_bf[:, :], 1.0))
    V(lambda e: e.memset(identF[:, :], 1.0))
    G(lambda e: e.affine_select(out=identF[:, :], in_=identF[:, :],
                                compare_op=ALU.is_equal, fill=0.0,
                                pattern=[[-1, P]], base=0, channel_multiplier=1),
      wait=prog.after("vx"))
    for it in range(4):
        G(lambda e, it=it: e.iota(sk_t[0:16, 0:P].rearrange("p (a c) -> p a c", c=32),
                                  pattern=[[-1, 4], [0, 32]], channel_multiplier=1,
                                  base=-4 * it,
                                  allow_small_or_imprecise_dtypes=True),
          wait=prog.after("vx"))
        V(lambda e, it=it: e.tensor_single_scalar(Eit[:, P * it:P * (it + 1)],
                                                  sk_t[0:16, 0:P], 0.0, ALU.is_equal),
          wait=prog.after("gx"))
    G(lambda e: e.iota(sk_t[:, 0:4], pattern=[[32, 4]], channel_multiplier=-1,
                       allow_small_or_imprecise_dtypes=True), wait=prog.after("vx"))
    V(lambda e: e.tensor_single_scalar(sk_t[:, 4:8], sk_t[:, 0:4], 0.0, ALU.is_le),
      wait=prog.after("gx"))
    V(lambda e: e.tensor_single_scalar(sk_t[:, 8:12], sk_t[:, 0:4], -31.5, ALU.is_ge))
    V(lambda e: e.tensor_tensor(P32[:, :], sk_t[:, 4:8], sk_t[:, 8:12], ALU.mult))
    CONSTS = prog.after("gx", "vx")
    if dev == 20:
        return early_finish()

    dma("gpsimd", "dG", idxbuf[:, 0:RHI], hi_idx)
    dma("gpsimd", "dG", idxbuf[:, RHI:RHI + RLO], lo_idx)
    dma("sync", "dW", r0col[:, :], r_base_col)
    dma("sync", "dW", ibase[:, :], tok_base)
    dma("sync", "dW", ROW(0, D), l1_bias)
    for k in range(4):
        V(lambda e, k=k: e.tensor_copy(f16row[:, :], ROW(1024 * k, 1024 * (k + 1))),
          wait=prog.after("dW") if k == 0 else ())
        V(lambda e, k=k: e.tensor_copy(ROW(1024 * k, 1024 * (k + 1)), f16row[:, :]))
    for k in range(4):
        dma("sync", "dO", rowstash.ap()[3 + k:4 + k, :], ROW(1024 * k, 1024 * (k + 1)),
            wait=prog.after("vx"))
    IDX_DONE = prog.after("dG", "dW")

    def member_cols(dst_col, idx_off, nidx, base_col):
        for j in range(0, nidx, 512):
            w = min(512, nidx - j)
            T(lambda e, j=j, w=w: e.matmul(pkA[:, j:j + w], onerow_f[:, :],
                                           idxbuf[:, idx_off + j: idx_off + j + w],
                                           start=True, stop=True),
              wait=IDX_DONE + prog.after("vx"),
              inc=[("mm", 1)] if j + 512 >= nidx else [])
        mmv = prog.val("mm")
        V(lambda e, n=nidx, bc=base_col: e.tensor_scalar(
            XW(0, nidx), pkA[:, 0:nidx], bc, None, ALU.is_equal), wait=[("mm", mmv)])
        V(lambda e, n=nidx, dc=dst_col: e.reduce_max(dc, XW(0, n), axis=AX.X))

    member_cols(mh_col[:, :], 0, RHI, r0col[:, 0:1])
    member_cols(ml_col[:, :], RHI, RLO, r0col[:, 0:1])
    G(lambda e: e.iota(stat[:, 63:64], pattern=[[0, 1]], base=0, channel_multiplier=1,
                       allow_small_or_imprecise_dtypes=True), wait=CONSTS)
    for k in range(8):
        V(lambda e, k=k: e.tensor_scalar(stat[:, 62:63], stat[:, 63:64],
                                         float(128 * k), None, ALU.add),
          wait=prog.after("gx"))
        member_cols(stat[:, 40 + k:41 + k], 0, RHI, stat[:, 62:63])
        member_cols(stat[:, 48 + k:49 + k], RHI, RLO, stat[:, 62:63])
    T(lambda e: e.transpose(pkC[0:8, 0:P], stat[:, 40:48], identF[:, :]),
      wait=prog.after("vx"), inc=[])
    T(lambda e: e.transpose(pkC[0:8, 128:256], stat[:, 48:56], identF[:, :]),
      inc=[("mm", 1)])
    V(lambda e: e.tensor_copy(sk_t[0:8, 0:P], pkC[0:8, 0:P]), wait=prog.after("mm"))
    V(lambda e: e.tensor_copy(mixbuf[0:8, 0:P], pkC[0:8, 128:256]))
    dma("sync", "dW", ROW(13312, 14336), sk_t[0:8, 0:P], wait=prog.after("vx"))
    dma("sync", "dW", ROW(31744, 32768), mixbuf[0:8, 0:P])
    dma("sync", "dO", rowstash.ap()[0:1, :], sk_t[0:8, 0:P])
    dma("sync", "dO", rowstash.ap()[1:2, :], mixbuf[0:8, 0:P])
    prog.emit("vector", lambda e: e.engine_nop(), wait=prog.after("dW"), inc=[("vx", 1)])

    if dev == 21:
        return early_finish()
    dma("sync", "dW", XW(0, D), uT_sl, wait=prog.after("vx"))
    Sc(lambda e: e.activation(XW(D, 2 * D), XW(0, D), AF.Abs), wait=prog.after("dW"))
    V(lambda e: e.tensor_scalar(XW(2 * D, 3 * D), XW(D, 2 * D), mh_col[:, 0:1], None, ALU.mult),
      wait=prog.after("sx"))
    for j in range(32):
        slot = pkA[:, 0:128] if j % 2 == 0 else pkA[:, 512:640]
        T(lambda e, j=j, slot=slot: e.transpose(slot, XW(2 * D, 3 * D)[:, 128 * j:128 * (j + 1)],
                                                identF[:, :]),
          wait=prog.after("vx") , inc=[("mm", 1)])
        mv = prog.val("mm")
        V(lambda e, j=j, slot=slot: e.reduce_max(mixbuf[:, 0 + j:0 + j + 1],
                                                 slot, axis=AX.X), wait=[("mm", mv)])
    T(lambda e: e.transpose(pkB[0:32, 0:128], mixbuf[:, 0:0 + 32],
                            identF[:, :]), wait=prog.after("vx"), inc=[("mm", 1)])
    mv = prog.val("mm")
    V(lambda e: e.tensor_copy(mixbuf[0:32, 512:640], pkB[0:32, 0:128]), wait=[("mm", mv)])
    dma("sync", "dW", ar_in[0:1, 0:D].rearrange("one (j c) -> (one j) c", c=P),
        mixbuf[0:32, 512:640], wait=prog.after("vx"))
    V(lambda e: e.tensor_scalar(XW(2 * D, 3 * D), XW(D, 2 * D), ml_col[:, 0:1], None, ALU.mult),
      wait=prog.after("dW"))
    for j in range(32):
        slot = pkA[:, 0:128] if j % 2 == 0 else pkA[:, 512:640]
        T(lambda e, j=j, slot=slot: e.transpose(slot, XW(2 * D, 3 * D)[:, 128 * j:128 * (j + 1)],
                                                identF[:, :]),
          wait=prog.after("vx") , inc=[("mm", 1)])
        mv = prog.val("mm")
        V(lambda e, j=j, slot=slot: e.reduce_max(mixbuf[:, 32 + j:32 + j + 1],
                                                 slot, axis=AX.X), wait=[("mm", mv)])
    T(lambda e: e.transpose(pkB[0:32, 0:128], mixbuf[:, 32:32 + 32],
                            identF[:, :]), wait=prog.after("vx"), inc=[("mm", 1)])
    mv = prog.val("mm")
    V(lambda e: e.tensor_copy(mixbuf[0:32, 512:640], pkB[0:32, 0:128]), wait=[("mm", mv)])
    dma("sync", "dW", ar_in[0:1, D:2 * D].rearrange("one (j c) -> (one j) c", c=P),
        mixbuf[0:32, 512:640], wait=prog.after("vx"))

    for vt in range(4):
        dma("scalar", "dS", XW(16384, 17408), v_slT[P * vt:P * (vt + 1), :],
            wait=prog.after("sx", "vx"))
        Sc(lambda e: e.activation(XW(17408, 18432),
                                  XW(16384, 17408), AF.Abs),
           wait=prog.after("dS"))
        if vt == 0:
            V(lambda e: e.tensor_copy(XW(18432, 19456),
                                      XW(17408, 18432)),
              wait=prog.after("sx"))
        else:
            V(lambda e: e.tensor_tensor(XW(18432, 19456),
                                        XW(18432, 19456),
                                        XW(17408, 18432), ALU.max),
              wait=prog.after("sx"))
    for j in range(8):
        slot = pkA[:, 0:128] if j % 2 == 0 else pkA[:, 512:640]
        T(lambda e, j=j, slot=slot: e.transpose(slot, XW(18432, 19456)[:, 128 * j:128 * (j + 1)],
                                                identF[:, :]),
          wait=prog.after("vx") , inc=[("mm", 1)])
        mv = prog.val("mm")
        V(lambda e, j=j, slot=slot: e.reduce_max(mixbuf[:, 64 + j:64 + j + 1],
                                                 slot, axis=AX.X), wait=[("mm", mv)])
    T(lambda e: e.transpose(pkB[0:8, 0:128], mixbuf[:, 64:64 + 8],
                            identF[:, :]), wait=prog.after("vx"), inc=[("mm", 1)])
    mv = prog.val("mm")
    V(lambda e: e.tensor_copy(mixbuf[0:8, 512:640], pkB[0:8, 0:128]), wait=[("mm", mv)])
    dma("sync", "dW", ar_in[0:1, 2 * D:2 * D + RANK].rearrange("one (j c) -> (one j) c", c=P),
        mixbuf[0:8, 512:640], wait=prog.after("vx"))

    prog.emit("gpsimd",
              lambda e: e.collective_compute("AllReduce", ALU.max,
                                             replica_groups=[list(range(NCORES))],
                                             ins=[ar_in.ap()], outs=[ar_out.ap()]),
              wait=prog.after("dW"), inc=[("cc", 1)])
    dma("sync", "dW", ROW(20480, 20480 + ARN), ar_out.ap(), wait=prog.after("cc"))
    AR_DONE = prog.after("dW")
    if dev == 22:
        return early_finish()

    US8, US4, I8, I4 = 16384, 4096, 20480, 24576
    U16A, U16B = 16384, 4096
    SC, ISC = 29696, 30720
    V(lambda e: e.tensor_scalar(ROW(US8, US8 + D), ROW(20480, 24576), 1e-8, 1.0 / 127.0,
                                ALU.max, ALU.mult), wait=AR_DONE)
    V(lambda e: e.tensor_scalar(ROW(US4, US4 + D), ROW(24576, 28672), 1e-8, 1.0 / 7.0,
                                ALU.max, ALU.mult))
    V(lambda e: e.reciprocal(ROW(I8, I8 + D), ROW(US8, US8 + D)))
    V(lambda e: e.reciprocal(ROW(I4, I4 + D), ROW(US4, US4 + D)))
    for k in range(4):
        V(lambda e, k=k: e.tensor_copy(f16row[:, :], ROW(US8 + 1024 * k, US8 + 1024 * (k + 1))))
        V(lambda e, k=k: e.tensor_copy(ROW(U16A + 1024 * k, U16A + 1024 * (k + 1)), f16row[:, :]))
    for k in range(4):
        V(lambda e, k=k: e.tensor_copy(f16row[:, :], ROW(US4 + 1024 * k, US4 + 1024 * (k + 1))))
        V(lambda e, k=k: e.tensor_copy(ROW(U16B + 1024 * k, U16B + 1024 * (k + 1)), f16row[:, :]))
    V(lambda e: e.tensor_scalar(ROW(SC + RANK, SC + 2 * RANK), ROW(13312, 14336),
                                1.0 / 127.0, None, ALU.mult))
    V(lambda e: e.scalar_tensor_tensor(ROW(SC + RANK, SC + 2 * RANK), ROW(31744, 32768),
                                       1.0 / 7.0, ROW(SC + RANK, SC + 2 * RANK),
                                       ALU.mult, ALU.add))
    V(lambda e: e.tensor_scalar(ROW(SC, SC + RANK), ROW(28672, 29696), 1e-8, None,
                                ALU.max))
    V(lambda e: e.tensor_tensor(ROW(SC, SC + RANK), ROW(SC, SC + RANK),
                                ROW(SC + RANK, SC + 2 * RANK), ALU.mult))
    V(lambda e: e.reciprocal(ROW(ISC, ISC + RANK), ROW(SC, SC + RANK)))
    V(lambda e: e.tensor_copy(f16row[:, 0:RANK], ROW(SC, SC + RANK)))
    V(lambda e: e.tensor_copy(ROW(28672, 29696), f16row[:, 0:RANK]))
    dma("sync", "dO", rowstash.ap()[2:3, :], ROW(28672, 29696), wait=prog.after("vx"))

    if dev == 23:
        return early_finish()
    for oc in range(8):
        sl = slice(512 * oc, 512 * (oc + 1))
        T(lambda e, oc=oc: e.matmul(pkA[:, 0:512], onerow_f[:, :],
                                    ROW(I8 + 512 * oc, I8 + 512 * (oc + 1)),
                                    start=True, stop=True),
          wait=prog.after("vx"), inc=[])
        T(lambda e, oc=oc: e.matmul(pkB[:, 0:512], onerow_f[:, :],
                                    ROW(I4 + 512 * oc, I4 + 512 * (oc + 1)),
                                    start=True, stop=True), inc=[])
        T(lambda e, oc=oc: e.matmul(pkC[:, 0:512], onerow_f[:, :],
                                    ROW(U16A + 512 * oc, U16A + 512 * (oc + 1)),
                                    start=True, stop=True), inc=[])
        T(lambda e, oc=oc: e.matmul(pkD[:, 0:512], onerow_f[:, :],
                                    ROW(U16B + 512 * oc, U16B + 512 * (oc + 1)),
                                    start=True, stop=True), inc=[("mm", 1)])
        mmv = prog.val("mm")
        V(lambda e: e.tensor_scalar(XW(2 * D, 2 * D + 512), pkA[:, 0:512],
                                    mh_col[:, 0:1], None, ALU.mult), wait=[("mm", mmv)])
        V(lambda e: e.scalar_tensor_tensor(XW(2 * D, 2 * D + 512), pkB[:, 0:512],
                                           ml_col[:, 0:1], XW(2 * D, 2 * D + 512),
                                           ALU.mult, ALU.add))
        V(lambda e: e.tensor_scalar(XW(3 * D, 3 * D + 512), pkC[:, 0:512],
                                    mh_col[:, 0:1], None, ALU.mult))
        V(lambda e: e.scalar_tensor_tensor(XW(3 * D, 3 * D + 512), pkD[:, 0:512],
                                           ml_col[:, 0:1], XW(3 * D, 3 * D + 512),
                                           ALU.mult, ALU.add))
        V(lambda e, sl=sl: e.tensor_tensor(XW(2 * D, 2 * D + 512), XW(0, D)[:, sl],
                                           XW(2 * D, 2 * D + 512), ALU.mult))
        V(lambda e: e.tensor_scalar(XW(2 * D, 2 * D + 512), XW(2 * D, 2 * D + 512),
                                    MAGIC, MAGIC, ALU.add, ALU.subtract))
        V(lambda e, sl=sl: e.tensor_tensor(big[:, sl], XW(2 * D, 2 * D + 512),
                                           XW(3 * D, 3 * D + 512), ALU.mult))
    dma("sync", "dO", contrib[BT_OFF: BT_OFF + P * D].rearrange("(p f) -> p f", f=D),
        big[:, 0:D], wait=prog.after("vx"))

    if dev == 24:
        return early_finish()
    T(lambda e: e.matmul(pkA[:, 0:512], onerow_f[:, :], ROW(ISC, ISC + 512),
                         start=True, stop=True), wait=prog.after("vx"), inc=[])
    T(lambda e: e.matmul(pkA[:, 512:1024], onerow_f[:, :], ROW(ISC + 512, ISC + 1024),
                         start=True, stop=True), inc=[("mm", 1)])
    mmv = prog.val("mm")
    for vt in range(4):
        dma("scalar", "dS", XW(0, RANK), v_slT[P * vt:P * (vt + 1), :],
            wait=prog.after("vx") if vt else prog.after("dO", "vx"))
        V(lambda e: e.tensor_tensor(XW(RANK, 2 * RANK), XW(0, RANK), pkA[:, 0:RANK], ALU.mult),
          wait=[("mm", mmv)] + prog.after("dS"))
        V(lambda e, vt=vt: e.tensor_scalar(big[:, D + RANK * vt: D + RANK * (vt + 1)],
                                           XW(RANK, 2 * RANK), MAGIC, MAGIC,
                                           ALU.add, ALU.subtract))
    for vt in range(4):
        dma("sync", "dO", contrib[VQ_OFF + vt * P * RANK: VQ_OFF + (vt + 1) * P * RANK]
            .rearrange("(p f) -> p f", f=RANK), big[:, D + RANK * vt:D + RANK * (vt + 1)],
            wait=prog.after("vx"))

    for wt in range(4):
        dma("gpsimd", "dG", big[:, 8192:12288], wo_slT[P * wt:P * (wt + 1), :],
            wait=prog.after("dO"))
        dma("sync", "dO", contrib[WO_OFF + wt * P * D: WO_OFF + (wt + 1) * P * D]
            .rearrange("(p f) -> p f", f=D), big[:, 8192:12288], wait=prog.after("dG"))

    if dev == 25:
        return early_finish()
    for mt in range(32):
        dma("gpsimd", "dG", XW(0, 512), l0_mask_sl[P * mt:P * (mt + 1), :],
            wait=prog.after("vx", "mm"))
        V(lambda e: e.reduce_sum(XW(512, 528), XW(0, 512).rearrange("p (g k) -> p g k", k=32),
                                 axis=AX.X), wait=prog.after("dG"))
        V(lambda e: e.tensor_scalar(wbuf[:, 0:16], XW(512, 528), 0.5, None, ALU.is_gt))
        T(lambda e: e.matmul(pkA[0:4, 0:16], P32[:, :], wbuf[:, 0:16], start=True, stop=True),
          wait=prog.after("vx"), inc=[("mm", 1)])
        V(lambda e, mt=mt: e.tensor_scalar(hbuf[0:4, 16 * mt:16 * (mt + 1)], pkA[0:4, 0:16],
                                           0.5, None, ALU.is_gt), wait=prog.after("mm"))
    for mt in range(32):
        T(lambda e, mt=mt: e.transpose(pkC[0:16, 4 * mt:4 * mt + 4],
                                       hbuf[0:4, 16 * mt:16 * (mt + 1)], identF[0:4, 0:4]),
          wait=prog.after("vx"), inc=[("mm", 1)] if mt == 31 else [])
    mmv = prog.val("mm")
    V(lambda e: e.tensor_copy(sk_t[0:16, 0:P], pkC[0:16, 0:P]), wait=[("mm", mmv)])
    for it in range(4):
        dma("scalar", "dS", XW(D, 2 * D), l0_wt_sl[P * it:P * (it + 1), :],
            wait=prog.after("vx"))
        if it == 0:
            for s32 in range(32):
                V(lambda e, s32=s32: e.tensor_copy(
                    big[0:16, 2 * D:3 * D].rearrange("p (n s) -> p n s", s=32)
                    [:, :, s32:s32 + 1],
                    sk_t[0:16, 0:P].rearrange("p (n o) -> p n o", o=1)),
                  wait=prog.after("mm") if s32 == 0 else ())
        BMT_V = prog.val("vx")
        for oc in range(8):
            sl = slice(512 * oc, 512 * (oc + 1))
            T(lambda e, it=it, sl=sl: e.matmul(
                pkA[:, 0:512], Eit[:, P * it:P * (it + 1)],
                big[0:16, 2 * D + sl.start:2 * D + sl.stop],
                start=True, stop=True),
              wait=[("vx", BMT_V)] + prog.after("dS", "vx"), inc=[("mm", 1)])
            V(lambda e, sl=sl: e.tensor_tensor(big[:, D + sl.start:D + sl.stop],
                                               XW(D, 2 * D)[:, sl], pkA[:, 0:512], ALU.mult),
              wait=prog.after("mm"))
        dma("sync", "dO", contrib[WT_OFF + it * P * D: WT_OFF + (it + 1) * P * D]
            .rearrange("(p f) -> p f", f=D), big[:, D:2 * D], wait=prog.after("vx"))

    if dev == 26:
        return early_finish()
    prog.emit("gpsimd",
              lambda e: e.collective_compute("AllGather", ALU.bypass,
                                             replica_groups=[list(range(NCORES))],
                                             ins=[contrib.ap()], outs=[agw.ap()]),
              wait=prog.after("dO"), inc=[("cc", 1)])
    AGW_DONE = prog.after("cc")
    if dev == 5:
        for ic in range(4):
            dma("gpsimd", "dG", big[:, 0:1024], agw_vq(ic, 0, 1024),
                wait=list(AGW_DONE) + prog.after("vx", "dO"))
            V(lambda e: e.tensor_copy(mixbuf[:, 0:1024], big[:, 0:1024]),
              wait=prog.after("dG"))
            dma("sync", "dO", out[P * ic:P * (ic + 1), 1024:2048], mixbuf[:, 0:1024],
                wait=prog.after("vx"))
        for q in range(4):
            dma("gpsimd", "dG", big[:, 0:1024], agw_bt(0, 1024 * q, 1024),
                wait=prog.after("vx", "dO"))
            V(lambda e: e.tensor_copy(mixbuf[:, 0:1024], big[:, 0:1024]),
              wait=prog.after("dG"))
            dma("sync", "dO", out[512:640, 1024 * q:1024 * (q + 1)], mixbuf[:, 0:1024],
                wait=prog.after("vx"))
        for rw in range(7):
            dma("sync", "dW", skrow[0:1, :], rowstash.ap()[rw:rw + 1, :],
                wait=prog.after("dO"))
            dma("sync", "dO", out[640 + rw:641 + rw, 0:1024], skrow[0:1, :],
                wait=prog.after("dW"))
        return early_finish()
    if dev == 27:
        return early_finish()

    def big_proj(pre_grp, rhs_fn, lhs_fn, n_groups, nob, nk, evac_fn, bias_row_fn=None):
        kguards = []
        for grp in range(n_groups):
            extra = pre_grp(grp) if pre_grp else []
            for ob in range(nob):
                pk0, pk1 = (pkA, pkB) if ob % 2 == 0 else (pkC, pkD)
                banks = [(pk0, 0), (pk0, 512), (pk1, 0), (pk1, 512)]
                br = bias_row_fn(grp, ob) if bias_row_fn is not None else None
                brw = []
                if isinstance(br, tuple):
                    brw = prog.after("dG")
                    br = br[1]
                for k in range(nk):
                    gw = [("mm", kguards[-2])] if len(kguards) >= 2 else None
                    wsl = rhs_fn(grp, ob, k, gw)
                    wv = prog.after("dW")
                    for ti in range(4):
                        pk, off = banks[ti]
                        w = list(wv) if ti == 0 else []
                        if k == 0 and ti == 0:
                            w += prog.after("vx") + list(extra)
                        if k == 0 and br is not None:
                            T(lambda e, pk=pk, off=off, br=br: e.matmul(
                                pk[:, off:off + 512], ones_bf[:, 0:P], br,
                                start=True, stop=False), wait=list(w) + list(brw), inc=[])
                            w = []
                        lhs = lhs_fn(grp, ti, k)
                        T(lambda e, pk=pk, off=off, lhs=lhs, wsl=wsl, k=k, nk=nk,
                          s=(k == 0 and bias_row_fn is None): e.matmul(
                              pk[:, off:off + 512], lhs, wsl, start=s, stop=(k == nk - 1)),
                          wait=w, inc=[("mm", 1)] if ti == 3 else [])
                    kguards.append(prog.val("mm"))
                mmv = prog.val("mm")
                for ti in range(4):
                    pk, off = banks[ti]
                    evac_fn(grp, ob, ti, pk, off, mmv)

    def p1_pre(grp):
        for ic in range(NIF):
            dma("gpsimd", "dG", big[:, 512 * ic:512 * (ic + 1)],
                x0_fm[P * ic:P * (ic + 1), 512 * grp:512 * (grp + 1)],
                wait=(AGW_DONE + prog.after("mm")) if ic == 0 else ())
        return prog.after("dG")

    def p1_rhs(grp, ob, ic, gw):
        wsl = wbuf[:, 512 * (ic % 2):512 * (ic % 2) + 512]
        dma("sync", "dW", wsl, agw_wt(ic, 512 * ob, 512),
            wait=(gw if gw is not None else list(AGW_DONE)))
        return wsl

    def p1_lhs(grp, ti, ic):
        return big[:, 512 * ic + P * ti:512 * ic + P * (ti + 1)]

    def p1_bias(grp, ob):
        dma("gpsimd", "dG", wbuf2[0:1, 0:512], l0_bias[0:1, 512 * ob:512 * (ob + 1)],
            wait=prog.after("mm"))
        return ("DG", wbuf2[0:1, 0:512])

    def p1_evac(grp, ob, ti, pk, off, mmv):
        tt = 4 * grp + ti
        xb = x0buf[:, 512 * (ti % 2):512 * (ti % 2) + 512]
        dma("scalar", "dS", xb, x0_tm[P * tt:P * (tt + 1), 512 * ob:512 * (ob + 1)],
            wait=prog.after("vx"))
        dst = xs[:, D * tt + 512 * ob: D * tt + 512 * (ob + 1)]
        V(lambda e, dst=dst, pk=pk, off=off, xb=xb, tt=tt, ob=ob: e.scalar_tensor_tensor(
            dst, pk[:, off:off + 512], 1.0, xb, ALU.mult, ALU.add,
            accum_out=sums_s[:, 8 * tt + ob: 8 * tt + ob + 1]),
          wait=[("mm", mmv)] + prog.after("dS"))
        Sc(lambda e, dst=dst, tt=tt, ob=ob: e.activation(
            mixbuf[:, 0:512], dst, AF.Square,
            accum_out=sqs_s[:, 8 * tt + ob: 8 * tt + ob + 1]), wait=prog.after("vx"))

    big_proj(p1_pre, p1_rhs, p1_lhs, 2, 8, NIF, p1_evac, bias_row_fn=p1_bias)
    if dev == 28:
        return early_finish()

    def layernorm(tt, rinv_col, nmr_col, rms_col=None):
        r = stat
        V(lambda e, tt=tt: e.reduce_sum(r[:, 48:49], sums_s[:, 8 * tt:8 * tt + 8], axis=AX.X),
          wait=prog.after("sx"))
        V(lambda e, tt=tt: e.reduce_sum(r[:, 49:50], sqs_s[:, 8 * tt:8 * tt + 8], axis=AX.X))
        V(lambda e: e.tensor_scalar(r[:, 50:51], r[:, 48:49], 1.0 / D, None, ALU.mult))
        V(lambda e: e.tensor_tensor(r[:, 51:52], r[:, 50:51], r[:, 50:51], ALU.mult))
        V(lambda e: e.scalar_tensor_tensor(r[:, 52:53], r[:, 49:50], 1.0 / D, r[:, 51:52],
                                           ALU.mult, ALU.subtract))
        V(lambda e: e.tensor_scalar(r[:, 53:54], r[:, 52:53], LN_EPS, None, ALU.add))
        Sc(lambda e: e.activation(r[:, 54:55], r[:, 53:54], AF.Sqrt), wait=prog.after("vx"))
        V(lambda e: e.reciprocal(r[:, 55:56], r[:, 54:55]), wait=prog.after("sx"))
        V(lambda e: e.tensor_tensor(r[:, 56:57], r[:, 55:56], r[:, 55:56], ALU.mult))
        V(lambda e: e.tensor_tensor(r[:, 57:58], r[:, 56:57], r[:, 53:54], ALU.mult))
        V(lambda e: e.tensor_scalar(r[:, 58:59], r[:, 57:58], -0.5, 1.5, ALU.mult, ALU.add))
        V(lambda e, rc=rinv_col: e.tensor_tensor(rc, r[:, 55:56], r[:, 58:59], ALU.mult))
        V(lambda e, rc=rinv_col, nm=nmr_col: e.scalar_tensor_tensor(
            nm, r[:, 50:51], -1.0, rc, ALU.mult, ALU.mult))
        if rms_col is not None:
            V(lambda e, rc=rinv_col: e.tensor_tensor(r[:, 56:57], rc, rc, ALU.mult))
            V(lambda e: e.tensor_tensor(r[:, 57:58], r[:, 56:57], r[:, 52:53], ALU.mult))
            V(lambda e: e.tensor_scalar(r[:, 57:58], r[:, 57:58], RMS_EPS, None, ALU.add))
            Sc(lambda e: e.activation(r[:, 59:60], r[:, 57:58], AF.Sqrt), wait=prog.after("vx"))
            V(lambda e: e.reciprocal(r[:, 60:61], r[:, 59:60]), wait=prog.after("sx"))
            V(lambda e: e.tensor_tensor(r[:, 56:57], r[:, 60:61], r[:, 60:61], ALU.mult))
            V(lambda e: e.tensor_tensor(r[:, 58:59], r[:, 56:57], r[:, 57:58], ALU.mult))
            V(lambda e: e.tensor_scalar(r[:, 58:59], r[:, 58:59], -0.5, 1.5, ALU.mult, ALU.add))
            V(lambda e, xc=rms_col: e.tensor_tensor(xc, r[:, 60:61], r[:, 58:59], ALU.mult))
        V(lambda e, tt=tt, rc=rinv_col, nm=nmr_col: e.tensor_scalar(
            xs[:, D * tt:D * (tt + 1)], xs[:, D * tt:D * (tt + 1)], rc, nm,
            ALU.mult, ALU.add))

    for tt in range(NTT):
        layernorm(tt, stat[:, tt:tt + 1], stat[:, 8 + tt:9 + tt],
                  rms_col=stat[:, 16 + tt:17 + tt])

    def dump_and_finish(key):
        for tt in range(NTT):
            dma("sync", "dO", dv[key][P * tt:P * (tt + 1), :], xs[:, D * tt:D * (tt + 1)],
                wait=prog.after("sx", "vx"))
        for tt in range(NTT):
            dma("sync", "dO", out[P * tt:P * (tt + 1), :], xs[:, D * tt:D * (tt + 1)])
        prog.emit("sync", lambda e: e.nop(), wait=prog.after("dO"), inc=[])
        finish(nc, st, prog, sems, semnames)
        return nc

    if dev == 1:
        return dump_and_finish("x1")

    WCAT0, X2ST, XPF, BR0 = 4096, 2048, 1024, 4864
    dma("gpsimd", "dG", big[:, WCAT0:WCAT0 + 32 * 24].rearrange("p (c k) -> p c k", k=24),
        wcatT.rearrange("(c p) k -> p c k", p=P), wait=prog.after("mm", "sx"))
    dma("sync", "dW", skrow[0:1, 0:24], hbases)
    dma("sync", "dW", skrow[0:1, 24:25], gate_in)
    WCAT_L = prog.after("dG", "dW")

    for ic in range(NIF):
        for tt in range(NTT):
            T(lambda e, ic=ic, tt=tt: e.transpose(
                pkC[:, P * tt:P * (tt + 1)],
                xs[:, D * tt + P * ic: D * tt + P * (ic + 1)], identF[:, :]),
              wait=prog.after("sx", "vx") if tt == 0 else (),
              inc=[("mm", 1)] if tt == NTT - 1 else [])
        mmv = prog.val("mm")
        V(lambda e: e.tensor_copy(big[:, 0:1024], pkC[:, 0:1024]), wait=[("mm", mmv)])
        vxv = prog.val("vx")
        for half in range(2):
            T(lambda e, ic=ic, half=half: e.matmul(
                pkD[0:24, 512 * half:512 * (half + 1)],
                big[:, WCAT0 + 24 * ic:WCAT0 + 24 * (ic + 1)],
                big[:, 512 * half:512 * (half + 1)],
                start=(ic == 0), stop=(ic == NIF - 1)),
              wait=([("vx", vxv)] + (list(WCAT_L) if ic == 0 else [])),
              inc=[("mm", 1)] if (ic == NIF - 1 and half == 1) else [])
    LOGITS_MM = prog.val("mm")
    V(lambda e: e.tensor_copy(hbuf[:, :], pkD[0:24, :]), wait=[("mm", LOGITS_MM)])
    T(lambda e: e.matmul(pkC[:, 0:24], onerow_f[:, :], skrow[0:1, 0:24],
                         start=True, stop=True),
      wait=prog.after("vx") + list(WCAT_L), inc=[("mm", 1)])
    BASES_MM = prog.val("mm")
    V(lambda e: e.tensor_copy(y1g[:, 0:24], pkC[:, 0:24]), wait=[("mm", BASES_MM)])
    for tt in range(NTT):
        T(lambda e, tt=tt: e.transpose(pkD[:, 0:24], hbuf[:, P * tt:P * (tt + 1)],
                                       identF[0:24, 0:24]),
          wait=prog.after("vx"), inc=[("mm", 1)])
        mv = prog.val("mm")
        V(lambda e, tt=tt: e.scalar_tensor_tensor(
            x0buf[:, 24 * tt:24 * tt + 24], pkD[:, 0:24], stat[:, 16 + tt:17 + tt],
            y1g[:, 0:24], ALU.mult, ALU.add), wait=[("mm", mv)])
    hview = x0buf[:, 0:192].rearrange("p (t j) -> p t j", j=24)
    for r in range(4):
        V(lambda e, r=r: e.reduce_max(bass.AP(sk_t, r, [[P, P], [4, 8]]),
                                      bass.AP(x0buf, 4 * r, [[1024, P], [24, 8], [1, 4]]),
                                      axis=AX.X))
    for c in range(4):
        V(lambda e, c=c: e.tensor_tensor(bass.AP(x0buf, c, [[1024, P], [24, 8], [4, 4]]),
                                         bass.AP(x0buf, c, [[1024, P], [24, 8], [4, 4]]),
                                         bass.AP(sk_t, 0, [[P, P], [4, 8], [1, 4]]),
                                         ALU.subtract))
    Sc(lambda e: e.activation(hview[:, :, 0:16], hview[:, :, 0:16], AF.Exp,
                              scale=INV_TAU), wait=prog.after("vx"))
    Sc(lambda e: e.activation(hview[:, :, 16:24], hview[:, :, 16:24], AF.Exp))
    V(lambda e: e.reduce_sum(sk_t[:, 0:8], hview[:, :, 16:20], axis=AX.X),
      wait=prog.after("sx"))
    V(lambda e: e.reduce_sum(sk_t[:, 8:16], hview[:, :, 20:24], axis=AX.X))
    V(lambda e: e.reciprocal(sk_t[:, 16:24], sk_t[:, 0:8]))
    V(lambda e: e.reciprocal(sk_t[:, 24:32], sk_t[:, 8:16]))
    for j in range(4):
        V(lambda e, j=j: e.tensor_tensor(hview[:, :, 16 + j:17 + j], hview[:, :, 16 + j:17 + j],
                                         sk_t[:, 16:24].rearrange("p (t o) -> p t o", o=1),
                                         ALU.mult))
        V(lambda e, j=j: e.tensor_tensor(hview[:, :, 20 + j:21 + j], hview[:, :, 20 + j:21 + j],
                                         sk_t[:, 24:32].rearrange("p (t o) -> p t o", o=1),
                                         ALU.mult))
    def pres_r(r):
        return bass.AP(x0buf, 4 * r, [[1024, P], [24, 8], [1, 4]])

    def pres_c(c):
        return bass.AP(x0buf, c, [[1024, P], [24, 8], [4, 4]])
    aview = sk_a[:, :].rearrange("p (t r) -> p t r", r=4)
    bview = sk_b[:, :].rearrange("p (t c) -> p t c", c=4)
    V(lambda e: e.memset(sk_b[:, :], 1.0))
    for it in range(SINK_ITERS):
        for r in range(4):
            V(lambda e, r=r: e.tensor_tensor(
                sk_t[:, 32 * r:32 * r + 32].rearrange("p (t c) -> p t c", c=4),
                pres_r(r), bview, ALU.mult))
        for r in range(4):
            V(lambda e, r=r: e.reduce_sum(
                aview[:, :, r:r + 1],
                sk_t[:, 32 * r:32 * r + 32].rearrange("p (t c) -> p t c", c=4), axis=AX.X))
        V(lambda e: e.reciprocal(sk_a[:, :], sk_a[:, :]))
        for c in range(4):
            V(lambda e, c=c: e.tensor_tensor(
                sk_t[:, 32 * c:32 * c + 32].rearrange("p (t r) -> p t r", r=4),
                pres_c(c), aview, ALU.mult))
        for c in range(4):
            V(lambda e, c=c: e.reduce_sum(
                bview[:, :, c:c + 1],
                sk_t[:, 32 * c:32 * c + 32].rearrange("p (t r) -> p t r", r=4), axis=AX.X))
        V(lambda e: e.reciprocal(sk_b[:, :], sk_b[:, :]))
    for r in range(4):
        V(lambda e, r=r: e.tensor_tensor(pres_r(r), pres_r(r), bview, ALU.mult))
    for c in range(4):
        V(lambda e, c=c: e.tensor_tensor(pres_c(c), pres_c(c), aview, ALU.mult))
    Sc(lambda e: e.activation(skrow[0:1, 25:26], skrow[0:1, 24:25], AF.Sigmoid))
    sxv = prog.val("sx")
    T(lambda e: e.matmul(pkD[:, 32:33], onerow_f[:, :], skrow[0:1, 25:26],
                         start=True, stop=True), wait=[("sx", sxv)] + prog.after("vx"),
      inc=[("mm", 1)])
    mmv = prog.val("mm")
    V(lambda e: e.tensor_copy(gcol[:, :], pkD[:, 32:33]), wait=[("mm", mmv)])
    V(lambda e: e.tensor_scalar(stat[:, 60:61], gcol[:, 0:1], -1.0, 1.0, ALU.mult, ALU.add))
    V(lambda e: e.tensor_scalar(hview[:, :, 0:16], hview[:, :, 0:16], gcol[:, 0:1],
                                None, ALU.mult))
    V(lambda e: e.tensor_scalar(bass.AP(x0buf, 0, [[1024, P], [24, 8], [5, 4]]),
                                bass.AP(x0buf, 0, [[1024, P], [24, 8], [5, 4]]),
                                stat[:, 60:61], None, ALU.add))
    if dev == 3:
        dma("sync", "dO", out[0:P, 256:448], x0buf[:, 0:192], wait=prog.after("vx", "sx"))
        dma("sync", "dO", out[0:P, 1024:1056], sk_a[:, :])
        dma("sync", "dO", out[0:P, 2048:2080], sk_b[:, :])
        dma("sync", "dO", out[0:P, 3072:3136], stat[:, 0:64])
        return early_finish()
    for ci in range(8):
        dma("gpsimd", "dG", big[:, BR0 + 1024 * ci:BR0 + 1024 * (ci + 1)],
            wbr[P * ci:P * (ci + 1), :], wait=prog.after("mm") if ci == 0 else ())
    WBR_L = prog.after("dG")

    for tt in range(NTT):
        hcol = lambda j, tt=tt: x0buf[:, 24 * tt + j:24 * tt + j + 1]
        V(lambda e, tt=tt, hcol=hcol: e.tensor_scalar(
            mixbuf[:, :], xs[:, D * tt:D * tt + 1024], hcol(16), None, ALU.mult),
          wait=prog.after("mm"))
        for j in range(1, 4):
            V(lambda e, tt=tt, j=j, hcol=hcol: e.scalar_tensor_tensor(
                mixbuf[:, :], xs[:, D * tt + 1024 * j:D * tt + 1024 * (j + 1)],
                hcol(16 + j), mixbuf[:, :], ALU.mult, ALU.add))
        for ci in range(8):
            T(lambda e, ci=ci: e.transpose(pkC[:, P * ci:P * (ci + 1)],
                                           mixbuf[:, P * ci:P * (ci + 1)], identF[:, :]),
              wait=prog.after("vx") if ci == 0 else (), inc=[("mm", 1)] if ci == 7 else [])
        mv = prog.val("mm")
        V(lambda e: e.tensor_copy(big[:, XPF:XPF + 1024], pkC[:, 0:1024]), wait=[("mm", mv)])
        vv = prog.val("vx")
        for ci in range(8):
            for half in range(2):
                T(lambda e, ci=ci, half=half: e.matmul(
                    pkA[:, 512 * half:512 * (half + 1)],
                    big[:, XPF + P * ci:XPF + P * (ci + 1)],
                    big[:, BR0 + 1024 * ci + 512 * half:BR0 + 1024 * ci + 512 * (half + 1)],
                    start=(ci == 0), stop=(ci == 7)),
                  wait=([("vx", vv)] + (list(WBR_L) if tt == 0 else [])) if (ci == 0 and half == 0) else (),
                  inc=[("mm", 1)] if (ci == 7 and half == 1) else [])
        YMM = prog.val("mm")
        for q in range(4):
            for j in range(4):
                V(lambda e, tt=tt, q=q, j=j: e.tensor_copy(
                    mixbuf[:, 256 * j:256 * (j + 1)],
                    xs[:, D * tt + 1024 * j + 256 * q: D * tt + 1024 * j + 256 * (q + 1)]),
                  wait=[("mm", YMM)] if (q == 0 and j == 0) else ())
            for i in range(4):
                dst = xs[:, D * tt + 1024 * i + 256 * q: D * tt + 1024 * i + 256 * (q + 1)]
                for j in range(4):
                    V(lambda e, dst=dst, hcol=hcol, i=i, j=j: e.scalar_tensor_tensor(
                        dst, mixbuf[:, 256 * j:256 * (j + 1)], hcol(4 * i + j), dst,
                        ALU.mult, ALU.add))
                V(lambda e, dst=dst, hcol=hcol, i=i, q=q: e.scalar_tensor_tensor(
                    dst, pkA[:, 256 * q:256 * (q + 1)], hcol(20 + i), dst, ALU.mult, ALU.add))
        for ob in range(8):
            seg = xs[:, D * tt + 512 * ob: D * tt + 512 * (ob + 1)]
            V(lambda e, seg=seg, tt=tt, ob=ob: e.tensor_scalar(
                seg, seg, 0.0, 0.0, ALU.add, ALU.add,
                accum_out=sums_s[:, 8 * tt + ob:8 * tt + ob + 1]))
            Sc(lambda e, seg=seg, tt=tt, ob=ob: e.activation(
                mixbuf[:, 0:512], seg, AF.Square,
                accum_out=sqs_s[:, 8 * tt + ob:8 * tt + ob + 1]), wait=prog.after("vx"))
        layernorm(tt, stat[:, tt:tt + 1], stat[:, 8 + tt:9 + tt])
        for icb in range(4):
            for k8 in range(8):
                ic = 8 * (icb % 2) + k8 if False else icb * 8 + k8
                pk = pkC if icb % 2 == 0 else pkD
                T(lambda e, tt=tt, ic=ic, k8=k8, pk=pk: e.transpose(
                    pk[:, P * k8:P * (k8 + 1)],
                    xs[:, D * tt + P * ic: D * tt + P * (ic + 1)], identF[:, :]),
                  wait=prog.after("sx", "vx") if k8 == 0 else (),
                  inc=[("mm", 1)] if k8 == 7 else [])
            mv = prog.val("mm")
            pk = pkC if icb % 2 == 0 else pkD
            stg = big[:, X2ST + 1024 * (icb % 2):X2ST + 1024 * (icb % 2) + 1024]
            V(lambda e, pk=pk, stg=stg: e.tensor_copy(stg, pk[:, 0:1024]), wait=[("mm", mv)])
            dma("sync", "dO",
                x2c.ap()[P * 8 * icb:P * 8 * (icb + 1), P * tt:P * (tt + 1)]
                .rearrange("(c p) t -> p c t", p=P),
                stg.rearrange("p (c t) -> p c t", t=P), wait=prog.after("vx"))
    prog.emit("gpsimd",
              lambda e: e.collective_compute("AllGather", ALU.bypass,
                                             replica_groups=[list(range(NCORES))],
                                             ins=[x2c.ap()], outs=[x2full.ap()]),
              wait=prog.after("dO"), inc=[("cc", 1)])
    X2F = prog.after("cc")

    if dev == 2:
        return dump_and_finish("x2")

    for ic in range(NIF):
        dma("gpsimd", "dG", big[:, 512 * ic:512 * (ic + 1)],
            wq_slT[P * ic:P * (ic + 1), :], wait=prog.after("mm", "sx") if ic == 0 else ())
    WQ_L = prog.after("dG")
    qguards = []
    for tb in range(16):
        pk0, pk1 = (pkA, pkB) if tb % 2 == 0 else (pkC, pkD)
        banks = [(pk0, 0), (pk0, 512), (pk1, 0), (pk1, 512)]
        tglob = 512 * tb
        r, tloc = tglob // TCORE, tglob % TCORE
        for ic in range(NIF):
            buf = wbuf[:, 512 * (ic % 2):512 * (ic % 2) + 512]
            gw = [("mm", qguards[-2])] if len(qguards) >= 2 else \
                (prog.after("mm") + list(X2F))
            dma("sync", "dW", buf,
                x2full.ap()[D * r + P * ic: D * r + P * (ic + 1), tloc:tloc + 512],
                wait=gw)
            wv = prog.after("dW")
            for oc in range(4):
                pk, off = banks[oc]
                w = list(wv) if oc == 0 else []
                if ic == 0 and oc == 0:
                    w += prog.after("sx") + (list(WQ_L) if tb == 0 else [])
                T(lambda e, pk=pk, off=off, ic=ic, oc=oc, buf=buf: e.matmul(
                    pk[:, off:off + 512],
                    big[:, 512 * ic + P * oc:512 * ic + P * (oc + 1)], buf,
                    start=(ic == 0), stop=(ic == NIF - 1)),
                  wait=w, inc=[("mm", 1)] if oc == 3 else [])
            qguards.append(prog.val("mm"))
        mmv = prog.val("mm")
        for oc in range(4):
            pk, off = banks[oc]
            stg = wbuf2[:, 512 * (oc % 2):512 * (oc % 2) + 512]
            Sc(lambda e, pk=pk, off=off, stg=stg: e.copy(stg, pk[:, off:off + 512]),
               wait=[("mm", mmv)] + prog.after("dO"))
            dma("sync", "dO", q_scr.ap()[P * oc:P * (oc + 1), 512 * tb:512 * (tb + 1)], stg,
                wait=prog.after("sx"))
    for ic in range(NIF):
        dma("gpsimd", "dG", big[:, 256 * ic:256 * (ic + 1)],
            wkv_slT[P * ic:P * (ic + 1), :], wait=prog.after("mm", "sx") if ic == 0 else ())
    WKV_L = prog.after("dG")
    kvguards = []
    for tb in range(16):
        pk0 = pkA if tb % 2 == 0 else pkC
        tglob = 512 * tb
        r, tloc = tglob // TCORE, tglob % TCORE
        for ic in range(NIF):
            buf = wbuf[:, 512 * (ic % 2):512 * (ic % 2) + 512]
            gw = [("mm", kvguards[-2])] if len(kvguards) >= 2 else prog.after("mm")
            dma("sync", "dW", buf,
                x2full.ap()[D * r + P * ic: D * r + P * (ic + 1), tloc:tloc + 512],
                wait=gw)
            wv = prog.after("dW")
            for oc in range(2):
                w = list(wv) if oc == 0 else []
                if ic == 0 and oc == 0:
                    w += prog.after("sx", "vx") + (list(WKV_L) if tb == 0 else [])
                T(lambda e, pk0=pk0, oc=oc, ic=ic, buf=buf: e.matmul(
                    pk0[:, 512 * oc:512 * (oc + 1)],
                    big[:, 256 * ic + P * oc:256 * ic + P * (oc + 1)], buf,
                    start=(ic == 0), stop=(ic == NIF - 1)),
                  wait=w, inc=[("mm", 1)] if oc == 1 else [])
            kvguards.append(prog.val("mm"))
        mmv = prog.val("mm")
        Sc(lambda e, pk0=pk0: e.copy(big[:, 8192:8704], pk0[:, 0:512]),
           wait=[("mm", mmv)] + prog.after("dO"))
        dma("sync", "dO", k_scr.ap()[:, 512 * tb:512 * (tb + 1)], big[:, 8192:8704],
            wait=prog.after("sx"))
        V(lambda e, pk0=pk0: e.tensor_copy(mixbuf[:, 0:512], pk0[:, 512:1024]),
          wait=[("mm", mmv)])
        vv = prog.val("vx")
        pkt = pkB if tb % 2 == 0 else pkD
        for s_ in range(4):
            T(lambda e, s_=s_, pkt=pkt: e.transpose(pkt[:, P * s_:P * (s_ + 1)],
                                                    mixbuf[:, P * s_:P * (s_ + 1)],
                                                    identF[:, :]),
              wait=[("vx", vv)] if s_ == 0 else (), inc=[("mm", 1)] if s_ == 3 else [])
        mv2 = prog.val("mm")
        V(lambda e, pkt=pkt: e.tensor_copy(big[:, 8704:9216], pkt[:, 0:512]),
          wait=[("mm", mv2)] + prog.after("dO"))
        dma("sync", "dO",
            v_scr.ap()[512 * tb:512 * (tb + 1), :].rearrange("(s p) d -> p s d", p=P),
            big[:, 8704:9216].rearrange("p (s d) -> p s d", d=P), wait=prog.after("vx"))

    KB0, VB0, QB0, OS0 = 0, 4096, 8224, 12320
    V(lambda e: e.memset(big[:, VB0:VB0 + 4128].rearrange("p (k c) -> p k c", c=129)
                         [:, :, 128:129], 1.0), wait=prog.after("dO", "mm", "sx"))
    ONES_SET = prog.after("vx")
    sx_par = {0: 0, 1: 0}
    pv_par = {0: 0, 1: 0}
    for b in range(2):
        dma("sync", "dW", big[:, KB0:KB0 + 4096], k_scr.ap()[:, S * b:S * (b + 1)],
            wait=prog.after("mm") + (list(ONES_SET) if b == 0 else []))
        dma("sync", "dW",
            big[:, VB0:VB0 + 4128].rearrange("p (k c) -> p k c", c=129)[:, :, 0:128],
            v_scr.ap()[S * b:S * (b + 1), :].rearrange("(k p) d -> p k d", p=P))
        KVL = prog.after("dW")
        for qh in range(4):
            dma("scalar", "dS", big[:, QB0:QB0 + 4096],
                q_scr.ap()[P * qh:P * (qh + 1), S * b:S * (b + 1)],
                wait=prog.after("mm"))
            QL = prog.after("dS")
            kjp_abs = 0
            for qi in range(8):
                oslots = [(pkC, 0), (pkC, 512), (pkD, 0), (pkD, 512)]
                nkj = 4 * (qi + 1)
                ev_vx = prog.val("vx")
                for kjp in range((nkj + 1) // 2):
                    par = kjp_abs % 2
                    spk = pkA if par == 0 else pkB
                    kj0 = 2 * kjp
                    npair = min(2, nkj - kj0)
                    w = [("sx", sx_par[par])] + list(KVL) + list(QL)
                    for kk in range(npair):
                        kj = kj0 + kk
                        T(lambda e, spk=spk, kk=kk, kj=kj, qi=qi: e.matmul(
                            spk[:, 512 * kk:512 * (kk + 1)],
                            big[:, KB0 + P * kj:KB0 + P * (kj + 1)],
                            big[:, QB0 + 512 * qi:QB0 + 512 * (qi + 1)],
                            start=True, stop=True),
                          wait=w if kk == 0 else (),
                          inc=[("mm", 1)] if kk == npair - 1 else [])
                    mmv = prog.val("mm")
                    pt = ptbuf[:, 1024 * par:1024 * par + 1024]
                    Sc(lambda e, spk=spk, pt=pt, npair=npair: e.activation(
                        pt[:, 0:512 * npair], spk[:, 0:512 * npair], AF.Exp,
                        scale=INV_SQRT_HD),
                       wait=[("mm", max(mmv, pv_par[par]))] if False else
                       [("mm", mmv), ("mm", pv_par[par])], inc=[("sx", 1)])
                    sx_par[par] = prog.val("sx")
                    gx_need = None
                    for kk in range(npair):
                        kj = kj0 + kk
                        d_ = kj - 4 * qi
                        if d_ >= 0:
                            G(lambda e, pt=pt, kk=kk, d_=d_: e.affine_select(
                                out=pt[:, 512 * kk:512 * (kk + 1)],
                                in_=pt[:, 512 * kk:512 * (kk + 1)],
                                compare_op=ALU.is_ge, fill=0.0,
                                pattern=[[1, 512]], base=-128 * d_,
                                channel_multiplier=-1),
                              wait=prog.after("sx"))
                            gx_need = prog.val("gx")
                    pvw = [("sx", sx_par[par]), ("vx", ev_vx)]
                    if gx_need is not None:
                        pvw.append(("gx", gx_need))
                    for kk in range(npair):
                        kj = kj0 + kk
                        for qt in range(4):
                            pko, ooff = oslots[qt]
                            T(lambda e, pko=pko, ooff=ooff, pt=pt, kk=kk, kj=kj, qt=qt,
                              nkj=nkj: e.matmul(
                                  pko[:, ooff:ooff + 129],
                                  pt[:, 512 * kk + P * qt:512 * kk + P * (qt + 1)],
                                  big[:, VB0 + 129 * kj:VB0 + 129 * (kj + 1)],
                                  start=(kj == 0), stop=(kj == nkj - 1)),
                              wait=pvw if (qt == 0 and kk == 0) else (),
                              inc=[("mm", 1)] if (kj == nkj - 1 and qt == 3) else [])
                    pv_par[par] = prog.val("mm")
                    kjp_abs += 1
                mmo = prog.val("mm")
                for qt in range(4):
                    pko, ooff = oslots[qt]
                    V(lambda e, pko=pko, ooff=ooff: e.reciprocal(
                        stat[:, 62:63], pko[:, ooff + 128:ooff + 129]),
                      wait=[("mm", mmo)] + prog.after("dO"))
                    V(lambda e, pko=pko, ooff=ooff, qt=qt: e.tensor_scalar(
                        big[:, OS0 + P * qt:OS0 + P * (qt + 1)],
                        pko[:, ooff:ooff + 128], stat[:, 62:63], None, ALU.mult))
                dma("sync", "dO",
                    c_o.ap()[S * b + 512 * qi:S * b + 512 * (qi + 1),
                             P * qh:P * (qh + 1)].rearrange("(s p) d -> p s d", p=P),
                    big[:, OS0:OS0 + 512].rearrange("p (s d) -> p s d", d=P),
                    wait=prog.after("vx"))
    prog.emit("gpsimd",
              lambda e: e.collective_compute("AllGather", ALU.bypass,
                                             replica_groups=[list(range(NCORES))],
                                             ins=[c_o.ap()], outs=[ag_o.ap()]),
              wait=prog.after("dO"), inc=[("cc", 1)])
    AGO = prog.after("cc")

    reg_cell = {}

    def p5_pre(grp):
        qeng = "gpsimd" if grp == 0 else "scalar"
        qsem = "dG" if grp == 0 else "dS"
        for i in range(NCORES):
            def fn(e, i=i, grp=grp, qeng=qeng):
                key = qeng
                if key not in reg_cell:
                    reg_cell[key] = e.alloc_register(name="ob_" + key)
                    e.reg_load(reg_cell[key], ibase[0:1, 0:1])
                base = e.snap(reg_cell[key], min_val=0, max_val=NTOK - TCORE)
                return e.dma_start(
                    out=ptbuf[:, :].rearrange("p (tt d) -> p tt d", d=512),
                    in_=ag_o.ap()[bass.ds(base + NTOK * i + 512 * grp, 512), :]
                    .rearrange("(tt p) d -> p tt d", p=P))
            prog.emit(qeng, fn,
                      wait=(list(AGO) + prog.after("mm", "sx", "vx", "dW", "gx"))
                      if (grp == 0 and i == 0) else prog.after("vx", "mm"),
                      inc=[(qsem, 16)])
            LD = prog.after(qsem)
            pview = ptbuf[:, :].rearrange("p (tt d) -> p tt d", d=512)
            for tt in range(4):
                V(lambda e, tt=tt, pview=pview: e.tensor_copy(
                    mixbuf[:, 512:1024], pview[:, tt, :]), wait=LD)
                vv = prog.val("vx")
                pk = pkC if tt % 2 == 0 else pkD
                for cc in range(4):
                    T(lambda e, cc=cc, pk=pk: e.transpose(
                        pk[:, P * cc:P * (cc + 1)], mixbuf[:, 512 + P * cc:512 + P * (cc + 1)],
                        identF[:, :]),
                      wait=([("vx", vv)] + prog.after("mm")) if cc == 0 else (),
                      inc=[("mm", 1)] if cc == 3 else [])
                mv = prog.val("mm")
                V(lambda e, i=i, tt=tt, pk=pk: e.tensor_copy(
                    bass.AP(big, 512 * 4 * i + P * tt, [[16384, P], [512, 4], [1, P]]),
                    pk[:, 0:512].rearrange("p (c d) -> p c d", d=P)), wait=[("mm", mv)])
        return prog.after("vx")

    def wo_rhs(grp, ob, fc, gw):
        wsl = wbuf[:, 512:1024] if fc % 2 else wbuf2[:, 512:1024]
        dma("sync", "dW", wsl, agw_wo(fc, 512 * ob, 512),
            wait=(gw if gw is not None else prog.after("mm")))
        return wsl

    def wo_lhs(grp, ti, fc):
        return big[:, 512 * fc + P * ti:512 * fc + P * (ti + 1)]

    def wo_evac(grp, ob, ti, pk, off, mmv):
        tt = 4 * grp + ti
        dst = xs[:, D * tt + 512 * ob: D * tt + 512 * (ob + 1)]
        V(lambda e, dst=dst, pk=pk, off=off, tt=tt, ob=ob: e.scalar_tensor_tensor(
            dst, pk[:, off:off + 512], 1.0, dst, ALU.mult, ALU.add,
            accum_out=sums_s[:, 8 * tt + ob: 8 * tt + ob + 1]), wait=[("mm", mmv)])
        Sc(lambda e, dst=dst, tt=tt, ob=ob: e.activation(
            mixbuf[:, 0:512], dst, AF.Square,
            accum_out=sqs_s[:, 8 * tt + ob:8 * tt + ob + 1]), wait=prog.after("vx"))

    big_proj(p5_pre, wo_rhs, wo_lhs, 2, 8, NIF, wo_evac)

    for tt in range(NTT):
        layernorm(tt, stat[:, tt:tt + 1], stat[:, 8 + tt:9 + tt])
        V(lambda e, tt=tt: e.reduce_max(stat[:, 16 + tt:17 + tt],
                                        xs[:, D * tt:D * (tt + 1)], axis=AX.X,
                                        apply_absolute_value=True), wait=prog.after("sx"))
        V(lambda e, tt=tt: e.tensor_scalar(stat[:, 24 + tt:25 + tt],
                                           stat[:, 16 + tt:17 + tt], 1e-8, 1.0 / 127.0,
                                           ALU.max, ALU.mult))
        V(lambda e, tt=tt: e.reciprocal(stat[:, 32 + tt:33 + tt], stat[:, 24 + tt:25 + tt]))

    if dev == 4:
        return dump_and_finish("x3")

    T(lambda e: e.transpose(pkC[0:8, 0:P], stat[:, 32:40], identF[:, :]),
      wait=prog.after("vx", "sx"), inc=[("mm", 1)])
    mmv = prog.val("mm")
    V(lambda e: e.tensor_copy(sk_t[0:8, 0:P], pkC[0:8, 0:P]), wait=[("mm", mmv)])
    dma("sync", "dW", hbuf[0:1, :], sk_t[0:8, 0:P], wait=prog.after("vx"))
    INVROW = prog.after("dW")
    dma("sync", "dW", skrow[0:1, :], rowstash.ap()[2:3, :], wait=prog.after("mm"))
    for half in range(2):
        T(lambda e, half=half: e.matmul(pkC[:, 512 * half:512 * (half + 1)], onerow_f[:, :],
                                        skrow[0:1, 512 * half:512 * (half + 1)],
                                        start=True, stop=True),
          wait=prog.after("vx", "dW"), inc=[("mm", 1)] if half else [])
    mmv = prog.val("mm")
    V(lambda e: e.tensor_copy(x0buf[:, 0:1024], pkC[:, 0:1024]), wait=[("mm", mmv)])
    BIA = 12288
    for k in range(4):
        dma("sync", "dW", skrow[0:1, :], rowstash.ap()[3 + k:4 + k, :],
            wait=prog.after("vx"))
        V(lambda e, k=k: e.tensor_copy(big[0:1, BIA + 1024 * k:BIA + 1024 * (k + 1)],
                                       skrow[0:1, :]), wait=prog.after("dW"))
    MH0, ML0, XQ0, AF0 = 8192, 9216, 0, 10240
    dma("sync", "dW", skrow[0:1, :], rowstash.ap()[0:1, :], wait=prog.after("vx"))
    for half in range(2):
        T(lambda e, half=half: e.matmul(pkC[:, 512 * half:512 * (half + 1)], onerow_f[:, :],
                                        skrow[0:1, 512 * half:512 * (half + 1)],
                                        start=True, stop=True),
          wait=prog.after("dW", "vx"), inc=[("mm", 1)] if half else [])
    mmv = prog.val("mm")
    V(lambda e: e.tensor_copy(big[:, MH0:MH0 + 1024], pkC[:, 0:1024]), wait=[("mm", mmv)])
    dma("sync", "dW", skrow[0:1, :], rowstash.ap()[1:2, :], wait=prog.after("vx"))
    for half in range(2):
        T(lambda e, half=half: e.matmul(pkD[:, 512 * half:512 * (half + 1)], onerow_f[:, :],
                                        skrow[0:1, 512 * half:512 * (half + 1)],
                                        start=True, stop=True),
          wait=prog.after("vx", "dW"), inc=[("mm", 1)] if half else [])
    mmv = prog.val("mm")
    V(lambda e: e.tensor_copy(big[:, ML0:ML0 + 1024], pkD[:, 0:1024]), wait=[("mm", mmv)])
    REPS_V = prog.val("vx")
    s1guards = []
    s2guards = []

    for grp in range(4):
        tiles = (2 * grp, 2 * grp + 1)
        T(lambda e, grp=grp: e.matmul(pkC[:, 0:256], onerow_f[:, :],
                                      hbuf[0:1, 256 * grp:256 * (grp + 1)],
                                      start=True, stop=True),
          wait=prog.after("vx") + [("vx", REPS_V)] + list(INVROW), inc=[("mm", 1)])
        mmv = prog.val("mm")
        V(lambda e: e.tensor_copy(mixbuf[:, 0:256], pkC[:, 0:256]), wait=[("mm", mmv)])
        for ic in range(NIF):
            for k2 in range(2):
                tt = tiles[k2]
                T(lambda e, ic=ic, k2=k2, tt=tt: e.transpose(
                    pkC[:, 512 + P * k2:512 + P * (k2 + 1)],
                    xs[:, D * tt + P * ic: D * tt + P * (ic + 1)], identF[:, :]),
                  wait=prog.after("vx") if k2 == 0 else (),
                  inc=[("mm", 1)] if k2 == 1 else [])
            mv = prog.val("mm")
            V(lambda e, ic=ic: e.tensor_tensor(mixbuf[:, 256:512], pkC[:, 512:768],
                                               mixbuf[:, 0:256], ALU.mult), wait=[("mm", mv)])
            V(lambda e, ic=ic: e.tensor_scalar(big[:, XQ0 + 256 * ic:XQ0 + 256 * (ic + 1)],
                                               mixbuf[:, 256:512], MAGIC, MAGIC,
                                               ALU.add, ALU.subtract))
        XQV = prog.val("vx")
        for rb in range(2):
            for ic in range(NIF):
                buf = wbuf[:, 512 * (ic % 2):512 * (ic % 2) + 512]
                gw = [("mm", s1guards[-2])] if len(s1guards) >= 2 else prog.after("mm")
                dma("sync", "dW", buf, agw_vq(ic, 512 * rb, 512), wait=gw)
                wv = prog.after("dW")
                for k2 in range(2):
                    w = list(wv) if k2 == 0 else []
                    if ic == 0 and k2 == 0:
                        w += [("vx", XQV)] + prog.after("vx")
                    T(lambda e, k2=k2, ic=ic, buf=buf: e.matmul(
                        pkA[:, 512 * k2:512 * (k2 + 1)],
                        big[:, XQ0 + 256 * ic + P * k2:XQ0 + 256 * ic + P * (k2 + 1)],
                        buf, start=(ic == 0), stop=(ic == NIF - 1)),
                      wait=w, inc=[("mm", 1)] if (k2 == 1 or ic == NIF - 1) else [])
                s1guards.append(prog.val("mm"))
            mmv = prog.val("mm")
            for k2 in range(2):
                tt = tiles[k2]
                V(lambda e, k2=k2, rb=rb, tt=tt: e.scalar_tensor_tensor(
                    y1g[:, 1024 * k2 + 512 * rb:1024 * k2 + 512 * (rb + 1)],
                    pkA[:, 512 * k2:512 * (k2 + 1)], stat[:, 24 + tt:25 + tt],
                    x0buf[:, 512 * rb:512 * (rb + 1)], ALU.mult, ALU.mult),
                  wait=[("mm", mmv)])
        if dev == 6 and grp == 0:
            dma("sync", "dO", out[128:256, 1088:1344], mixbuf[:, 0:256],
                wait=prog.after("vx"))
            dma("sync", "dO", out[0:P, 0:2048], y1g[:, :])
            V(lambda e: e.tensor_copy(mixbuf[:, 0:1024], big[:, XQ0:XQ0 + 1024]),
              wait=prog.after("dO"))
            dma("sync", "dO", out[0:P, 2048:3072], mixbuf[:, 0:1024],
                wait=prog.after("vx"))
            dma("sync", "dO", out[128:256, 0:64], stat[:, 0:64])
            dma("sync", "dO", out[128:256, 64:1088], x0buf[:, 0:1024])
            return early_finish()
        for k2 in range(2):
            tt = tiles[k2]
            y1 = y1g[:, 1024 * k2:1024 * (k2 + 1)]
            V(lambda e, y1=y1: e.tensor_tensor(mixbuf[:, :], y1, big[:, MH0:MH0 + 1024],
                                               ALU.mult))
            V(lambda e: e.reduce_max(stat[:, 40:41], mixbuf[:, :], axis=AX.X,
                                     apply_absolute_value=True))
            V(lambda e, y1=y1: e.tensor_tensor(mixbuf[:, :], y1, big[:, ML0:ML0 + 1024],
                                               ALU.mult))
            V(lambda e: e.reduce_max(stat[:, 41:42], mixbuf[:, :], axis=AX.X,
                                     apply_absolute_value=True))
            V(lambda e: e.tensor_scalar(stat[:, 42:43], stat[:, 40:41], 1e-8, 1.0 / 127.0,
                                        ALU.max, ALU.mult))
            V(lambda e: e.tensor_scalar(stat[:, 43:44], stat[:, 41:42], 1e-8, 1.0 / 127.0,
                                        ALU.max, ALU.mult))
            V(lambda e: e.reciprocal(stat[:, 44:45], stat[:, 42:43]))
            V(lambda e: e.reciprocal(stat[:, 45:46], stat[:, 43:44]))
            V(lambda e: e.tensor_scalar(mixbuf[:, :], big[:, MH0:MH0 + 1024],
                                        stat[:, 44:45], None, ALU.mult))
            V(lambda e: e.scalar_tensor_tensor(mixbuf[:, :], big[:, ML0:ML0 + 1024],
                                               stat[:, 45:46], mixbuf[:, :],
                                               ALU.mult, ALU.add))
            V(lambda e, y1=y1: e.tensor_tensor(y1, y1, mixbuf[:, :], ALU.mult))
            V(lambda e, y1=y1: e.tensor_scalar(y1, y1, MAGIC, MAGIC, ALU.add, ALU.subtract))
            V(lambda e: e.tensor_scalar(mixbuf[:, :], big[:, MH0:MH0 + 1024],
                                        stat[:, 42:43], None, ALU.mult))
            V(lambda e: e.scalar_tensor_tensor(mixbuf[:, :], big[:, ML0:ML0 + 1024],
                                               stat[:, 43:44], mixbuf[:, :],
                                               ALU.mult, ALU.add))
            V(lambda e, y1=y1: e.tensor_tensor(y1, y1, mixbuf[:, :], ALU.mult))
            vv = prog.val("vx")
            for rc in range(8):
                T(lambda e, y1=y1, rc=rc: e.transpose(
                    pkB[:, P * (rc % 8):P * (rc % 8) + P], y1[:, P * rc:P * (rc + 1)],
                    identF[:, :]),
                  wait=([("vx", vv)] + prog.after("mm")) if rc == 0 else (),
                  inc=[("mm", 1)] if rc == 7 else [])
            mv = prog.val("mm")
            V(lambda e, k2=k2: e.tensor_copy(big[:, AF0 + 1024 * k2:AF0 + 1024 * (k2 + 1)],
                                             pkB[:, 0:1024]), wait=[("mm", mv)])
        AFV = prog.val("vx")
        for ob in range(8):
            pks = pkA if ob % 2 == 0 else pkB
            for rc in range(8):
                buf = wbuf2[:, 512 * (rc % 2):512 * (rc % 2) + 512]
                gw = [("mm", s2guards[-2])] if len(s2guards) >= 2 else prog.after("mm")
                dma("scalar", "dS", buf, agw_bt(rc, 512 * ob, 512), wait=gw)
                wv = prog.after("dS")
                for k2 in range(2):
                    w = list(wv) if k2 == 0 else []
                    if rc == 0:
                        if k2 == 0:
                            w += prog.after("vx") + ([("vx", AFV)] if ob == 0 else [])
                        T(lambda e, pks=pks, k2=k2, ob=ob: e.matmul(
                            pks[:, 512 * k2:512 * (k2 + 1)], ones_bf[:, 0:P],
                            big[0:1, BIA + 512 * ob:BIA + 512 * ob + 512],
                            start=True, stop=False), wait=w, inc=[])
                        w = []
                    T(lambda e, pks=pks, k2=k2, rc=rc, buf=buf: e.matmul(
                        pks[:, 512 * k2:512 * (k2 + 1)],
                        big[:, AF0 + 1024 * k2 + P * rc:AF0 + 1024 * k2 + P * (rc + 1)],
                        buf, start=False, stop=(rc == 7)),
                      wait=w, inc=[("mm", 1)] if (k2 == 1 or rc == 7) else [])
                s2guards.append(prog.val("mm"))
            mmv = prog.val("mm")
            for k2 in range(2):
                tt = tiles[k2]
                dst = xs[:, D * tt + 512 * ob: D * tt + 512 * (ob + 1)]
                V(lambda e, dst=dst, pks=pks, k2=k2, tt=tt, ob=ob: e.scalar_tensor_tensor(
                    dst, pks[:, 512 * k2:512 * (k2 + 1)], 1.0, dst, ALU.mult, ALU.add,
                    accum_out=sums_s[:, 8 * tt + ob:8 * tt + ob + 1]), wait=[("mm", mmv)])
                Sc(lambda e, dst=dst, tt=tt, ob=ob: e.activation(
                    mixbuf[:, 0:512], dst, AF.Square,
                    accum_out=sqs_s[:, 8 * tt + ob:8 * tt + ob + 1]), wait=prog.after("vx"))
        for k2 in range(2):
            tt = tiles[k2]
            layernorm(tt, stat[:, tt:tt + 1], stat[:, 8 + tt:9 + tt])
            dma("sync", "dO", out[P * tt:P * (tt + 1), :], xs[:, D * tt:D * (tt + 1)],
                wait=prog.after("sx", "vx"))

    prog.emit("sync", lambda e: e.nop(), wait=prog.after("dO"), inc=[])
    finish(nc, st, prog, sems, semnames)
    return nc


def finish(nc, st, prog, sems, semnames):
    for name in semnames:
        sems[name] = st.enter_context(nc.semaphore(name))
    with nc.Block() as block:
        @block.tensor
        def _(e):
            prog.replay("tensor", e, sems)

        @block.vector
        def _(e):
            prog.replay("vector", e, sems)

        @block.scalar
        def _(e):
            prog.replay("scalar", e, sems)

        @block.gpsimd
        def _(e):
            prog.replay("gpsimd", e, sems)

        @block.sync
        def _(e):
            prog.replay("sync", e, sems)
    st.close()


def make_in_maps(inputs):
    x0 = np.ascontiguousarray(np.asarray(inputs["L0_attn_data"], dtype=np.float32)
                              .reshape(NTOK, D))
    x0T = np.ascontiguousarray(x0.T)
    W = np.asarray(inputs["L0_attn_weight"], dtype=np.float32)
    WT = np.ascontiguousarray(W.T)
    mask = np.asarray(inputs["L0_attn_mask"], dtype=np.int32)
    wcatT = np.ascontiguousarray(np.concatenate(
        [np.asarray(inputs["L0_ffn_W_res"], np.float32),
         np.asarray(inputs["L0_ffn_W_pre"], np.float32),
         np.asarray(inputs["L0_ffn_W_post"], np.float32)], axis=0).T)
    hb = np.concatenate([np.asarray(inputs["L0_ffn_H_res_base"], np.float32).reshape(16),
                         np.asarray(inputs["L0_ffn_H_pre_base"], np.float32).reshape(4),
                         np.asarray(inputs["L0_ffn_H_post_base"], np.float32).reshape(4)])
    Wq = np.asarray(inputs["L1_attn_Wq"], np.float32)
    Wkv = np.asarray(inputs["L1_attn_Wkv"], np.float32)
    WoT = np.ascontiguousarray(np.asarray(inputs["L1_attn_Wo"], np.float32).T)
    vT = np.ascontiguousarray(np.asarray(inputs["L1_ffn_v_weight"], np.float32).T)
    uT = np.ascontiguousarray(np.asarray(inputs["L1_ffn_u_weight"], np.float32).T)
    maps = []
    for c in range(NCORES):
        kvrows = list(range(128 * c, 128 * (c + 1))) + \
            list(range(1024 + 128 * c, 1024 + 128 * (c + 1)))
        maps.append({
            "x0_tm": np.ascontiguousarray(x0[TCORE * c:TCORE * (c + 1), :]),
            "x0_fm": np.ascontiguousarray(x0T[:, TCORE * c:TCORE * (c + 1)]),
            "l0_wt_sl": np.ascontiguousarray(WT[512 * c:512 * (c + 1), :]),
            "l0_mask_sl": np.ascontiguousarray(mask[:, 512 * c:512 * (c + 1)]),
            "l0_bias": np.ascontiguousarray(
                np.asarray(inputs["L0_attn_bias"], np.float32).reshape(1, D)),
            "wcatT": wcatT,
            "hbases": np.ascontiguousarray(hb.reshape(1, 24)),
            "gate_in": np.ascontiguousarray(
                np.asarray(inputs["L0_ffn_gate"], np.float32).reshape(1, 1)),
            "wbr": np.ascontiguousarray(np.asarray(inputs["L0_ffn_W_branch"], np.float32)),
            "wq_slT": np.ascontiguousarray(Wq[512 * c:512 * (c + 1), :].T),
            "wkv_slT": np.ascontiguousarray(Wkv[kvrows, :].T),
            "wo_slT": np.ascontiguousarray(WoT[512 * c:512 * (c + 1), :]),
            "v_slT": np.ascontiguousarray(vT[512 * c:512 * (c + 1), :]),
            "uT_sl": np.ascontiguousarray(uT[128 * c:128 * (c + 1), :]),
            "r_base_col": np.arange(128 * c, 128 * (c + 1), dtype=np.float32).reshape(P, 1),
            "hi_idx": np.ascontiguousarray(
                np.asarray(inputs["L1_ffn_high_idx"], np.int32).reshape(1, RHI)),
            "lo_idx": np.ascontiguousarray(
                np.asarray(inputs["L1_ffn_low_idx"], np.int32).reshape(1, RLO)),
            "l1_bias": np.ascontiguousarray(
                np.asarray(inputs["L1_ffn_bias"], np.float32).reshape(1, D)),
            "tok_base": np.array([[TCORE * c]], dtype=np.int32),
        })
    return maps


def kernel(**inputs):
    nc = build_program(dev=0)
    maps = make_in_maps(inputs)
    res = run_bass_kernel_spmd(nc, maps, core_ids=list(range(NCORES)))
    full = np.concatenate([res.results[c]["out"] for c in range(NCORES)], axis=0)
    return full.reshape(B, S, D).astype(np.float32)


# version 10 (final)
# speedup vs baseline: 1.0637x; 1.0054x over previous
import math
from collections import defaultdict

import numpy as np
import concourse.bass as bass
import concourse.mybir as mybir
from concourse.bass_isa import ReduceOp
from concourse.bass_utils import run_bass_kernel_spmd

F32 = mybir.dt.float32
BF16 = mybir.dt.bfloat16
F16 = mybir.dt.float16
I32 = mybir.dt.int32
AF = mybir.ActivationFunctionType
ALU = mybir.AluOpType
AX = mybir.AxisListType

NCORES = 8
P = 128
D = 4096
B = 2
S = 4096
NTOK = B * S
TCORE = NTOK // NCORES
NTT = TCORE // P
NIF = D // P
MAGIC = float(np.float32(1.5 * 2 ** 23))
LN_EPS = 1e-5
RMS_EPS = 1.1920929e-07
INV_TAU = 10.0
SINK_ITERS = 50
RANK = 1024
RHI = 256
RLO = 768
INV_SQRT_HD = 1.0 / math.sqrt(128.0)

WT_OFF = 0
VQ_OFF = 512 * D
BT_OFF = VQ_OFF + 512 * RANK
WO_OFF = BT_OFF + 128 * D
CONTRIB = WO_OFF + 512 * D
ARN = 2 * D + RANK

ENGINES = ("tensor", "vector", "scalar", "gpsimd", "sync")


class Prog:
    def __init__(self):
        self.ops = {e: [] for e in ENGINES}
        self.cnt = defaultdict(int)

    def val(self, sem):
        return self.cnt[sem]

    def emit(self, eng, fn, wait=(), inc=()):
        self.ops[eng].append((tuple(wait), fn, tuple(inc)))
        for sem, amt in inc:
            self.cnt[sem] += amt

    def after(self, *sems):
        return [(s, self.cnt[s]) for s in sems]

    def replay(self, eng_name, eng, sems):
        hw = defaultdict(int)
        for waits, fn, incs in self.ops[eng_name]:
            for sem, v in waits:
                if v > hw[sem]:
                    eng.wait_ge(sems[sem], v)
                    hw[sem] = v
            inst = fn(eng)
            for sem, amt in incs:
                inst.then_inc(sems[sem], amt)


def build_program(dev=0):
    nc = bass.Bass("TRN2", target_bir_lowering=False)

    x0_tm = nc.dram_tensor("x0_tm", [TCORE, D], F32, kind="ExternalInput").ap()
    x0_fm = nc.dram_tensor("x0_fm", [D, TCORE], F32, kind="ExternalInput").ap()
    l0_wt_sl = nc.dram_tensor("l0_wt_sl", [512, D], F32, kind="ExternalInput").ap()
    l0_mask_sl = nc.dram_tensor("l0_mask_sl", [D, 512], I32, kind="ExternalInput").ap()
    l0_bias = nc.dram_tensor("l0_bias", [1, D], F32, kind="ExternalInput").ap()
    wcatT = nc.dram_tensor("wcatT", [D, 24], F32, kind="ExternalInput").ap()
    hbases = nc.dram_tensor("hbases", [1, 24], F32, kind="ExternalInput").ap()
    gate_in = nc.dram_tensor("gate_in", [1, 1], F32, kind="ExternalInput").ap()
    wbr = nc.dram_tensor("wbr", [1024, 1024], F32, kind="ExternalInput").ap()
    wq_slT = nc.dram_tensor("wq_slT", [D, 512], F32, kind="ExternalInput").ap()
    wkv_slT = nc.dram_tensor("wkv_slT", [D, 256], F32, kind="ExternalInput").ap()
    wo_slT = nc.dram_tensor("wo_slT", [512, D], F32, kind="ExternalInput").ap()
    v_slT = nc.dram_tensor("v_slT", [512, RANK], F32, kind="ExternalInput").ap()
    uT_sl = nc.dram_tensor("uT_sl", [128, D], F32, kind="ExternalInput").ap()
    r_base_col = nc.dram_tensor("r_base_col", [P, 1], F32, kind="ExternalInput").ap()
    hi_idx = nc.dram_tensor("hi_idx", [1, RHI], I32, kind="ExternalInput").ap()
    lo_idx = nc.dram_tensor("lo_idx", [1, RLO], I32, kind="ExternalInput").ap()
    l1_bias = nc.dram_tensor("l1_bias", [1, D], F32, kind="ExternalInput").ap()
    tok_base = nc.dram_tensor("tok_base", [1, 1], I32, kind="ExternalInput").ap()

    out = nc.dram_tensor("out", [TCORE, D], F32, kind="ExternalOutput").ap()

    contrib = nc.dram_tensor("contrib", [CONTRIB], BF16)
    agw = nc.dram_tensor("agw", [NCORES * CONTRIB], BF16, addr_space="Shared")
    ar_in = nc.dram_tensor("ar_in", [1, ARN], F32)
    ar_out = nc.dram_tensor("ar_out", [1, ARN], F32)
    x2c = nc.dram_tensor("x2c", [D, TCORE], BF16)
    x2full = nc.dram_tensor("x2full", [NCORES * D, TCORE], BF16, addr_space="Shared")
    q_scr = nc.dram_tensor("q_scr", [512, NTOK], BF16)
    k_scr = nc.dram_tensor("k_scr", [128, NTOK], BF16)
    v_scr = nc.dram_tensor("v_scr", [NTOK, 128], BF16)
    c_o = nc.dram_tensor("c_o", [NTOK, 512], BF16)
    rowstash = nc.dram_tensor("rowstash", [16, 1024], F32)
    ag_o = nc.dram_tensor("ag_o", [NCORES * NTOK, 512], BF16, addr_space="Shared")

    dv = {}
    if dev == 1:
        dv["x1"] = nc.dram_tensor("dv_x1", [TCORE, D], F32, kind="ExternalOutput").ap()
    if dev == 2:
        dv["x2"] = nc.dram_tensor("dv_x2", [TCORE, D], F32, kind="ExternalOutput").ap()
    if dev == 4:
        dv["x3"] = nc.dram_tensor("dv_x3", [TCORE, D], F32, kind="ExternalOutput").ap()

    prog = Prog()
    import contextlib
    st = contextlib.ExitStack()
    sb = lambda name, shape, dt: st.enter_context(nc.sbuf_tensor(name, shape, dt))
    pst = lambda name, shape, dt: st.enter_context(nc.psum_tensor(name, shape, dt))

    xs = sb("xs", [P, NTT * D], F32)
    big = sb("big", [P, 16384], BF16)
    mixbuf = sb("mixbuf", [P, 1024], F32)
    y1g = sb("y1g", [P, 2048], F32)
    x0buf = sb("x0buf", [P, 1024], F32)
    hbuf = sb("hbuf", [24, 1024], F32)
    sk_a = sb("sk_a", [P, 32], F32)
    sk_b = sb("sk_b", [P, 32], F32)
    sk_t = sb("sk_t", [P, P], F32)
    skrow = sb("skrow", [1, 1024], F32)
    Eit = sb("Eit", [16, 512], BF16)
    P32 = sb("P32", [P, 4], BF16)
    identF = sb("identF", [P, P], F32)
    ones_f32 = sb("ones_f32", [P, 1], F32)
    ones_bf = sb("ones_bf", [1, P], BF16)
    wbuf = sb("wbuf", [P, 1024], BF16)
    wbuf2 = sb("wbuf2", [P, 1024], BF16)
    onerow_f = sb("onerow_f", [1, P], F32)
    sums_s = sb("sums_s", [P, 64], F32)
    sqs_s = sb("sqs_s", [P, 64], F32)
    stat = sb("stat", [P, 64], F32)
    mh_col = sb("mh_col", [P, 1], F32)
    ml_col = sb("ml_col", [P, 1], F32)
    r0col = sb("r0col", [P, 1], F32)
    f16row = sb("f16row", [1, 1024], F16)

    ibase = sb("ibase", [1, 1], I32)
    gcol = sb("gcol", [P, 1], F32)
    ipat = sb("ipat", [P, 1], F32)
    ptbuf = sb("ptbuf", [P, 2048], BF16)

    XW = lambda a, b: xs[:, a:b]
    idxbuf = xs[0:1, 31744:32768]
    ROW = lambda a, b: xs[0:1, a:b]

    pkA = pst("pkA", [P, 1024], F32)
    pkB = pst("pkB", [P, 1024], F32)
    pkC = pst("pkC", [P, 1024], F32)
    pkD = pst("pkD", [P, 1024], F32)

    sems = {}
    semnames = ["dW", "dS", "dG", "dO", "mm", "vx", "sx", "gx", "cc"]

    def dma(queue, sem, dst, src, wait=()):
        prog.emit(queue, lambda e, d=dst, s=src: e.dma_start(out=d, in_=s),
                  wait=wait, inc=[(sem, 16)])

    def agw_wt(ic, of0, nof):
        r, it = ic // 4, ic % 4
        base = r * CONTRIB + WT_OFF + it * P * D + (of0 // 512) * P * 512
        return agw[base: base + P * 512].rearrange("(p f) -> p f", f=512)

    def agw_vq(ic, r0, nr):
        r, row0 = ic // 4, (ic % 4) * P
        base = r * CONTRIB + VQ_OFF + row0 * RANK
        return agw[base: base + P * RANK].rearrange("(p f) -> p f", f=RANK)[:, r0: r0 + nr]

    def agw_bt(rc, of0, nof):
        base = rc * CONTRIB + BT_OFF
        return agw[base: base + P * D].rearrange("(p f) -> p f", f=D)[:, of0: of0 + nof]

    def agw_wo(fc, of0, nof):
        r, row0 = fc // 4, (fc % 4) * P
        base = r * CONTRIB + WO_OFF + row0 * D
        return agw[base: base + P * D].rearrange("(p f) -> p f", f=D)[:, of0: of0 + nof]

    T = lambda fn, wait=(), inc=(): prog.emit("tensor", fn, wait, inc)

    def V(fn, wait=(), inc=(("vx", 1),)):
        prog.emit("vector", fn, tuple(wait) + tuple(prog.after("vx")), inc)

    def Sc(fn, wait=(), inc=(("sx", 1),)):
        prog.emit("scalar", fn, tuple(wait) + tuple(prog.after("sx")), inc)
    def G(fn, wait=(), inc=(("gx", 1),)):
        prog.emit("gpsimd", fn, tuple(wait) + tuple(prog.after("gx")), inc)

    def early_finish():
        dma("sync", "dO", out[0:P, 0:64], stat[:, :],
            wait=prog.after("mm", "vx", "sx", "gx", "dW", "dS", "dG", "cc"))
        prog.emit("sync", lambda e: e.nop(), wait=prog.after("dO"), inc=[])
        finish(nc, st, prog, sems, semnames)
        return nc

    V(lambda e: e.memset(ones_f32[:, :], 1.0))
    V(lambda e: e.memset(onerow_f[:, :], 1.0))
    V(lambda e: e.memset(ones_bf[:, :], 1.0))
    V(lambda e: e.memset(identF[:, :], 1.0))
    G(lambda e: e.affine_select(out=identF[:, :], in_=identF[:, :],
                                compare_op=ALU.is_equal, fill=0.0,
                                pattern=[[-1, P]], base=0, channel_multiplier=1),
      wait=prog.after("vx"))
    for it in range(4):
        G(lambda e, it=it: e.iota(sk_t[0:16, 0:P].rearrange("p (a c) -> p a c", c=32),
                                  pattern=[[-1, 4], [0, 32]], channel_multiplier=1,
                                  base=-4 * it,
                                  allow_small_or_imprecise_dtypes=True),
          wait=prog.after("vx"))
        V(lambda e, it=it: e.tensor_single_scalar(Eit[:, P * it:P * (it + 1)],
                                                  sk_t[0:16, 0:P], 0.0, ALU.is_equal),
          wait=prog.after("gx"))
    G(lambda e: e.iota(sk_t[:, 0:4], pattern=[[32, 4]], channel_multiplier=-1,
                       allow_small_or_imprecise_dtypes=True), wait=prog.after("vx"))
    V(lambda e: e.tensor_single_scalar(sk_t[:, 4:8], sk_t[:, 0:4], 0.0, ALU.is_le),
      wait=prog.after("gx"))
    V(lambda e: e.tensor_single_scalar(sk_t[:, 8:12], sk_t[:, 0:4], -31.5, ALU.is_ge))
    V(lambda e: e.tensor_tensor(P32[:, :], sk_t[:, 4:8], sk_t[:, 8:12], ALU.mult))
    CONSTS = prog.after("gx", "vx")
    if dev == 20:
        return early_finish()

    dma("gpsimd", "dG", idxbuf[:, 0:RHI], hi_idx)
    dma("gpsimd", "dG", idxbuf[:, RHI:RHI + RLO], lo_idx)
    dma("sync", "dW", r0col[:, :], r_base_col)
    dma("sync", "dW", ibase[:, :], tok_base)
    dma("sync", "dW", ROW(0, D), l1_bias)
    for k in range(4):
        V(lambda e, k=k: e.tensor_copy(f16row[:, :], ROW(1024 * k, 1024 * (k + 1))),
          wait=prog.after("dW") if k == 0 else ())
        V(lambda e, k=k: e.tensor_copy(ROW(1024 * k, 1024 * (k + 1)), f16row[:, :]))
    for k in range(4):
        dma("sync", "dO", rowstash.ap()[3 + k:4 + k, :], ROW(1024 * k, 1024 * (k + 1)),
            wait=prog.after("vx"))
    IDX_DONE = prog.after("dG", "dW")

    def member_cols(dst_col, idx_off, nidx, base_col):
        for j in range(0, nidx, 512):
            w = min(512, nidx - j)
            T(lambda e, j=j, w=w: e.matmul(pkA[:, j:j + w], onerow_f[:, :],
                                           idxbuf[:, idx_off + j: idx_off + j + w],
                                           start=True, stop=True),
              wait=IDX_DONE + prog.after("vx"),
              inc=[("mm", 1)] if j + 512 >= nidx else [])
        mmv = prog.val("mm")
        V(lambda e, n=nidx, bc=base_col: e.tensor_scalar(
            XW(0, nidx), pkA[:, 0:nidx], bc, None, ALU.is_equal), wait=[("mm", mmv)])
        V(lambda e, n=nidx, dc=dst_col: e.reduce_max(dc, XW(0, n), axis=AX.X))

    member_cols(mh_col[:, :], 0, RHI, r0col[:, 0:1])
    member_cols(ml_col[:, :], RHI, RLO, r0col[:, 0:1])
    G(lambda e: e.iota(stat[:, 63:64], pattern=[[0, 1]], base=0, channel_multiplier=1,
                       allow_small_or_imprecise_dtypes=True), wait=CONSTS)
    for k in range(8):
        V(lambda e, k=k: e.tensor_scalar(stat[:, 62:63], stat[:, 63:64],
                                         float(128 * k), None, ALU.add),
          wait=prog.after("gx"))
        member_cols(stat[:, 40 + k:41 + k], 0, RHI, stat[:, 62:63])
        member_cols(stat[:, 48 + k:49 + k], RHI, RLO, stat[:, 62:63])
    T(lambda e: e.transpose(pkC[0:8, 0:P], stat[:, 40:48], identF[:, :]),
      wait=prog.after("vx"), inc=[])
    T(lambda e: e.transpose(pkC[0:8, 128:256], stat[:, 48:56], identF[:, :]),
      inc=[("mm", 1)])
    V(lambda e: e.tensor_copy(sk_t[0:8, 0:P], pkC[0:8, 0:P]), wait=prog.after("mm"))
    V(lambda e: e.tensor_copy(mixbuf[0:8, 0:P], pkC[0:8, 128:256]))
    dma("sync", "dW", ROW(13312, 14336), sk_t[0:8, 0:P], wait=prog.after("vx"))
    dma("sync", "dW", ROW(31744, 32768), mixbuf[0:8, 0:P])
    dma("sync", "dO", rowstash.ap()[0:1, :], sk_t[0:8, 0:P])
    dma("sync", "dO", rowstash.ap()[1:2, :], mixbuf[0:8, 0:P])
    prog.emit("vector", lambda e: e.engine_nop(), wait=prog.after("dW"), inc=[("vx", 1)])

    if dev == 21:
        return early_finish()
    dma("sync", "dW", XW(0, D), uT_sl, wait=prog.after("vx"))
    Sc(lambda e: e.activation(XW(D, 2 * D), XW(0, D), AF.Abs), wait=prog.after("dW"))
    V(lambda e: e.tensor_scalar(XW(2 * D, 3 * D), XW(D, 2 * D), mh_col[:, 0:1], None, ALU.mult),
      wait=prog.after("sx"))
    for j in range(32):
        slot = pkA[:, 0:128] if j % 2 == 0 else pkA[:, 512:640]
        T(lambda e, j=j, slot=slot: e.transpose(slot, XW(2 * D, 3 * D)[:, 128 * j:128 * (j + 1)],
                                                identF[:, :]),
          wait=prog.after("vx") , inc=[("mm", 1)])
        mv = prog.val("mm")
        V(lambda e, j=j, slot=slot: e.reduce_max(mixbuf[:, 0 + j:0 + j + 1],
                                                 slot, axis=AX.X), wait=[("mm", mv)])
    T(lambda e: e.transpose(pkB[0:32, 0:128], mixbuf[:, 0:0 + 32],
                            identF[:, :]), wait=prog.after("vx"), inc=[("mm", 1)])
    mv = prog.val("mm")
    V(lambda e: e.tensor_copy(mixbuf[0:32, 512:640], pkB[0:32, 0:128]), wait=[("mm", mv)])
    dma("sync", "dW", ar_in[0:1, 0:D].rearrange("one (j c) -> (one j) c", c=P),
        mixbuf[0:32, 512:640], wait=prog.after("vx"))
    V(lambda e: e.tensor_scalar(XW(2 * D, 3 * D), XW(D, 2 * D), ml_col[:, 0:1], None, ALU.mult),
      wait=prog.after("dW"))
    for j in range(32):
        slot = pkA[:, 0:128] if j % 2 == 0 else pkA[:, 512:640]
        T(lambda e, j=j, slot=slot: e.transpose(slot, XW(2 * D, 3 * D)[:, 128 * j:128 * (j + 1)],
                                                identF[:, :]),
          wait=prog.after("vx") , inc=[("mm", 1)])
        mv = prog.val("mm")
        V(lambda e, j=j, slot=slot: e.reduce_max(mixbuf[:, 32 + j:32 + j + 1],
                                                 slot, axis=AX.X), wait=[("mm", mv)])
    T(lambda e: e.transpose(pkB[0:32, 0:128], mixbuf[:, 32:32 + 32],
                            identF[:, :]), wait=prog.after("vx"), inc=[("mm", 1)])
    mv = prog.val("mm")
    V(lambda e: e.tensor_copy(mixbuf[0:32, 512:640], pkB[0:32, 0:128]), wait=[("mm", mv)])
    dma("sync", "dW", ar_in[0:1, D:2 * D].rearrange("one (j c) -> (one j) c", c=P),
        mixbuf[0:32, 512:640], wait=prog.after("vx"))

    for vt in range(4):
        dma("scalar", "dS", XW(16384, 17408), v_slT[P * vt:P * (vt + 1), :],
            wait=prog.after("sx", "vx"))
        Sc(lambda e: e.activation(XW(17408, 18432),
                                  XW(16384, 17408), AF.Abs),
           wait=prog.after("dS"))
        if vt == 0:
            V(lambda e: e.tensor_copy(XW(18432, 19456),
                                      XW(17408, 18432)),
              wait=prog.after("sx"))
        else:
            V(lambda e: e.tensor_tensor(XW(18432, 19456),
                                        XW(18432, 19456),
                                        XW(17408, 18432), ALU.max),
              wait=prog.after("sx"))
    for j in range(8):
        slot = pkA[:, 0:128] if j % 2 == 0 else pkA[:, 512:640]
        T(lambda e, j=j, slot=slot: e.transpose(slot, XW(18432, 19456)[:, 128 * j:128 * (j + 1)],
                                                identF[:, :]),
          wait=prog.after("vx") , inc=[("mm", 1)])
        mv = prog.val("mm")
        V(lambda e, j=j, slot=slot: e.reduce_max(mixbuf[:, 64 + j:64 + j + 1],
                                                 slot, axis=AX.X), wait=[("mm", mv)])
    T(lambda e: e.transpose(pkB[0:8, 0:128], mixbuf[:, 64:64 + 8],
                            identF[:, :]), wait=prog.after("vx"), inc=[("mm", 1)])
    mv = prog.val("mm")
    V(lambda e: e.tensor_copy(mixbuf[0:8, 512:640], pkB[0:8, 0:128]), wait=[("mm", mv)])
    dma("sync", "dW", ar_in[0:1, 2 * D:2 * D + RANK].rearrange("one (j c) -> (one j) c", c=P),
        mixbuf[0:8, 512:640], wait=prog.after("vx"))

    prog.emit("gpsimd",
              lambda e: e.collective_compute("AllReduce", ALU.max,
                                             replica_groups=[list(range(NCORES))],
                                             ins=[ar_in.ap()], outs=[ar_out.ap()]),
              wait=prog.after("dW"), inc=[("cc", 1)])
    dma("sync", "dW", ROW(20480, 20480 + ARN), ar_out.ap(), wait=prog.after("cc"))
    AR_DONE = prog.after("dW")
    if dev == 22:
        return early_finish()

    US8, US4, I8, I4 = 16384, 4096, 20480, 24576
    U16A, U16B = 16384, 4096
    SC, ISC = 29696, 30720
    V(lambda e: e.tensor_scalar(ROW(US8, US8 + D), ROW(20480, 24576), 1e-8, 1.0 / 127.0,
                                ALU.max, ALU.mult), wait=AR_DONE)
    V(lambda e: e.tensor_scalar(ROW(US4, US4 + D), ROW(24576, 28672), 1e-8, 1.0 / 7.0,
                                ALU.max, ALU.mult))
    V(lambda e: e.reciprocal(ROW(I8, I8 + D), ROW(US8, US8 + D)))
    V(lambda e: e.reciprocal(ROW(I4, I4 + D), ROW(US4, US4 + D)))
    for k in range(4):
        V(lambda e, k=k: e.tensor_copy(f16row[:, :], ROW(US8 + 1024 * k, US8 + 1024 * (k + 1))))
        V(lambda e, k=k: e.tensor_copy(ROW(U16A + 1024 * k, U16A + 1024 * (k + 1)), f16row[:, :]))
    for k in range(4):
        V(lambda e, k=k: e.tensor_copy(f16row[:, :], ROW(US4 + 1024 * k, US4 + 1024 * (k + 1))))
        V(lambda e, k=k: e.tensor_copy(ROW(U16B + 1024 * k, U16B + 1024 * (k + 1)), f16row[:, :]))
    V(lambda e: e.tensor_scalar(ROW(SC + RANK, SC + 2 * RANK), ROW(13312, 14336),
                                1.0 / 127.0, None, ALU.mult))
    V(lambda e: e.scalar_tensor_tensor(ROW(SC + RANK, SC + 2 * RANK), ROW(31744, 32768),
                                       1.0 / 7.0, ROW(SC + RANK, SC + 2 * RANK),
                                       ALU.mult, ALU.add))
    V(lambda e: e.tensor_scalar(ROW(SC, SC + RANK), ROW(28672, 29696), 1e-8, None,
                                ALU.max))
    V(lambda e: e.tensor_tensor(ROW(SC, SC + RANK), ROW(SC, SC + RANK),
                                ROW(SC + RANK, SC + 2 * RANK), ALU.mult))
    V(lambda e: e.reciprocal(ROW(ISC, ISC + RANK), ROW(SC, SC + RANK)))
    V(lambda e: e.tensor_copy(f16row[:, 0:RANK], ROW(SC, SC + RANK)))
    V(lambda e: e.tensor_copy(ROW(28672, 29696), f16row[:, 0:RANK]))
    dma("sync", "dO", rowstash.ap()[2:3, :], ROW(28672, 29696), wait=prog.after("vx"))

    if dev == 23:
        return early_finish()
    for oc in range(8):
        sl = slice(512 * oc, 512 * (oc + 1))
        T(lambda e, oc=oc: e.matmul(pkA[:, 0:512], onerow_f[:, :],
                                    ROW(I8 + 512 * oc, I8 + 512 * (oc + 1)),
                                    start=True, stop=True),
          wait=prog.after("vx"), inc=[])
        T(lambda e, oc=oc: e.matmul(pkB[:, 0:512], onerow_f[:, :],
                                    ROW(I4 + 512 * oc, I4 + 512 * (oc + 1)),
                                    start=True, stop=True), inc=[])
        T(lambda e, oc=oc: e.matmul(pkC[:, 0:512], onerow_f[:, :],
                                    ROW(U16A + 512 * oc, U16A + 512 * (oc + 1)),
                                    start=True, stop=True), inc=[])
        T(lambda e, oc=oc: e.matmul(pkD[:, 0:512], onerow_f[:, :],
                                    ROW(U16B + 512 * oc, U16B + 512 * (oc + 1)),
                                    start=True, stop=True), inc=[("mm", 1)])
        mmv = prog.val("mm")
        V(lambda e: e.tensor_scalar(XW(2 * D, 2 * D + 512), pkA[:, 0:512],
                                    mh_col[:, 0:1], None, ALU.mult), wait=[("mm", mmv)])
        V(lambda e: e.scalar_tensor_tensor(XW(2 * D, 2 * D + 512), pkB[:, 0:512],
                                           ml_col[:, 0:1], XW(2 * D, 2 * D + 512),
                                           ALU.mult, ALU.add))
        V(lambda e: e.tensor_scalar(XW(3 * D, 3 * D + 512), pkC[:, 0:512],
                                    mh_col[:, 0:1], None, ALU.mult))
        V(lambda e: e.scalar_tensor_tensor(XW(3 * D, 3 * D + 512), pkD[:, 0:512],
                                           ml_col[:, 0:1], XW(3 * D, 3 * D + 512),
                                           ALU.mult, ALU.add))
        V(lambda e, sl=sl: e.tensor_tensor(XW(2 * D, 2 * D + 512), XW(0, D)[:, sl],
                                           XW(2 * D, 2 * D + 512), ALU.mult))
        V(lambda e: e.tensor_scalar(XW(2 * D, 2 * D + 512), XW(2 * D, 2 * D + 512),
                                    MAGIC, MAGIC, ALU.add, ALU.subtract))
        V(lambda e, sl=sl: e.tensor_tensor(big[:, sl], XW(2 * D, 2 * D + 512),
                                           XW(3 * D, 3 * D + 512), ALU.mult))
    dma("sync", "dO", contrib[BT_OFF: BT_OFF + P * D].rearrange("(p f) -> p f", f=D),
        big[:, 0:D], wait=prog.after("vx"))

    if dev == 24:
        return early_finish()
    T(lambda e: e.matmul(pkA[:, 0:512], onerow_f[:, :], ROW(ISC, ISC + 512),
                         start=True, stop=True), wait=prog.after("vx"), inc=[])
    T(lambda e: e.matmul(pkA[:, 512:1024], onerow_f[:, :], ROW(ISC + 512, ISC + 1024),
                         start=True, stop=True), inc=[("mm", 1)])
    mmv = prog.val("mm")
    for vt in range(4):
        dma("scalar", "dS", XW(0, RANK), v_slT[P * vt:P * (vt + 1), :],
            wait=prog.after("vx") if vt else prog.after("dO", "vx"))
        V(lambda e: e.tensor_tensor(XW(RANK, 2 * RANK), XW(0, RANK), pkA[:, 0:RANK], ALU.mult),
          wait=[("mm", mmv)] + prog.after("dS"))
        V(lambda e, vt=vt: e.tensor_scalar(big[:, D + RANK * vt: D + RANK * (vt + 1)],
                                           XW(RANK, 2 * RANK), MAGIC, MAGIC,
                                           ALU.add, ALU.subtract))
    for vt in range(4):
        dma("sync", "dO", contrib[VQ_OFF + vt * P * RANK: VQ_OFF + (vt + 1) * P * RANK]
            .rearrange("(p f) -> p f", f=RANK), big[:, D + RANK * vt:D + RANK * (vt + 1)],
            wait=prog.after("vx"))

    for wt in range(4):
        dma("gpsimd", "dG", big[:, 8192:12288], wo_slT[P * wt:P * (wt + 1), :],
            wait=prog.after("dO"))
        dma("sync", "dO", contrib[WO_OFF + wt * P * D: WO_OFF + (wt + 1) * P * D]
            .rearrange("(p f) -> p f", f=D), big[:, 8192:12288], wait=prog.after("dG"))

    if dev == 25:
        return early_finish()
    for mt in range(32):
        dma("gpsimd", "dG", XW(0, 512), l0_mask_sl[P * mt:P * (mt + 1), :],
            wait=prog.after("vx", "mm"))
        V(lambda e: e.reduce_sum(XW(512, 528), XW(0, 512).rearrange("p (g k) -> p g k", k=32),
                                 axis=AX.X), wait=prog.after("dG"))
        V(lambda e: e.tensor_scalar(wbuf[:, 0:16], XW(512, 528), 0.5, None, ALU.is_gt))
        T(lambda e: e.matmul(pkA[0:4, 0:16], P32[:, :], wbuf[:, 0:16], start=True, stop=True),
          wait=prog.after("vx"), inc=[("mm", 1)])
        V(lambda e, mt=mt: e.tensor_scalar(hbuf[0:4, 16 * mt:16 * (mt + 1)], pkA[0:4, 0:16],
                                           0.5, None, ALU.is_gt), wait=prog.after("mm"))
    for mt in range(32):
        T(lambda e, mt=mt: e.transpose(pkC[0:16, 4 * mt:4 * mt + 4],
                                       hbuf[0:4, 16 * mt:16 * (mt + 1)], identF[0:4, 0:4]),
          wait=prog.after("vx"), inc=[("mm", 1)] if mt == 31 else [])
    mmv = prog.val("mm")
    V(lambda e: e.tensor_copy(sk_t[0:16, 0:P], pkC[0:16, 0:P]), wait=[("mm", mmv)])
    for it in range(4):
        dma("scalar", "dS", XW(D, 2 * D), l0_wt_sl[P * it:P * (it + 1), :],
            wait=prog.after("vx"))
        if it == 0:
            for s32 in range(32):
                V(lambda e, s32=s32: e.tensor_copy(
                    big[0:16, 2 * D:3 * D].rearrange("p (n s) -> p n s", s=32)
                    [:, :, s32:s32 + 1],
                    sk_t[0:16, 0:P].rearrange("p (n o) -> p n o", o=1)),
                  wait=prog.after("mm") if s32 == 0 else ())
        BMT_V = prog.val("vx")
        for oc in range(8):
            sl = slice(512 * oc, 512 * (oc + 1))
            T(lambda e, it=it, sl=sl: e.matmul(
                pkA[:, 0:512], Eit[:, P * it:P * (it + 1)],
                big[0:16, 2 * D + sl.start:2 * D + sl.stop],
                start=True, stop=True),
              wait=[("vx", BMT_V)] + prog.after("dS", "vx"), inc=[("mm", 1)])
            V(lambda e, sl=sl: e.tensor_tensor(big[:, D + sl.start:D + sl.stop],
                                               XW(D, 2 * D)[:, sl], pkA[:, 0:512], ALU.mult),
              wait=prog.after("mm"))
        dma("sync", "dO", contrib[WT_OFF + it * P * D: WT_OFF + (it + 1) * P * D]
            .rearrange("(ob p f) -> p ob f", p=P, f=512),
            big[:, D:2 * D].rearrange("p (ob f) -> p ob f", f=512),
            wait=prog.after("vx"))

    if dev == 26:
        return early_finish()
    prog.emit("gpsimd",
              lambda e: e.collective_compute("AllGather", ALU.bypass,
                                             replica_groups=[list(range(NCORES))],
                                             ins=[contrib.ap()], outs=[agw.ap()]),
              wait=prog.after("dO"), inc=[("cc", 1)])
    AGW_DONE = prog.after("cc")
    if dev == 5:
        for ic in range(4):
            dma("gpsimd", "dG", big[:, 0:1024], agw_vq(ic, 0, 1024),
                wait=list(AGW_DONE) + prog.after("vx", "dO"))
            V(lambda e: e.tensor_copy(mixbuf[:, 0:1024], big[:, 0:1024]),
              wait=prog.after("dG"))
            dma("sync", "dO", out[P * ic:P * (ic + 1), 1024:2048], mixbuf[:, 0:1024],
                wait=prog.after("vx"))
        for q in range(4):
            dma("gpsimd", "dG", big[:, 0:1024], agw_bt(0, 1024 * q, 1024),
                wait=prog.after("vx", "dO"))
            V(lambda e: e.tensor_copy(mixbuf[:, 0:1024], big[:, 0:1024]),
              wait=prog.after("dG"))
            dma("sync", "dO", out[512:640, 1024 * q:1024 * (q + 1)], mixbuf[:, 0:1024],
                wait=prog.after("vx"))
        for rw in range(7):
            dma("sync", "dW", skrow[0:1, :], rowstash.ap()[rw:rw + 1, :],
                wait=prog.after("dO"))
            dma("sync", "dO", out[640 + rw:641 + rw, 0:1024], skrow[0:1, :],
                wait=prog.after("dW"))
        return early_finish()
    if dev == 27:
        return early_finish()

    def big_proj(pre_grp, rhs_fn, lhs_fn, n_groups, nob, nk, evac_fn, bias_row_fn=None):
        kguards = []
        for grp in range(n_groups):
            extra = pre_grp(grp) if pre_grp else []
            for ob in range(nob):
                pk0, pk1 = (pkA, pkB) if ob % 2 == 0 else (pkC, pkD)
                banks = [(pk0, 0), (pk0, 512), (pk1, 0), (pk1, 512)]
                br = bias_row_fn(grp, ob) if bias_row_fn is not None else None
                brw = []
                if isinstance(br, tuple):
                    brw = prog.after("dG")
                    br = br[1]
                for k in range(nk):
                    gw = [("mm", kguards[-2])] if len(kguards) >= 2 else None
                    wsl = rhs_fn(grp, ob, k, gw)
                    wv = prog.after("dW")
                    for ti in range(4):
                        pk, off = banks[ti]
                        w = list(wv) if ti == 0 else []
                        if k == 0 and ti == 0:
                            w += prog.after("vx") + list(extra)
                        if k == 0 and br is not None:
                            T(lambda e, pk=pk, off=off, br=br: e.matmul(
                                pk[:, off:off + 512], ones_bf[:, 0:P], br,
                                start=True, stop=False), wait=list(w) + list(brw), inc=[])
                            w = []
                        lhs = lhs_fn(grp, ti, k)
                        T(lambda e, pk=pk, off=off, lhs=lhs, wsl=wsl, k=k, nk=nk,
                          s=(k == 0 and bias_row_fn is None): e.matmul(
                              pk[:, off:off + 512], lhs, wsl, start=s, stop=(k == nk - 1)),
                          wait=w, inc=[("mm", 1)] if ti == 3 else [])
                    kguards.append(prog.val("mm"))
                mmv = prog.val("mm")
                for ti in range(4):
                    pk, off = banks[ti]
                    evac_fn(grp, ob, ti, pk, off, mmv)

    def p1_pre(grp):
        for ic in range(NIF):
            dma("gpsimd", "dG", big[:, 512 * ic:512 * (ic + 1)],
                x0_fm[P * ic:P * (ic + 1), 512 * grp:512 * (grp + 1)],
                wait=(AGW_DONE + prog.after("mm")) if ic == 0 else ())
        return prog.after("dG")

    def p1_rhs(grp, ob, ic, gw):
        wsl = wbuf[:, 512 * (ic % 2):512 * (ic % 2) + 512]
        dma("sync", "dW", wsl, agw_wt(ic, 512 * ob, 512),
            wait=(gw if gw is not None else list(AGW_DONE)))
        return wsl

    def p1_lhs(grp, ti, ic):
        return big[:, 512 * ic + P * ti:512 * ic + P * (ti + 1)]

    def p1_bias(grp, ob):
        dma("gpsimd", "dG", wbuf2[0:1, 0:512], l0_bias[0:1, 512 * ob:512 * (ob + 1)],
            wait=prog.after("mm"))
        return ("DG", wbuf2[0:1, 0:512])

    def p1_evac(grp, ob, ti, pk, off, mmv):
        tt = 4 * grp + ti
        xb = x0buf[:, 512 * (ti % 2):512 * (ti % 2) + 512]
        dma("scalar", "dS", xb, x0_tm[P * tt:P * (tt + 1), 512 * ob:512 * (ob + 1)],
            wait=prog.after("vx"))
        dst = xs[:, D * tt + 512 * ob: D * tt + 512 * (ob + 1)]
        V(lambda e, dst=dst, pk=pk, off=off, xb=xb, tt=tt, ob=ob: e.scalar_tensor_tensor(
            dst, pk[:, off:off + 512], 1.0, xb, ALU.mult, ALU.add,
            accum_out=sums_s[:, 8 * tt + ob: 8 * tt + ob + 1]),
          wait=[("mm", mmv)] + prog.after("dS"))
        Sc(lambda e, dst=dst, tt=tt, ob=ob: e.activation(
            mixbuf[:, 0:512], dst, AF.Square,
            accum_out=sqs_s[:, 8 * tt + ob: 8 * tt + ob + 1]), wait=prog.after("vx"))

    big_proj(p1_pre, p1_rhs, p1_lhs, 2, 8, NIF, p1_evac, bias_row_fn=p1_bias)
    if dev == 28:
        return early_finish()

    def layernorm(tt, rinv_col, nmr_col, rms_col=None):
        r = stat
        V(lambda e, tt=tt: e.reduce_sum(r[:, 48:49], sums_s[:, 8 * tt:8 * tt + 8], axis=AX.X),
          wait=prog.after("sx"))
        V(lambda e, tt=tt: e.reduce_sum(r[:, 49:50], sqs_s[:, 8 * tt:8 * tt + 8], axis=AX.X))
        V(lambda e: e.tensor_scalar(r[:, 50:51], r[:, 48:49], 1.0 / D, None, ALU.mult))
        V(lambda e: e.tensor_tensor(r[:, 51:52], r[:, 50:51], r[:, 50:51], ALU.mult))
        V(lambda e: e.scalar_tensor_tensor(r[:, 52:53], r[:, 49:50], 1.0 / D, r[:, 51:52],
                                           ALU.mult, ALU.subtract))
        V(lambda e: e.tensor_scalar(r[:, 53:54], r[:, 52:53], LN_EPS, None, ALU.add))
        Sc(lambda e: e.activation(r[:, 54:55], r[:, 53:54], AF.Sqrt), wait=prog.after("vx"))
        V(lambda e: e.reciprocal(r[:, 55:56], r[:, 54:55]), wait=prog.after("sx"))
        V(lambda e: e.tensor_tensor(r[:, 56:57], r[:, 55:56], r[:, 55:56], ALU.mult))
        V(lambda e: e.tensor_tensor(r[:, 57:58], r[:, 56:57], r[:, 53:54], ALU.mult))
        V(lambda e: e.tensor_scalar(r[:, 58:59], r[:, 57:58], -0.5, 1.5, ALU.mult, ALU.add))
        V(lambda e, rc=rinv_col: e.tensor_tensor(rc, r[:, 55:56], r[:, 58:59], ALU.mult))
        V(lambda e, rc=rinv_col, nm=nmr_col: e.scalar_tensor_tensor(
            nm, r[:, 50:51], -1.0, rc, ALU.mult, ALU.mult))
        if rms_col is not None:
            V(lambda e, rc=rinv_col: e.tensor_tensor(r[:, 56:57], rc, rc, ALU.mult))
            V(lambda e: e.tensor_tensor(r[:, 57:58], r[:, 56:57], r[:, 52:53], ALU.mult))
            V(lambda e: e.tensor_scalar(r[:, 57:58], r[:, 57:58], RMS_EPS, None, ALU.add))
            Sc(lambda e: e.activation(r[:, 59:60], r[:, 57:58], AF.Sqrt), wait=prog.after("vx"))
            V(lambda e: e.reciprocal(r[:, 60:61], r[:, 59:60]), wait=prog.after("sx"))
            V(lambda e: e.tensor_tensor(r[:, 56:57], r[:, 60:61], r[:, 60:61], ALU.mult))
            V(lambda e: e.tensor_tensor(r[:, 58:59], r[:, 56:57], r[:, 57:58], ALU.mult))
            V(lambda e: e.tensor_scalar(r[:, 58:59], r[:, 58:59], -0.5, 1.5, ALU.mult, ALU.add))
            V(lambda e, xc=rms_col: e.tensor_tensor(xc, r[:, 60:61], r[:, 58:59], ALU.mult))
        V(lambda e, tt=tt, rc=rinv_col, nm=nmr_col: e.tensor_scalar(
            xs[:, D * tt:D * (tt + 1)], xs[:, D * tt:D * (tt + 1)], rc, nm,
            ALU.mult, ALU.add))

    for tt in range(NTT):
        layernorm(tt, stat[:, tt:tt + 1], stat[:, 8 + tt:9 + tt],
                  rms_col=stat[:, 16 + tt:17 + tt])

    def dump_and_finish(key):
        for tt in range(NTT):
            dma("sync", "dO", dv[key][P * tt:P * (tt + 1), :], xs[:, D * tt:D * (tt + 1)],
                wait=prog.after("sx", "vx"))
        for tt in range(NTT):
            dma("sync", "dO", out[P * tt:P * (tt + 1), :], xs[:, D * tt:D * (tt + 1)])
        prog.emit("sync", lambda e: e.nop(), wait=prog.after("dO"), inc=[])
        finish(nc, st, prog, sems, semnames)
        return nc

    if dev == 1:
        return dump_and_finish("x1")

    WCAT0, X2ST, XPF, BR0 = 4096, 2048, 1024, 4864
    dma("gpsimd", "dG", big[:, WCAT0:WCAT0 + 32 * 24].rearrange("p (c k) -> p c k", k=24),
        wcatT.rearrange("(c p) k -> p c k", p=P), wait=prog.after("mm", "sx"))
    dma("sync", "dW", skrow[0:1, 0:24], hbases)
    dma("sync", "dW", skrow[0:1, 24:25], gate_in)
    WCAT_L = prog.after("dG", "dW")

    for ic in range(NIF):
        for tt in range(NTT):
            T(lambda e, ic=ic, tt=tt: e.transpose(
                pkC[:, P * tt:P * (tt + 1)],
                xs[:, D * tt + P * ic: D * tt + P * (ic + 1)], identF[:, :]),
              wait=prog.after("sx", "vx") if tt == 0 else (),
              inc=[("mm", 1)] if tt == NTT - 1 else [])
        mmv = prog.val("mm")
        V(lambda e: e.tensor_copy(big[:, 0:1024], pkC[:, 0:1024]), wait=[("mm", mmv)])
        vxv = prog.val("vx")
        for half in range(2):
            T(lambda e, ic=ic, half=half: e.matmul(
                pkD[0:24, 512 * half:512 * (half + 1)],
                big[:, WCAT0 + 24 * ic:WCAT0 + 24 * (ic + 1)],
                big[:, 512 * half:512 * (half + 1)],
                start=(ic == 0), stop=(ic == NIF - 1)),
              wait=([("vx", vxv)] + (list(WCAT_L) if ic == 0 else [])),
              inc=[("mm", 1)] if (ic == NIF - 1 and half == 1) else [])
    LOGITS_MM = prog.val("mm")
    V(lambda e: e.tensor_copy(hbuf[:, :], pkD[0:24, :]), wait=[("mm", LOGITS_MM)])
    T(lambda e: e.matmul(pkC[:, 0:24], onerow_f[:, :], skrow[0:1, 0:24],
                         start=True, stop=True),
      wait=prog.after("vx") + list(WCAT_L), inc=[("mm", 1)])
    BASES_MM = prog.val("mm")
    V(lambda e: e.tensor_copy(y1g[:, 0:24], pkC[:, 0:24]), wait=[("mm", BASES_MM)])
    for tt in range(NTT):
        T(lambda e, tt=tt: e.transpose(pkD[:, 0:24], hbuf[:, P * tt:P * (tt + 1)],
                                       identF[0:24, 0:24]),
          wait=prog.after("vx"), inc=[("mm", 1)])
        mv = prog.val("mm")
        V(lambda e, tt=tt: e.scalar_tensor_tensor(
            x0buf[:, 24 * tt:24 * tt + 24], pkD[:, 0:24], stat[:, 16 + tt:17 + tt],
            y1g[:, 0:24], ALU.mult, ALU.add), wait=[("mm", mv)])
    hview = x0buf[:, 0:192].rearrange("p (t j) -> p t j", j=24)
    for r in range(4):
        V(lambda e, r=r: e.reduce_max(bass.AP(sk_t, r, [[P, P], [4, 8]]),
                                      bass.AP(x0buf, 4 * r, [[1024, P], [24, 8], [1, 4]]),
                                      axis=AX.X))
    for c in range(4):
        V(lambda e, c=c: e.tensor_tensor(bass.AP(x0buf, c, [[1024, P], [24, 8], [4, 4]]),
                                         bass.AP(x0buf, c, [[1024, P], [24, 8], [4, 4]]),
                                         bass.AP(sk_t, 0, [[P, P], [4, 8], [1, 4]]),
                                         ALU.subtract))
    Sc(lambda e: e.activation(hview[:, :, 0:16], hview[:, :, 0:16], AF.Exp,
                              scale=INV_TAU), wait=prog.after("vx"))
    Sc(lambda e: e.activation(hview[:, :, 16:24], hview[:, :, 16:24], AF.Exp))
    V(lambda e: e.reduce_sum(sk_t[:, 0:8], hview[:, :, 16:20], axis=AX.X),
      wait=prog.after("sx"))
    V(lambda e: e.reduce_sum(sk_t[:, 8:16], hview[:, :, 20:24], axis=AX.X))
    V(lambda e: e.reciprocal(sk_t[:, 16:24], sk_t[:, 0:8]))
    V(lambda e: e.reciprocal(sk_t[:, 24:32], sk_t[:, 8:16]))
    for j in range(4):
        V(lambda e, j=j: e.tensor_tensor(hview[:, :, 16 + j:17 + j], hview[:, :, 16 + j:17 + j],
                                         sk_t[:, 16:24].rearrange("p (t o) -> p t o", o=1),
                                         ALU.mult))
        V(lambda e, j=j: e.tensor_tensor(hview[:, :, 20 + j:21 + j], hview[:, :, 20 + j:21 + j],
                                         sk_t[:, 24:32].rearrange("p (t o) -> p t o", o=1),
                                         ALU.mult))
    def pres_r(r):
        return bass.AP(x0buf, 4 * r, [[1024, P], [24, 8], [1, 4]])

    def pres_c(c):
        return bass.AP(x0buf, c, [[1024, P], [24, 8], [4, 4]])
    aview = sk_a[:, :].rearrange("p (t r) -> p t r", r=4)
    bview = sk_b[:, :].rearrange("p (t c) -> p t c", c=4)
    V(lambda e: e.memset(sk_b[:, :], 1.0))
    for it in range(SINK_ITERS):
        for r in range(4):
            V(lambda e, r=r: e.tensor_tensor(
                sk_t[:, 32 * r:32 * r + 32].rearrange("p (t c) -> p t c", c=4),
                pres_r(r), bview, ALU.mult))
        for r in range(4):
            V(lambda e, r=r: e.reduce_sum(
                aview[:, :, r:r + 1],
                sk_t[:, 32 * r:32 * r + 32].rearrange("p (t c) -> p t c", c=4), axis=AX.X))
        V(lambda e: e.reciprocal(sk_a[:, :], sk_a[:, :]))
        for c in range(4):
            V(lambda e, c=c: e.tensor_tensor(
                sk_t[:, 32 * c:32 * c + 32].rearrange("p (t r) -> p t r", r=4),
                pres_c(c), aview, ALU.mult))
        for c in range(4):
            V(lambda e, c=c: e.reduce_sum(
                bview[:, :, c:c + 1],
                sk_t[:, 32 * c:32 * c + 32].rearrange("p (t r) -> p t r", r=4), axis=AX.X))
        V(lambda e: e.reciprocal(sk_b[:, :], sk_b[:, :]))
    for r in range(4):
        V(lambda e, r=r: e.tensor_tensor(pres_r(r), pres_r(r), bview, ALU.mult))
    for c in range(4):
        V(lambda e, c=c: e.tensor_tensor(pres_c(c), pres_c(c), aview, ALU.mult))
    Sc(lambda e: e.activation(skrow[0:1, 25:26], skrow[0:1, 24:25], AF.Sigmoid))
    sxv = prog.val("sx")
    T(lambda e: e.matmul(pkD[:, 32:33], onerow_f[:, :], skrow[0:1, 25:26],
                         start=True, stop=True), wait=[("sx", sxv)] + prog.after("vx"),
      inc=[("mm", 1)])
    mmv = prog.val("mm")
    V(lambda e: e.tensor_copy(gcol[:, :], pkD[:, 32:33]), wait=[("mm", mmv)])
    V(lambda e: e.tensor_scalar(stat[:, 60:61], gcol[:, 0:1], -1.0, 1.0, ALU.mult, ALU.add))
    V(lambda e: e.tensor_scalar(hview[:, :, 0:16], hview[:, :, 0:16], gcol[:, 0:1],
                                None, ALU.mult))
    V(lambda e: e.tensor_scalar(bass.AP(x0buf, 0, [[1024, P], [24, 8], [5, 4]]),
                                bass.AP(x0buf, 0, [[1024, P], [24, 8], [5, 4]]),
                                stat[:, 60:61], None, ALU.add))
    if dev == 3:
        dma("sync", "dO", out[0:P, 256:448], x0buf[:, 0:192], wait=prog.after("vx", "sx"))
        dma("sync", "dO", out[0:P, 1024:1056], sk_a[:, :])
        dma("sync", "dO", out[0:P, 2048:2080], sk_b[:, :])
        dma("sync", "dO", out[0:P, 3072:3136], stat[:, 0:64])
        return early_finish()
    for ci in range(8):
        dma("gpsimd", "dG", big[:, BR0 + 1024 * ci:BR0 + 1024 * (ci + 1)],
            wbr[P * ci:P * (ci + 1), :], wait=prog.after("mm") if ci == 0 else ())
    WBR_L = prog.after("dG")

    for tt in range(NTT):
        hcol = lambda j, tt=tt: x0buf[:, 24 * tt + j:24 * tt + j + 1]
        V(lambda e, tt=tt, hcol=hcol: e.tensor_scalar(
            mixbuf[:, :], xs[:, D * tt:D * tt + 1024], hcol(16), None, ALU.mult),
          wait=prog.after("mm"))
        for j in range(1, 4):
            V(lambda e, tt=tt, j=j, hcol=hcol: e.scalar_tensor_tensor(
                mixbuf[:, :], xs[:, D * tt + 1024 * j:D * tt + 1024 * (j + 1)],
                hcol(16 + j), mixbuf[:, :], ALU.mult, ALU.add))
        for ci in range(8):
            T(lambda e, ci=ci: e.transpose(pkC[:, P * ci:P * (ci + 1)],
                                           mixbuf[:, P * ci:P * (ci + 1)], identF[:, :]),
              wait=prog.after("vx") if ci == 0 else (), inc=[("mm", 1)] if ci == 7 else [])
        mv = prog.val("mm")
        V(lambda e: e.tensor_copy(big[:, XPF:XPF + 1024], pkC[:, 0:1024]), wait=[("mm", mv)])
        vv = prog.val("vx")
        for ci in range(8):
            for half in range(2):
                T(lambda e, ci=ci, half=half: e.matmul(
                    pkA[:, 512 * half:512 * (half + 1)],
                    big[:, XPF + P * ci:XPF + P * (ci + 1)],
                    big[:, BR0 + 1024 * ci + 512 * half:BR0 + 1024 * ci + 512 * (half + 1)],
                    start=(ci == 0), stop=(ci == 7)),
                  wait=([("vx", vv)] + (list(WBR_L) if tt == 0 else [])) if (ci == 0 and half == 0) else (),
                  inc=[("mm", 1)] if (ci == 7 and half == 1) else [])
        YMM = prog.val("mm")
        for q in range(4):
            for j in range(4):
                V(lambda e, tt=tt, q=q, j=j: e.tensor_copy(
                    mixbuf[:, 256 * j:256 * (j + 1)],
                    xs[:, D * tt + 1024 * j + 256 * q: D * tt + 1024 * j + 256 * (q + 1)]),
                  wait=[("mm", YMM)] if (q == 0 and j == 0) else ())
            for i in range(4):
                dst = xs[:, D * tt + 1024 * i + 256 * q: D * tt + 1024 * i + 256 * (q + 1)]
                for j in range(4):
                    V(lambda e, dst=dst, hcol=hcol, i=i, j=j: e.scalar_tensor_tensor(
                        dst, mixbuf[:, 256 * j:256 * (j + 1)], hcol(4 * i + j), dst,
                        ALU.mult, ALU.add))
                V(lambda e, dst=dst, hcol=hcol, i=i, q=q: e.scalar_tensor_tensor(
                    dst, pkA[:, 256 * q:256 * (q + 1)], hcol(20 + i), dst, ALU.mult, ALU.add))
        for ob in range(8):
            seg = xs[:, D * tt + 512 * ob: D * tt + 512 * (ob + 1)]
            V(lambda e, seg=seg, tt=tt, ob=ob: e.tensor_scalar(
                seg, seg, 0.0, 0.0, ALU.add, ALU.add,
                accum_out=sums_s[:, 8 * tt + ob:8 * tt + ob + 1]))
            Sc(lambda e, seg=seg, tt=tt, ob=ob: e.activation(
                mixbuf[:, 0:512], seg, AF.Square,
                accum_out=sqs_s[:, 8 * tt + ob:8 * tt + ob + 1]), wait=prog.after("vx"))
        layernorm(tt, stat[:, tt:tt + 1], stat[:, 8 + tt:9 + tt])
        for icb in range(4):
            for k8 in range(8):
                ic = 8 * (icb % 2) + k8 if False else icb * 8 + k8
                pk = pkC if icb % 2 == 0 else pkD
                T(lambda e, tt=tt, ic=ic, k8=k8, pk=pk: e.transpose(
                    pk[:, P * k8:P * (k8 + 1)],
                    xs[:, D * tt + P * ic: D * tt + P * (ic + 1)], identF[:, :]),
                  wait=prog.after("sx", "vx") if k8 == 0 else (),
                  inc=[("mm", 1)] if k8 == 7 else [])
            mv = prog.val("mm")
            pk = pkC if icb % 2 == 0 else pkD
            stg = big[:, X2ST + 1024 * (icb % 2):X2ST + 1024 * (icb % 2) + 1024]
            V(lambda e, pk=pk, stg=stg: e.tensor_copy(stg, pk[:, 0:1024]), wait=[("mm", mv)])
            dma("sync", "dO",
                x2c.ap()[P * 8 * icb:P * 8 * (icb + 1), P * tt:P * (tt + 1)]
                .rearrange("(c p) t -> p c t", p=P),
                stg.rearrange("p (c t) -> p c t", t=P), wait=prog.after("vx"))
    prog.emit("gpsimd",
              lambda e: e.collective_compute("AllGather", ALU.bypass,
                                             replica_groups=[list(range(NCORES))],
                                             ins=[x2c.ap()], outs=[x2full.ap()]),
              wait=prog.after("dO"), inc=[("cc", 1)])
    X2F = prog.after("cc")

    if dev == 2:
        return dump_and_finish("x2")

    for ic in range(NIF):
        dma("gpsimd", "dG", big[:, 512 * ic:512 * (ic + 1)],
            wq_slT[P * ic:P * (ic + 1), :], wait=prog.after("mm", "sx") if ic == 0 else ())
    WQ_L = prog.after("dG")
    qguards = []
    for tb in range(16):
        pk0, pk1 = (pkA, pkB) if tb % 2 == 0 else (pkC, pkD)
        banks = [(pk0, 0), (pk0, 512), (pk1, 0), (pk1, 512)]
        tglob = 512 * tb
        r, tloc = tglob // TCORE, tglob % TCORE
        for ic in range(NIF):
            buf = wbuf[:, 512 * (ic % 2):512 * (ic % 2) + 512]
            gw = [("mm", qguards[-2])] if len(qguards) >= 2 else \
                (prog.after("mm") + list(X2F))
            dma("sync", "dW", buf,
                x2full.ap()[D * r + P * ic: D * r + P * (ic + 1), tloc:tloc + 512],
                wait=gw)
            wv = prog.after("dW")
            for oc in range(4):
                pk, off = banks[oc]
                w = list(wv) if oc == 0 else []
                if ic == 0 and oc == 0:
                    w += prog.after("sx") + (list(WQ_L) if tb == 0 else [])
                T(lambda e, pk=pk, off=off, ic=ic, oc=oc, buf=buf: e.matmul(
                    pk[:, off:off + 512],
                    big[:, 512 * ic + P * oc:512 * ic + P * (oc + 1)], buf,
                    start=(ic == 0), stop=(ic == NIF - 1)),
                  wait=w, inc=[("mm", 1)] if oc == 3 else [])
            qguards.append(prog.val("mm"))
        mmv = prog.val("mm")
        for oc in range(4):
            pk, off = banks[oc]
            stg = wbuf2[:, 512 * (oc % 2):512 * (oc % 2) + 512]
            Sc(lambda e, pk=pk, off=off, stg=stg: e.copy(stg, pk[:, off:off + 512]),
               wait=[("mm", mmv)] + prog.after("dO"))
            dma("sync", "dO", q_scr.ap()[P * oc:P * (oc + 1), 512 * tb:512 * (tb + 1)], stg,
                wait=prog.after("sx"))
    for ic in range(NIF):
        dma("gpsimd", "dG", big[:, 256 * ic:256 * (ic + 1)],
            wkv_slT[P * ic:P * (ic + 1), :], wait=prog.after("mm", "sx") if ic == 0 else ())
    WKV_L = prog.after("dG")
    kvguards = []
    for tb in range(16):
        pk0 = pkA if tb % 2 == 0 else pkC
        tglob = 512 * tb
        r, tloc = tglob // TCORE, tglob % TCORE
        for ic in range(NIF):
            buf = wbuf[:, 512 * (ic % 2):512 * (ic % 2) + 512]
            gw = [("mm", kvguards[-2])] if len(kvguards) >= 2 else prog.after("mm")
            dma("sync", "dW", buf,
                x2full.ap()[D * r + P * ic: D * r + P * (ic + 1), tloc:tloc + 512],
                wait=gw)
            wv = prog.after("dW")
            for oc in range(2):
                w = list(wv) if oc == 0 else []
                if ic == 0 and oc == 0:
                    w += prog.after("sx", "vx") + (list(WKV_L) if tb == 0 else [])
                T(lambda e, pk0=pk0, oc=oc, ic=ic, buf=buf: e.matmul(
                    pk0[:, 512 * oc:512 * (oc + 1)],
                    big[:, 256 * ic + P * oc:256 * ic + P * (oc + 1)], buf,
                    start=(ic == 0), stop=(ic == NIF - 1)),
                  wait=w, inc=[("mm", 1)] if oc == 1 else [])
            kvguards.append(prog.val("mm"))
        mmv = prog.val("mm")
        Sc(lambda e, pk0=pk0: e.copy(big[:, 8192:8704], pk0[:, 0:512]),
           wait=[("mm", mmv)] + prog.after("dO"))
        dma("sync", "dO", k_scr.ap()[:, 512 * tb:512 * (tb + 1)], big[:, 8192:8704],
            wait=prog.after("sx"))
        V(lambda e, pk0=pk0: e.tensor_copy(mixbuf[:, 0:512], pk0[:, 512:1024]),
          wait=[("mm", mmv)])
        vv = prog.val("vx")
        pkt = pkB if tb % 2 == 0 else pkD
        for s_ in range(4):
            T(lambda e, s_=s_, pkt=pkt: e.transpose(pkt[:, P * s_:P * (s_ + 1)],
                                                    mixbuf[:, P * s_:P * (s_ + 1)],
                                                    identF[:, :]),
              wait=[("vx", vv)] if s_ == 0 else (), inc=[("mm", 1)] if s_ == 3 else [])
        mv2 = prog.val("mm")
        V(lambda e, pkt=pkt: e.tensor_copy(big[:, 8704:9216], pkt[:, 0:512]),
          wait=[("mm", mv2)] + prog.after("dO"))
        dma("sync", "dO",
            v_scr.ap()[512 * tb:512 * (tb + 1), :].rearrange("(s p) d -> p s d", p=P),
            big[:, 8704:9216].rearrange("p (s d) -> p s d", d=P), wait=prog.after("vx"))

    KB0, VB0, QB0, OS0 = 0, 4096, 8224, 12320
    V(lambda e: e.memset(big[:, VB0:VB0 + 4128].rearrange("p (k c) -> p k c", c=129)
                         [:, :, 128:129], 1.0), wait=prog.after("dO", "mm", "sx"))
    ONES_SET = prog.after("vx")
    sx_par = {0: 0, 1: 0}
    pv_par = {0: 0, 1: 0}
    for b in range(2):
        dma("sync", "dW", big[:, KB0:KB0 + 4096], k_scr.ap()[:, S * b:S * (b + 1)],
            wait=prog.after("mm") + (list(ONES_SET) if b == 0 else []))
        dma("sync", "dW",
            big[:, VB0:VB0 + 4128].rearrange("p (k c) -> p k c", c=129)[:, :, 0:128],
            v_scr.ap()[S * b:S * (b + 1), :].rearrange("(k p) d -> p k d", p=P))
        KVL = prog.after("dW")
        for qh in range(4):
            dma("scalar", "dS", big[:, QB0:QB0 + 4096],
                q_scr.ap()[P * qh:P * (qh + 1), S * b:S * (b + 1)],
                wait=prog.after("mm"))
            QL = prog.after("dS")
            kjp_abs = 0
            for qi in range(8):
                oslots = [(pkC, 0), (pkC, 512), (pkD, 0), (pkD, 512)]
                nkj = 4 * (qi + 1)
                ev_vx = prog.val("vx")
                for kjp in range((nkj + 1) // 2):
                    par = kjp_abs % 2
                    spk = pkA if par == 0 else pkB
                    kj0 = 2 * kjp
                    npair = min(2, nkj - kj0)
                    w = [("sx", sx_par[par])] + list(KVL) + list(QL)
                    for kk in range(npair):
                        kj = kj0 + kk
                        T(lambda e, spk=spk, kk=kk, kj=kj, qi=qi: e.matmul(
                            spk[:, 512 * kk:512 * (kk + 1)],
                            big[:, KB0 + P * kj:KB0 + P * (kj + 1)],
                            big[:, QB0 + 512 * qi:QB0 + 512 * (qi + 1)],
                            start=True, stop=True),
                          wait=w if kk == 0 else (),
                          inc=[("mm", 1)] if kk == npair - 1 else [])
                    mmv = prog.val("mm")
                    pt = ptbuf[:, 1024 * par:1024 * par + 1024]
                    Sc(lambda e, spk=spk, pt=pt, npair=npair: e.activation(
                        pt[:, 0:512 * npair], spk[:, 0:512 * npair], AF.Exp,
                        scale=INV_SQRT_HD),
                       wait=[("mm", max(mmv, pv_par[par]))] if False else
                       [("mm", mmv), ("mm", pv_par[par])], inc=[("sx", 1)])
                    sx_par[par] = prog.val("sx")
                    gx_need = None
                    for kk in range(npair):
                        kj = kj0 + kk
                        d_ = kj - 4 * qi
                        if d_ >= 0:
                            G(lambda e, pt=pt, kk=kk, d_=d_: e.affine_select(
                                out=pt[:, 512 * kk:512 * (kk + 1)],
                                in_=pt[:, 512 * kk:512 * (kk + 1)],
                                compare_op=ALU.is_ge, fill=0.0,
                                pattern=[[1, 512]], base=-128 * d_,
                                channel_multiplier=-1),
                              wait=prog.after("sx"))
                            gx_need = prog.val("gx")
                    pvw = [("sx", sx_par[par]), ("vx", ev_vx)]
                    if gx_need is not None:
                        pvw.append(("gx", gx_need))
                    for kk in range(npair):
                        kj = kj0 + kk
                        for qt in range(4):
                            pko, ooff = oslots[qt]
                            T(lambda e, pko=pko, ooff=ooff, pt=pt, kk=kk, kj=kj, qt=qt,
                              nkj=nkj: e.matmul(
                                  pko[:, ooff:ooff + 129],
                                  pt[:, 512 * kk + P * qt:512 * kk + P * (qt + 1)],
                                  big[:, VB0 + 129 * kj:VB0 + 129 * (kj + 1)],
                                  start=(kj == 0), stop=(kj == nkj - 1)),
                              wait=pvw if (qt == 0 and kk == 0) else (),
                              inc=[("mm", 1)] if (kj == nkj - 1 and qt == 3) else [])
                    pv_par[par] = prog.val("mm")
                    kjp_abs += 1
                mmo = prog.val("mm")
                for qt in range(4):
                    pko, ooff = oslots[qt]
                    V(lambda e, pko=pko, ooff=ooff: e.reciprocal(
                        stat[:, 62:63], pko[:, ooff + 128:ooff + 129]),
                      wait=[("mm", mmo)] + prog.after("dO"))
                    V(lambda e, pko=pko, ooff=ooff, qt=qt: e.tensor_scalar(
                        big[:, OS0 + P * qt:OS0 + P * (qt + 1)],
                        pko[:, ooff:ooff + 128], stat[:, 62:63], None, ALU.mult))
                dma("sync", "dO",
                    c_o.ap()[S * b + 512 * qi:S * b + 512 * (qi + 1),
                             P * qh:P * (qh + 1)].rearrange("(s p) d -> p s d", p=P),
                    big[:, OS0:OS0 + 512].rearrange("p (s d) -> p s d", d=P),
                    wait=prog.after("vx"))
    prog.emit("gpsimd",
              lambda e: e.collective_compute("AllGather", ALU.bypass,
                                             replica_groups=[list(range(NCORES))],
                                             ins=[c_o.ap()], outs=[ag_o.ap()]),
              wait=prog.after("dO"), inc=[("cc", 1)])
    AGO = prog.after("cc")

    reg_cell = {}

    def p5_pre(grp):
        qeng = "gpsimd" if grp == 0 else "scalar"
        qsem = "dG" if grp == 0 else "dS"
        for i in range(NCORES):
            def fn(e, i=i, grp=grp, qeng=qeng):
                key = qeng
                if key not in reg_cell:
                    reg_cell[key] = e.alloc_register(name="ob_" + key)
                    e.reg_load(reg_cell[key], ibase[0:1, 0:1])
                base = e.snap(reg_cell[key], min_val=0, max_val=NTOK - TCORE)
                return e.dma_start(
                    out=ptbuf[:, :].rearrange("p (tt d) -> p tt d", d=512),
                    in_=ag_o.ap()[bass.ds(base + NTOK * i + 512 * grp, 512), :]
                    .rearrange("(tt p) d -> p tt d", p=P))
            prog.emit(qeng, fn,
                      wait=(list(AGO) + prog.after("mm", "sx", "vx", "dW", "gx"))
                      if (grp == 0 and i == 0) else prog.after("vx", "mm"),
                      inc=[(qsem, 16)])
            LD = prog.after(qsem)
            pview = ptbuf[:, :].rearrange("p (tt d) -> p tt d", d=512)
            for tt in range(4):
                V(lambda e, tt=tt, pview=pview: e.tensor_copy(
                    mixbuf[:, 512:1024], pview[:, tt, :]), wait=LD)
                vv = prog.val("vx")
                pk = pkC if tt % 2 == 0 else pkD
                for cc in range(4):
                    T(lambda e, cc=cc, pk=pk: e.transpose(
                        pk[:, P * cc:P * (cc + 1)], mixbuf[:, 512 + P * cc:512 + P * (cc + 1)],
                        identF[:, :]),
                      wait=([("vx", vv)] + prog.after("mm")) if cc == 0 else (),
                      inc=[("mm", 1)] if cc == 3 else [])
                mv = prog.val("mm")
                V(lambda e, i=i, tt=tt, pk=pk: e.tensor_copy(
                    bass.AP(big, 512 * 4 * i + P * tt, [[16384, P], [512, 4], [1, P]]),
                    pk[:, 0:512].rearrange("p (c d) -> p c d", d=P)), wait=[("mm", mv)])
        return prog.after("vx")

    def wo_rhs(grp, ob, fc, gw):
        wsl = wbuf[:, 512:1024] if fc % 2 else wbuf2[:, 512:1024]
        dma("sync", "dW", wsl, agw_wo(fc, 512 * ob, 512),
            wait=(gw if gw is not None else prog.after("mm")))
        return wsl

    def wo_lhs(grp, ti, fc):
        return big[:, 512 * fc + P * ti:512 * fc + P * (ti + 1)]

    def wo_evac(grp, ob, ti, pk, off, mmv):
        tt = 4 * grp + ti
        dst = xs[:, D * tt + 512 * ob: D * tt + 512 * (ob + 1)]
        V(lambda e, dst=dst, pk=pk, off=off, tt=tt, ob=ob: e.scalar_tensor_tensor(
            dst, pk[:, off:off + 512], 1.0, dst, ALU.mult, ALU.add,
            accum_out=sums_s[:, 8 * tt + ob: 8 * tt + ob + 1]), wait=[("mm", mmv)])
        Sc(lambda e, dst=dst, tt=tt, ob=ob: e.activation(
            mixbuf[:, 0:512], dst, AF.Square,
            accum_out=sqs_s[:, 8 * tt + ob:8 * tt + ob + 1]), wait=prog.after("vx"))

    big_proj(p5_pre, wo_rhs, wo_lhs, 2, 8, NIF, wo_evac)

    for tt in range(NTT):
        layernorm(tt, stat[:, tt:tt + 1], stat[:, 8 + tt:9 + tt])
        V(lambda e, tt=tt: e.reduce_max(stat[:, 16 + tt:17 + tt],
                                        xs[:, D * tt:D * (tt + 1)], axis=AX.X,
                                        apply_absolute_value=True), wait=prog.after("sx"))
        V(lambda e, tt=tt: e.tensor_scalar(stat[:, 24 + tt:25 + tt],
                                           stat[:, 16 + tt:17 + tt], 1e-8, 1.0 / 127.0,
                                           ALU.max, ALU.mult))
        V(lambda e, tt=tt: e.reciprocal(stat[:, 32 + tt:33 + tt], stat[:, 24 + tt:25 + tt]))

    if dev == 4:
        return dump_and_finish("x3")

    T(lambda e: e.transpose(pkC[0:8, 0:P], stat[:, 32:40], identF[:, :]),
      wait=prog.after("vx", "sx"), inc=[("mm", 1)])
    mmv = prog.val("mm")
    V(lambda e: e.tensor_copy(sk_t[0:8, 0:P], pkC[0:8, 0:P]), wait=[("mm", mmv)])
    dma("sync", "dW", hbuf[0:1, :], sk_t[0:8, 0:P], wait=prog.after("vx"))
    INVROW = prog.after("dW")
    dma("sync", "dW", skrow[0:1, :], rowstash.ap()[2:3, :], wait=prog.after("mm"))
    for half in range(2):
        T(lambda e, half=half: e.matmul(pkC[:, 512 * half:512 * (half + 1)], onerow_f[:, :],
                                        skrow[0:1, 512 * half:512 * (half + 1)],
                                        start=True, stop=True),
          wait=prog.after("vx", "dW"), inc=[("mm", 1)] if half else [])
    mmv = prog.val("mm")
    V(lambda e: e.tensor_copy(x0buf[:, 0:1024], pkC[:, 0:1024]), wait=[("mm", mmv)])
    BIA = 12288
    for k in range(4):
        dma("sync", "dW", skrow[0:1, :], rowstash.ap()[3 + k:4 + k, :],
            wait=prog.after("vx"))
        V(lambda e, k=k: e.tensor_copy(big[0:1, BIA + 1024 * k:BIA + 1024 * (k + 1)],
                                       skrow[0:1, :]), wait=prog.after("dW"))
    MH0, ML0, XQ0, AF0 = 8192, 9216, 0, 10240
    dma("sync", "dW", skrow[0:1, :], rowstash.ap()[0:1, :], wait=prog.after("vx"))
    for half in range(2):
        T(lambda e, half=half: e.matmul(pkC[:, 512 * half:512 * (half + 1)], onerow_f[:, :],
                                        skrow[0:1, 512 * half:512 * (half + 1)],
                                        start=True, stop=True),
          wait=prog.after("dW", "vx"), inc=[("mm", 1)] if half else [])
    mmv = prog.val("mm")
    V(lambda e: e.tensor_copy(big[:, MH0:MH0 + 1024], pkC[:, 0:1024]), wait=[("mm", mmv)])
    dma("sync", "dW", skrow[0:1, :], rowstash.ap()[1:2, :], wait=prog.after("vx"))
    for half in range(2):
        T(lambda e, half=half: e.matmul(pkD[:, 512 * half:512 * (half + 1)], onerow_f[:, :],
                                        skrow[0:1, 512 * half:512 * (half + 1)],
                                        start=True, stop=True),
          wait=prog.after("vx", "dW"), inc=[("mm", 1)] if half else [])
    mmv = prog.val("mm")
    V(lambda e: e.tensor_copy(big[:, ML0:ML0 + 1024], pkD[:, 0:1024]), wait=[("mm", mmv)])
    REPS_V = prog.val("vx")
    s1guards = []
    s2guards = []

    for grp in range(4):
        tiles = (2 * grp, 2 * grp + 1)
        T(lambda e, grp=grp: e.matmul(pkC[:, 0:256], onerow_f[:, :],
                                      hbuf[0:1, 256 * grp:256 * (grp + 1)],
                                      start=True, stop=True),
          wait=prog.after("vx") + [("vx", REPS_V)] + list(INVROW), inc=[("mm", 1)])
        mmv = prog.val("mm")
        V(lambda e: e.tensor_copy(mixbuf[:, 0:256], pkC[:, 0:256]), wait=[("mm", mmv)])
        for ic in range(NIF):
            for k2 in range(2):
                tt = tiles[k2]
                T(lambda e, ic=ic, k2=k2, tt=tt: e.transpose(
                    pkC[:, 512 + P * k2:512 + P * (k2 + 1)],
                    xs[:, D * tt + P * ic: D * tt + P * (ic + 1)], identF[:, :]),
                  wait=prog.after("vx") if k2 == 0 else (),
                  inc=[("mm", 1)] if k2 == 1 else [])
            mv = prog.val("mm")
            V(lambda e, ic=ic: e.tensor_tensor(mixbuf[:, 256:512], pkC[:, 512:768],
                                               mixbuf[:, 0:256], ALU.mult), wait=[("mm", mv)])
            V(lambda e, ic=ic: e.tensor_scalar(big[:, XQ0 + 256 * ic:XQ0 + 256 * (ic + 1)],
                                               mixbuf[:, 256:512], MAGIC, MAGIC,
                                               ALU.add, ALU.subtract))
        XQV = prog.val("vx")
        for rb in range(2):
            for ic in range(NIF):
                buf = wbuf[:, 512 * (ic % 2):512 * (ic % 2) + 512]
                gw = [("mm", s1guards[-2])] if len(s1guards) >= 2 else prog.after("mm")
                dma("sync", "dW", buf, agw_vq(ic, 512 * rb, 512), wait=gw)
                wv = prog.after("dW")
                for k2 in range(2):
                    w = list(wv) if k2 == 0 else []
                    if ic == 0 and k2 == 0:
                        w += [("vx", XQV)] + prog.after("vx")
                    T(lambda e, k2=k2, ic=ic, buf=buf: e.matmul(
                        pkA[:, 512 * k2:512 * (k2 + 1)],
                        big[:, XQ0 + 256 * ic + P * k2:XQ0 + 256 * ic + P * (k2 + 1)],
                        buf, start=(ic == 0), stop=(ic == NIF - 1)),
                      wait=w, inc=[("mm", 1)] if (k2 == 1 or ic == NIF - 1) else [])
                s1guards.append(prog.val("mm"))
            mmv = prog.val("mm")
            for k2 in range(2):
                tt = tiles[k2]
                V(lambda e, k2=k2, rb=rb, tt=tt: e.scalar_tensor_tensor(
                    y1g[:, 1024 * k2 + 512 * rb:1024 * k2 + 512 * (rb + 1)],
                    pkA[:, 512 * k2:512 * (k2 + 1)], stat[:, 24 + tt:25 + tt],
                    x0buf[:, 512 * rb:512 * (rb + 1)], ALU.mult, ALU.mult),
                  wait=[("mm", mmv)])
        if dev == 6 and grp == 0:
            dma("sync", "dO", out[128:256, 1088:1344], mixbuf[:, 0:256],
                wait=prog.after("vx"))
            dma("sync", "dO", out[0:P, 0:2048], y1g[:, :])
            V(lambda e: e.tensor_copy(mixbuf[:, 0:1024], big[:, XQ0:XQ0 + 1024]),
              wait=prog.after("dO"))
            dma("sync", "dO", out[0:P, 2048:3072], mixbuf[:, 0:1024],
                wait=prog.after("vx"))
            dma("sync", "dO", out[128:256, 0:64], stat[:, 0:64])
            dma("sync", "dO", out[128:256, 64:1088], x0buf[:, 0:1024])
            return early_finish()
        for k2 in range(2):
            tt = tiles[k2]
            y1 = y1g[:, 1024 * k2:1024 * (k2 + 1)]
            V(lambda e, y1=y1: e.tensor_tensor(mixbuf[:, :], y1, big[:, MH0:MH0 + 1024],
                                               ALU.mult))
            V(lambda e: e.reduce_max(stat[:, 40:41], mixbuf[:, :], axis=AX.X,
                                     apply_absolute_value=True))
            V(lambda e, y1=y1: e.tensor_tensor(mixbuf[:, :], y1, big[:, ML0:ML0 + 1024],
                                               ALU.mult))
            V(lambda e: e.reduce_max(stat[:, 41:42], mixbuf[:, :], axis=AX.X,
                                     apply_absolute_value=True))
            V(lambda e: e.tensor_scalar(stat[:, 42:43], stat[:, 40:41], 1e-8, 1.0 / 127.0,
                                        ALU.max, ALU.mult))
            V(lambda e: e.tensor_scalar(stat[:, 43:44], stat[:, 41:42], 1e-8, 1.0 / 127.0,
                                        ALU.max, ALU.mult))
            V(lambda e: e.reciprocal(stat[:, 44:45], stat[:, 42:43]))
            V(lambda e: e.reciprocal(stat[:, 45:46], stat[:, 43:44]))
            V(lambda e: e.tensor_scalar(mixbuf[:, :], big[:, MH0:MH0 + 1024],
                                        stat[:, 44:45], None, ALU.mult))
            V(lambda e: e.scalar_tensor_tensor(mixbuf[:, :], big[:, ML0:ML0 + 1024],
                                               stat[:, 45:46], mixbuf[:, :],
                                               ALU.mult, ALU.add))
            V(lambda e, y1=y1: e.tensor_tensor(y1, y1, mixbuf[:, :], ALU.mult))
            V(lambda e, y1=y1: e.tensor_scalar(y1, y1, MAGIC, MAGIC, ALU.add, ALU.subtract))
            V(lambda e: e.tensor_scalar(mixbuf[:, :], big[:, MH0:MH0 + 1024],
                                        stat[:, 42:43], None, ALU.mult))
            V(lambda e: e.scalar_tensor_tensor(mixbuf[:, :], big[:, ML0:ML0 + 1024],
                                               stat[:, 43:44], mixbuf[:, :],
                                               ALU.mult, ALU.add))
            V(lambda e, y1=y1: e.tensor_tensor(y1, y1, mixbuf[:, :], ALU.mult))
            vv = prog.val("vx")
            for rc in range(8):
                T(lambda e, y1=y1, rc=rc: e.transpose(
                    pkB[:, P * (rc % 8):P * (rc % 8) + P], y1[:, P * rc:P * (rc + 1)],
                    identF[:, :]),
                  wait=([("vx", vv)] + prog.after("mm")) if rc == 0 else (),
                  inc=[("mm", 1)] if rc == 7 else [])
            mv = prog.val("mm")
            V(lambda e, k2=k2: e.tensor_copy(big[:, AF0 + 1024 * k2:AF0 + 1024 * (k2 + 1)],
                                             pkB[:, 0:1024]), wait=[("mm", mv)])
        AFV = prog.val("vx")
        for ob in range(8):
            pks = pkA if ob % 2 == 0 else pkB
            for rc in range(8):
                buf = wbuf2[:, 512 * (rc % 2):512 * (rc % 2) + 512]
                gw = [("mm", s2guards[-2])] if len(s2guards) >= 2 else prog.after("mm")
                dma("scalar", "dS", buf, agw_bt(rc, 512 * ob, 512), wait=gw)
                wv = prog.after("dS")
                for k2 in range(2):
                    w = list(wv) if k2 == 0 else []
                    if rc == 0:
                        if k2 == 0:
                            w += prog.after("vx") + ([("vx", AFV)] if ob == 0 else [])
                        T(lambda e, pks=pks, k2=k2, ob=ob: e.matmul(
                            pks[:, 512 * k2:512 * (k2 + 1)], ones_bf[:, 0:P],
                            big[0:1, BIA + 512 * ob:BIA + 512 * ob + 512],
                            start=True, stop=False), wait=w, inc=[])
                        w = []
                    T(lambda e, pks=pks, k2=k2, rc=rc, buf=buf: e.matmul(
                        pks[:, 512 * k2:512 * (k2 + 1)],
                        big[:, AF0 + 1024 * k2 + P * rc:AF0 + 1024 * k2 + P * (rc + 1)],
                        buf, start=False, stop=(rc == 7)),
                      wait=w, inc=[("mm", 1)] if (k2 == 1 or rc == 7) else [])
                s2guards.append(prog.val("mm"))
            mmv = prog.val("mm")
            for k2 in range(2):
                tt = tiles[k2]
                dst = xs[:, D * tt + 512 * ob: D * tt + 512 * (ob + 1)]
                V(lambda e, dst=dst, pks=pks, k2=k2, tt=tt, ob=ob: e.scalar_tensor_tensor(
                    dst, pks[:, 512 * k2:512 * (k2 + 1)], 1.0, dst, ALU.mult, ALU.add,
                    accum_out=sums_s[:, 8 * tt + ob:8 * tt + ob + 1]), wait=[("mm", mmv)])
                Sc(lambda e, dst=dst, tt=tt, ob=ob: e.activation(
                    mixbuf[:, 0:512], dst, AF.Square,
                    accum_out=sqs_s[:, 8 * tt + ob:8 * tt + ob + 1]), wait=prog.after("vx"))
        for k2 in range(2):
            tt = tiles[k2]
            layernorm(tt, stat[:, tt:tt + 1], stat[:, 8 + tt:9 + tt])
            dma("sync", "dO", out[P * tt:P * (tt + 1), :], xs[:, D * tt:D * (tt + 1)],
                wait=prog.after("sx", "vx"))

    prog.emit("sync", lambda e: e.nop(), wait=prog.after("dO"), inc=[])
    finish(nc, st, prog, sems, semnames)
    return nc


def finish(nc, st, prog, sems, semnames):
    for name in semnames:
        sems[name] = st.enter_context(nc.semaphore(name))
    with nc.Block() as block:
        @block.tensor
        def _(e):
            prog.replay("tensor", e, sems)

        @block.vector
        def _(e):
            prog.replay("vector", e, sems)

        @block.scalar
        def _(e):
            prog.replay("scalar", e, sems)

        @block.gpsimd
        def _(e):
            prog.replay("gpsimd", e, sems)

        @block.sync
        def _(e):
            prog.replay("sync", e, sems)
    st.close()


def make_in_maps(inputs):
    x0 = np.ascontiguousarray(np.asarray(inputs["L0_attn_data"], dtype=np.float32)
                              .reshape(NTOK, D))
    x0T = np.ascontiguousarray(x0.T)
    W = np.asarray(inputs["L0_attn_weight"], dtype=np.float32)
    WT = np.ascontiguousarray(W.T)
    mask = np.asarray(inputs["L0_attn_mask"], dtype=np.int32)
    wcatT = np.ascontiguousarray(np.concatenate(
        [np.asarray(inputs["L0_ffn_W_res"], np.float32),
         np.asarray(inputs["L0_ffn_W_pre"], np.float32),
         np.asarray(inputs["L0_ffn_W_post"], np.float32)], axis=0).T)
    hb = np.concatenate([np.asarray(inputs["L0_ffn_H_res_base"], np.float32).reshape(16),
                         np.asarray(inputs["L0_ffn_H_pre_base"], np.float32).reshape(4),
                         np.asarray(inputs["L0_ffn_H_post_base"], np.float32).reshape(4)])
    Wq = np.asarray(inputs["L1_attn_Wq"], np.float32)
    Wkv = np.asarray(inputs["L1_attn_Wkv"], np.float32)
    WoT = np.ascontiguousarray(np.asarray(inputs["L1_attn_Wo"], np.float32).T)
    vT = np.ascontiguousarray(np.asarray(inputs["L1_ffn_v_weight"], np.float32).T)
    uT = np.ascontiguousarray(np.asarray(inputs["L1_ffn_u_weight"], np.float32).T)
    maps = []
    for c in range(NCORES):
        kvrows = list(range(128 * c, 128 * (c + 1))) + \
            list(range(1024 + 128 * c, 1024 + 128 * (c + 1)))
        maps.append({
            "x0_tm": np.ascontiguousarray(x0[TCORE * c:TCORE * (c + 1), :]),
            "x0_fm": np.ascontiguousarray(x0T[:, TCORE * c:TCORE * (c + 1)]),
            "l0_wt_sl": np.ascontiguousarray(WT[512 * c:512 * (c + 1), :]),
            "l0_mask_sl": np.ascontiguousarray(mask[:, 512 * c:512 * (c + 1)]),
            "l0_bias": np.ascontiguousarray(
                np.asarray(inputs["L0_attn_bias"], np.float32).reshape(1, D)),
            "wcatT": wcatT,
            "hbases": np.ascontiguousarray(hb.reshape(1, 24)),
            "gate_in": np.ascontiguousarray(
                np.asarray(inputs["L0_ffn_gate"], np.float32).reshape(1, 1)),
            "wbr": np.ascontiguousarray(np.asarray(inputs["L0_ffn_W_branch"], np.float32)),
            "wq_slT": np.ascontiguousarray(Wq[512 * c:512 * (c + 1), :].T),
            "wkv_slT": np.ascontiguousarray(Wkv[kvrows, :].T),
            "wo_slT": np.ascontiguousarray(WoT[512 * c:512 * (c + 1), :]),
            "v_slT": np.ascontiguousarray(vT[512 * c:512 * (c + 1), :]),
            "uT_sl": np.ascontiguousarray(uT[128 * c:128 * (c + 1), :]),
            "r_base_col": np.arange(128 * c, 128 * (c + 1), dtype=np.float32).reshape(P, 1),
            "hi_idx": np.ascontiguousarray(
                np.asarray(inputs["L1_ffn_high_idx"], np.int32).reshape(1, RHI)),
            "lo_idx": np.ascontiguousarray(
                np.asarray(inputs["L1_ffn_low_idx"], np.int32).reshape(1, RLO)),
            "l1_bias": np.ascontiguousarray(
                np.asarray(inputs["L1_ffn_bias"], np.float32).reshape(1, D)),
            "tok_base": np.array([[TCORE * c]], dtype=np.int32),
        })
    return maps


def kernel(**inputs):
    nc = build_program(dev=0)
    maps = make_in_maps(inputs)
    res = run_bass_kernel_spmd(nc, maps, core_ids=list(range(NCORES)))
    full = np.concatenate([res.results[c]["out"] for c in range(NCORES)], axis=0)
    return full.reshape(B, S, D).astype(np.float32)
